# version 24
# baseline (speedup 1.0000x reference)
import contextlib
import numpy as np

import concourse.bass as bass
import concourse.mybir as mybir
from concourse.bass_utils import run_bass_kernel_spmd

F32 = mybir.dt.float32
I32 = mybir.dt.int32
BF16 = mybir.dt.bfloat16
U8 = mybir.dt.uint8
ALU = mybir.AluOpType
ACT = mybir.ActivationFunctionType
AX = mybir.AxisListType.X

D = 1024
E3 = 3072
FF = 4096
NCORE = 8
HD = 64



class Plan:

    ENGS = ("tensor", "vector", "scalar", "gpsimd", "sync")

    def __init__(self, nc):
        self.nc = nc
        self.recs = {e: [] for e in self.ENGS}
        self.streams = list(self.ENGS) + ["cc"]
        self.cnt = {s: 0 for s in self.streams}
        self.sems = {}
        self.spaces = {}
        self.wm = {e: {} for e in self.ENGS}
        self.n_inst = 0
        self._pool = []

    def make_sems(self, st):
        for s in self.streams:
            self.sems[s] = st.enter_context(self.nc.semaphore("sem_" + s))
        for i in range(70):
            self._pool.append(st.enter_context(self.nc.semaphore(f"dsem{i}")))

    def _fam_sem(self, fam):
        key = "d_" + fam
        if key not in self.sems:
            assert self._pool, "semaphore pool exhausted"
            self.sems[key] = self._pool.pop()
            self.cnt[key] = 0
        return key

    def _wait(self, eng, src, val):
        if src == eng == "tensor":
            return
        if src.startswith("d_") or src == "cc":
            val = self.cnt[src]
        if val <= self.wm[eng].get(src, 0):
            return
        self.wm[eng][src] = val
        self.recs[eng].append(("wait", src, val))

    def _deps_for(self, eng, key, write):
        sp = self.spaces.get(key[0])
        if sp is None:
            return
        lo, hi = key[1], key[2]
        for ent in sp:
            if ent[0] < hi and lo < ent[1]:
                if ent[2] is not None:
                    self._wait(eng, *ent[2])
                if write:
                    for s, c in ent[3].items():
                        self._wait(eng, s, c)

    def _record(self, key, write, tgt, cval):
        sp = self.spaces.setdefault(key[0], [])
        lo, hi = key[1], key[2]
        if write:
            new = []
            for ent in sp:
                if ent[0] < hi and lo < ent[1]:
                    if ent[0] < lo:
                        new.append([ent[0], lo, ent[2], dict(ent[3])])
                    if hi < ent[1]:
                        new.append([hi, ent[1], ent[2], dict(ent[3])])
                else:
                    new.append(ent)
            new.append([lo, hi, (tgt, cval), {}])
            self.spaces[key[0]] = new
        else:
            covered = False
            for ent in sp:
                if ent[0] < hi and lo < ent[1]:
                    ent[3][tgt] = max(ent[3].get(tgt, 0), cval)
                    covered = True
            if not covered:
                sp.append([lo, hi, None, {tgt: cval}])

    def emit(self, eng, fn, reads=(), writes=(), dma=False, cc=False, fam=None):
        if cc:
            tgt, step = "cc", 1
        elif dma:
            if fam is None:
                w0 = writes[0]
                fam = w0[0] if w0[0].startswith("dram:") else f"{w0[0]}:{w0[1]}"
            tgt, step = self._fam_sem(fam), 16
        else:
            tgt, step = eng, 1
        for r in reads:
            self._deps_for(eng, r, False)
        for w in writes:
            self._deps_for(eng, w, True)
        self.cnt[tgt] += step
        cval = self.cnt[tgt]
        self.n_inst += 1
        self.recs[eng].append(("inst", fn, tgt, cval, step))
        for r in reads:
            self._record(r, False, tgt, cval)
        for w in writes:
            self._record(w, True, tgt, cval)

    def raw(self, eng, fn):
        self.recs[eng].append(("raw", fn))

    def barrier_on(self, eng):
        for s in self.streams:
            if self.cnt[s] > 0:
                self._wait(eng, s, self.cnt[s])

    def replay(self, block):
        plan = self
        waited = {}
        for recs in self.recs.values():
            for r in recs:
                if r[0] == "wait":
                    waited.setdefault(r[1], set()).add(r[2])
        marked = {s: sorted(v) for s, v in waited.items()}

        import bisect

        def new_val(src, val):
            if src.startswith("d_") or src == "cc":
                return val
            return bisect.bisect_right(marked[src], val)

        def mk(name):
            def body(h):
                for r in plan.recs[name]:
                    if r[0] == "wait":
                        _, src, val = r
                        h.wait_ge(plan.sems[src], new_val(src, val))
                    elif r[0] == "inst":
                        _, fn, tgt, cval, step = r
                        inst = fn(h)
                        if tgt.startswith("d_") or tgt == "cc" or (
                                tgt in waited and cval in waited[tgt]):
                            inst.then_inc(plan.sems[tgt], step)
                    else:
                        r[1](h)
            return body

        block.tensor(mk("tensor"))
        block.vector(mk("vector"))
        block.scalar(mk("scalar"))
        block.gpsimd(mk("gpsimd"))
        block.sync(mk("sync"))


class Reg:

    __slots__ = ("ap", "space", "off", "isz", "n")

    def __init__(self, ap, space, off_bytes, itemsize, n_free):
        self.ap = ap
        self.space = space
        self.off = off_bytes
        self.isz = itemsize
        self.n = n_free

    def k(self, lo=0, hi=None):
        if hi is None:
            hi = self.n
        return (self.space, self.off + lo * self.isz, self.off + hi * self.isz)


class Arena:
    def __init__(self, handle, space, nbytes):
        self.h = handle
        self.space = space
        self.nbytes = nbytes

    def view(self, off, dtype, parts, free):
        isz = {F32: 4, BF16: 2, U8: 1, I32: 4}[dtype]
        assert off % 4 == 0, off
        assert off + free * isz <= self.nbytes, (self.space, off, free, isz)
        ap = self.h.ap()[0:parts, off:off + free * isz]
        if dtype != U8:
            ap = ap.bitcast(dtype)
        return Reg(ap, self.space, off, isz, free)


def dk(name, lo=0, hi=1 << 60):
    return ("dram:" + name, lo, hi)



def build_program(T=4096, ncores=NCORE):
    nc = bass.Bass("TRN2", target_bir_lowering=False)
    NT = T // 128
    NB = T // 512
    TA = T * ncores
    NCH = TA // 128
    NFC = FF // 128

    def din(name, shape, dtype=F32):
        return nc.dram_tensor(name, list(shape), dtype, kind="ExternalInput").ap()

    x_in = din("x", [T, D])
    wqkvT = din("wqkvT", [D, E3])
    bqkvp_d = din("bqkvp", [128, 24])
    wprojT = din("wprojT", [D, D])
    bprojr = din("bprojr", [1, D])
    w1n = din("w1n", [FF, D])
    b1p_d = din("b1p", [128, 32])
    w2n = din("w2n", [D, FF])
    b2p_d = din("b2p", [128, 8])
    wqT1 = din("wqT1", [D, HD])
    wkT1 = din("wkT1", [D, HD])
    wvT1 = din("wvT1", [D, HD])
    bqkv1_d = din("bqkv1", [HD, 3])
    wgblk = din("wgblk", [NFC, D, 128])
    wublk = din("wublk", [NFC, D, 128])
    wdT = din("wdT", [FF, D])
    sc0 = din("sc0", [1, 1])
    identf_d = din("identf", [128, 128])
    identb_d = din("identb", [128, 128], BF16)
    mask16_d = din("mask16", [128, 128], BF16)
    selb_d = din("selb", [16, 1024], BF16)
    onesb_d = din("onesb", [1, 128], BF16)
    onesf_d = din("onesf", [1, 128])

    out_h = nc.dram_tensor("out", [T, D], F32, kind="ExternalOutput").ap()
    h_hbm = nc.dram_tensor("h_hbm", [NFC, 128, T], F32).ap()
    gu_hbm = nc.dram_tensor("gu_hbm", [NFC, 128, T], F32).ap()
    wgq_hbm = nc.dram_tensor("wgq_hbm", [NFC, 128, D], BF16).ap()
    wuq_hbm = nc.dram_tensor("wuq_hbm", [NFC, 128, D], BF16).ap()
    wdq_hbm = nc.dram_tensor("wdq_hbm", [NFC, 128, D], BF16).ap()
    w2q_hbm = nc.dram_tensor("w2q_hbm", [64, 128, 512], BF16).ap()
    mxio = nc.dram_tensor("mxio", [2, 1], F32).ap()
    mxout = nc.dram_tensor("mxout", [2, 1], F32).ap()
    bq2s = nc.dram_tensor("bq2s", [128, 8], F32).ap()
    kg_in = nc.dram_tensor("kg_in", [HD, T], BF16).ap()
    kg_out = nc.dram_tensor("kg_out", [HD * ncores, T], BF16).ap()
    vg_in = nc.dram_tensor("vg_in", [T, 65], BF16).ap()
    vg_out = nc.dram_tensor("vg_out", [TA, 65], BF16).ap()

    st = contextlib.ExitStack()
    with st:
        stream_h = st.enter_context(nc.sbuf_tensor("stream", [128, NT * D], BF16))
        bigw_h = st.enter_context(nc.sbuf_tensor("bigw", [128, 65536], U8))
        arn_h = st.enter_context(nc.sbuf_tensor("arn", [128, 65536], U8))
        aux_h = st.enter_context(nc.sbuf_tensor("aux", [128, 12288], U8))
        cst_h = st.enter_context(nc.sbuf_tensor("cst", [128, 3904], U8))
        PS_h = st.enter_context(nc.psum_tensor("PS", [128, 4096], F32))

        ARN = Arena(arn_h, "arn", 65536)
        AUX = Arena(aux_h, "aux", 12288)
        CST = Arena(cst_h, "cst", 3904)
        BIG = Arena(bigw_h, "big", 65536)

        plan = Plan(nc)
        plan.make_sems(st)
        E = plan.emit

        def stream_t(i):
            return stream_h.ap()[:, i * D:(i + 1) * D]

        def skey(i, j=None):
            if j is None:
                j = i + 1
            return ("stream", i * D * 2, j * D * 2)

        def ps(lo, n):
            return PS_h.ap()[:, lo:lo + n]

        def psk(lo, n):
            return ("ps", lo * 4, (lo + n) * 4)

        co = {}
        coff = [0]

        def calloc(name, dtype, free, parts=128, align=4):
            isz = {F32: 4, BF16: 2, I32: 4}[dtype]
            coff[0] = ((coff[0] + align - 1) // align) * align
            co[name] = CST.view(coff[0], dtype, parts, free)
            coff[0] += free * isz

        calloc("identf", F32, 128, align=64)
        calloc("identb", BF16, 128, align=64)
        calloc("onesb", BF16, 128, parts=1, align=64)
        calloc("bqkvp", F32, 24, align=64)
        calloc("sbq", F32, 24)
        calloc("b1p", F32, 32, align=64)
        calloc("b2p", F32, 8, align=64)
        calloc("bq1", F32, 32)
        calloc("bq2p", F32, 8)
        calloc("bqkv1", F32, 3, parts=HD, align=64)
        calloc("sc128", F32, 1)
        calloc("gm", F32, 4)
        calloc("s1", F32, 1)
        calloc("rs1", F32, 1)
        calloc("s2", F32, 1)
        calloc("negs2", F32, 1)
        calloc("rs2", F32, 1)
        calloc("mg", F32, 1)
        calloc("rmg", F32, 1)
        calloc("negmg", F32, 1)
        calloc("mu", F32, 1)
        calloc("rmu", F32, 1)
        calloc("negmu", F32, 1)
        calloc("md", F32, 1)
        calloc("rmd", F32, 1)
        calloc("negmd", F32, 1)
        calloc("macc", F32, 4)
        calloc("carryk", F32, 8)
        calloc("carrykv", F32, 8)
        calloc("swt", F32, 4)
        calloc("rsw2", F32, 8)
        calloc("negsw2", F32, 8)
        calloc("ln", F32, 32)
        calloc("rsct", F32, 1)
        calloc("nsct", F32, 1)
        calloc("sctok", F32, NT)
        calloc("ct", F32, 1)
        calloc("half", F32, 1)
        calloc("rqa", I32, 1)
        calloc("onesf", F32, 128, parts=1, align=64)
        calloc("row", F32, 4, parts=4)
        calloc("mxtok", F32, NT)
        calloc("rs127", F32, NT)
        print("CST used:", coff[0]); assert coff[0] <= 3904, coff[0]

        co["bprojb"] = ARN.view(40960, BF16, 1, D)
        co["bq2r"] = ARN.view(0, BF16, 1, D)
        co["mask16"] = BIG.view(56320, BF16, 128, 128)
        co["selb"] = BIG.view(56576, BF16, 16, 1024)

        def K(name, lo=0, hi=None):
            return co[name].k(lo, hi)

        def bcast_row(src_ap, src_keys, dst_ap, dst_keys, n, bank=7):
            pslo = bank * 512
            E("tensor", lambda h: h.matmul(
                ps(pslo, n), co["onesf"].ap[0:1, :], src_ap,
                start=True, stop=True),
              reads=list(src_keys) + [K("onesf")], writes=[psk(pslo, 512)])
            E("vector", lambda h: h.tensor_copy(dst_ap, ps(pslo, n)),
              reads=[psk(pslo, 512)], writes=list(dst_keys))

        def preduce(src_ap, src_keys, dst_ap, dst_keys, n_part, op, bank=7):
            pslo = bank * 512
            E("tensor", lambda h: h.transpose(
                ps(pslo, 128)[0:1, 0:n_part], src_ap,
                co["identf"].ap[0:n_part, 0:n_part]),
              reads=list(src_keys) + [K("identf")], writes=[psk(pslo, 512)])
            E("vector", lambda h: h.tensor_reduce(
                dst_ap, ps(pslo, 128)[0:1, 0:n_part], AX, op),
              reads=[psk(pslo, 512)], writes=list(dst_keys))

        def transpose_128(src_ap, src_keys, dst_ap, dst_keys, bank,
                          ident="identb", evac="scalar", in_parts=128):
            pslo = bank * 512
            if ident == "identb":
                pv = ps(pslo, 64).bitcast(BF16)
            else:
                pv = ps(pslo, 128)
            pk_ = psk(pslo, 512)
            idap = co[ident].ap[0:in_parts, 0:in_parts]
            E("tensor", lambda h: h.transpose(pv[:, 0:in_parts], src_ap, idap),
              reads=list(src_keys) + [K(ident)], writes=[pk_])
            if evac == "vector":
                E("vector", lambda h: h.tensor_copy(dst_ap, pv[:, 0:in_parts]),
                  reads=[pk_], writes=list(dst_keys))
            else:
                E("scalar", lambda h: h.copy(dst_ap, pv[:, 0:in_parts]),
                  reads=[pk_], writes=list(dst_keys))


        def round_neg(m_ap, m_keys, qi, d, t, out_ap, out_keys,
                      eng="vector"):
            E("scalar", lambda h: h.copy(qi.ap, m_ap),
              reads=list(m_keys), writes=[qi.k()])
            E("vector", lambda h: h.tensor_tensor(d.ap, m_ap, qi.ap,
                                                  ALU.subtract),
              reads=list(m_keys) + [qi.k()], writes=[d.k()])
            E("vector", lambda h: h.scalar_tensor_tensor(
                t.ap, d.ap, 0.5, qi.ap, ALU.is_gt, ALU.add),
              reads=[d.k(), qi.k()], writes=[t.k()])
            E("vector", lambda h: h.scalar_tensor_tensor(
                out_ap, d.ap, -0.5, t.ap, ALU.is_lt, ALU.subtract),
              reads=[d.k(), t.k()], writes=list(out_keys))

        for nm, src in (("identf", identf_d), ("identb", identb_d),
                        ("mask16", mask16_d), ("selb", selb_d),
                        ("onesb", onesb_d), ("onesf", onesf_d), ("bqkvp", bqkvp_d),
                        ("b1p", b1p_d), ("b2p", b2p_d), ("bqkv1", bqkv1_d)):
            E("sync", lambda h, nm=nm, src=src: h.dma_start(
                out=co[nm].ap[:, :], in_=src[:, :]),
              writes=[K(nm)], dma=True)
        bpstg = AUX.view(0, F32, 1, D)
        E("sync", lambda h: h.dma_start(out=bpstg.ap, in_=bprojr[:, :]),
          writes=[bpstg.k()], dma=True)
        E("vector", lambda h: h.tensor_copy(co["bprojb"].ap[:, :], bpstg.ap),
          reads=[bpstg.k()], writes=[K("bprojb")])
        scstg = AUX.view(4096, F32, 1, 1)
        E("sync", lambda h: h.dma_start(out=scstg.ap, in_=sc0[:, :]),
          writes=[scstg.k()], dma=True)
        E("vector", lambda h: h.memset(co["half"].ap[:, :], 0.5),
          writes=[K("half")])
        bcast_row(scstg.ap, [scstg.k()], co["sc128"].ap[:, :], [K("sc128")], 1)

        _lnrot = [0]

        def layer_norm(tmp, outap, outkeys, lnsq):
            base = (_lnrot[0] % 4) * 6
            _lnrot[0] += 1
            ln = co["ln"].ap
            lnk = lambda i: K("ln", base + i, base + i + 1)
            c = lambda i: ln[:, base + i:base + i + 1]
            E("vector", lambda h: h.reduce_sum(c(0), tmp.ap, AX),
              reads=[tmp.k()], writes=[lnk(0)])
            E("scalar", lambda h: h.activation(lnsq.ap, tmp.ap, ACT.Square,
                                               accum_out=c(1)),
              reads=[tmp.k()], writes=[lnsq.k(), lnk(1)])
            E("vector", lambda h: h.tensor_scalar(c(2), c(0),
                                                  1.0 / D, None, ALU.mult),
              reads=[lnk(0)], writes=[lnk(2)])
            E("vector", lambda h: h.tensor_tensor(c(3), c(2), c(2), ALU.mult),
              reads=[lnk(2)], writes=[lnk(3)])
            E("vector", lambda h: h.scalar_tensor_tensor(
                c(4), c(1), 1.0 / D, c(3), ALU.mult, ALU.subtract),
              reads=[lnk(1), lnk(3)], writes=[lnk(4)])
            E("vector", lambda h: h.tensor_scalar(c(4), c(4), 1e-5, None,
                                                  ALU.add),
              reads=[lnk(4)], writes=[lnk(4)])
            E("vector", lambda h: h.reciprocal(c(5), c(4)),
              reads=[lnk(4)], writes=[lnk(5)])
            E("scalar", lambda h: h.activation(c(5), c(5), ACT.Sqrt),
              reads=[lnk(5)], writes=[lnk(5)])
            E("vector", lambda h: h.tensor_scalar(
                outap, tmp.ap, c(2), c(5), ALU.subtract, ALU.mult),
              reads=[tmp.k(), lnk(2), lnk(5)], writes=list(outkeys))

        xstg = [AUX.view(0, F32, 128, 1024), AUX.view(4096, F32, 128, 1024)]
        for i in range(NT):
            b = i % 2
            E("sync", lambda h, i=i, b=b: h.dma_start(
                out=xstg[b].ap, in_=x_in[i * 128:(i + 1) * 128, :]),
              reads=[xstg[b].k()], writes=[xstg[b].k()], dma=True)
            E("vector", lambda h, i=i, b=b: h.tensor_copy(
                stream_t(i), xstg[b].ap),
              reads=[xstg[b].k()], writes=[skey(i)])

        wq_bf = BIG.view(0, BF16, 128, 8 * E3)
        wp_bf = ARN.view(49152, BF16, 128, 8 * D)
        wstg = [AUX.view(0, F32, 128, E3)]
        for c in range(8):
            b = 0
            E("sync", lambda h, c=c, b=b: h.dma_start(
                out=wstg[b].ap, in_=wqkvT[c * 128:(c + 1) * 128, :]),
              reads=[wstg[b].k()], writes=[wstg[b].k()], dma=True)
            E("vector", lambda h, c=c, b=b: h.tensor_copy(
                wq_bf.ap[:, c * E3:(c + 1) * E3], wstg[b].ap),
              reads=[wstg[b].k()], writes=[wq_bf.k(c * E3, (c + 1) * E3)])
        wpstg = AUX.view(0, F32, 128, D)
        for c in range(8):
            E("sync", lambda h, c=c: h.dma_start(
                out=wpstg.ap, in_=wprojT[c * 128:(c + 1) * 128, :]),
              reads=[wpstg.k()], writes=[wpstg.k()], dma=True)
            E("vector", lambda h, c=c: h.tensor_copy(
                wp_bf.ap[:, c * D:(c + 1) * D], wpstg.ap),
              reads=[wpstg.k()], writes=[wp_bf.k(c * D, (c + 1) * D)])
        E("vector", lambda h: h.tensor_scalar(
            co["sbq"].ap[:, :], co["bqkvp"].ap[:, :], co["sc128"].ap[:, 0:1],
            None, ALU.mult),
          reads=[K("bqkvp"), K("sc128")], writes=[K("sbq")])

        xT = ARN.view(0, BF16, 128, 4096)
        qb = ARN.view(8192, BF16, 128, 4096)
        kb = ARN.view(16384, BF16, 128, 4096)
        vb = ARN.view(24576, BF16, 128, 4096)
        kvb = ARN.view(32768, BF16, 128, 4096)
        kcs = ARN.view(0, BF16, 128, 4096)
        kvcs = ARN.view(24576, BF16, 128, 4096)
        yb = kb
        zer = AUX.view(4096, BF16, 128, 512)
        ytmp = AUX.view(5120, F32, 128, 512)
        el1 = [AUX.view(7168, F32, 128, 512), AUX.view(7168, F32, 128, 512)]
        el2 = [AUX.view(9216, BF16, 128, 512), AUX.view(9216, BF16, 128, 512)]
        el3 = [AUX.view(10240, BF16, 128, 512), AUX.view(10240, BF16, 128, 512)]
        den = BIG.view(53248, F32, 16, 512)
        denb = BIG.view(55296, BF16, 16, 512)
        tmpr = AUX.view(0, F32, 128, 1024)
        lnsq_a = BIG.view(49152, F32, 128, 1024)

        E("vector", lambda h: h.memset(zer.ap, 0.0), writes=[zer.k()])
        E("vector", lambda h: h.memset(co["carryk"].ap[:, :], 0.0),
          writes=[K("carryk")])
        E("vector", lambda h: h.memset(co["carrykv"].ap[:, :], 0.0),
          writes=[K("carrykv")])

        w2stgA = [BIG.view(58624, F32, 128, 512), BIG.view(60672, F32, 128, 512)]
        w2mA = BIG.view(62720, F32, 128, 512)
        w2qiA = ARN.view(43008, I32, 128, 512)
        w2dA = ARN.view(45056, F32, 128, 512)
        w2tA = ARN.view(47104, F32, 128, 512)
        w2nqA = ARN.view(43008, F32, 128, 512)
        w2qnA = [ARN.view(45056, BF16, 128, 512),
                 ARN.view(46080, BF16, 128, 512)]
        sw2p = co["bq2p"].ap
        swtA = co["swt"].ap
        swkA = lambda i: K("swt", i, i + 1)

        def w2_scale(j):
            dch, hf = divmod(j, 8)
            b = j % 2
            E("scalar", lambda h, dch=dch, hf=hf, b=b: h.dma_start(
                out=w2stgA[b].ap,
                in_=w2n[dch * 128:(dch + 1) * 128, hf * 512:(hf + 1) * 512]),
              reads=[w2stgA[b].k()], writes=[w2stgA[b].k()], dma=True)
            if hf == 0:
                E("vector", lambda h, b=b: h.reduce_max(
                    swtA[:, 0:1], w2stgA[b].ap, AX,
                    apply_absolute_value=True),
                  reads=[w2stgA[b].k()], writes=[swkA(0)])
            else:
                E("vector", lambda h, b=b: h.reduce_max(
                    swtA[:, 1:2], w2stgA[b].ap, AX,
                    apply_absolute_value=True),
                  reads=[w2stgA[b].k()], writes=[swkA(1)])
                E("vector", lambda h: h.tensor_max(
                    swtA[:, 0:1], swtA[:, 0:1], swtA[:, 1:2]),
                  reads=[swkA(0), swkA(1)], writes=[swkA(0)])
            if hf == 7:
                E("vector", lambda h, dch=dch: h.tensor_scalar(
                    sw2p[:, dch:dch + 1], swtA[:, 0:1], 1e-8, 1.0 / 127.0,
                    ALU.max, ALU.mult),
                  reads=[swkA(0)], writes=[K("bq2p", dch, dch + 1)])
                E("vector", lambda h, dch=dch: h.reciprocal(
                    co["rsw2"].ap[:, dch:dch + 1], sw2p[:, dch:dch + 1]),
                  reads=[K("bq2p", dch, dch + 1)],
                  writes=[K("rsw2", dch, dch + 1)])
                E("vector", lambda h, dch=dch: h.tensor_scalar(
                    co["negsw2"].ap[:, dch:dch + 1],
                    sw2p[:, dch:dch + 1], -1.0, None, ALU.mult),
                  reads=[K("bq2p", dch, dch + 1)],
                  writes=[K("negsw2", dch, dch + 1)])

        def w2_quantp(j):
            dch, hf = divmod(j, 8)
            b = j % 2
            E("scalar", lambda h, dch=dch, hf=hf, b=b: h.dma_start(
                out=w2stgA[b].ap,
                in_=w2n[dch * 128:(dch + 1) * 128, hf * 512:(hf + 1) * 512]),
              reads=[w2stgA[b].k()], writes=[w2stgA[b].k()], dma=True)
            E("scalar", lambda h, b=b, dch=dch: h.activation(
                w2mA.ap, w2stgA[b].ap, ACT.Identity,
                scale=co["rsw2"].ap[:, dch:dch + 1]),
              reads=[w2stgA[b].k(), K("rsw2", dch, dch + 1)],
              writes=[w2mA.k()])
            E("scalar", lambda h: h.copy(w2qiA.ap, w2mA.ap),
              reads=[w2mA.k()], writes=[w2qiA.k()])
            E("vector", lambda h: h.tensor_tensor(
                w2dA.ap, w2mA.ap, w2qiA.ap, ALU.subtract),
              reads=[w2mA.k(), w2qiA.k()], writes=[w2dA.k()])
            E("vector", lambda h: h.scalar_tensor_tensor(
                w2tA.ap, w2dA.ap, 0.5, w2qiA.ap, ALU.is_gt, ALU.add),
              reads=[w2dA.k(), w2qiA.k()], writes=[w2tA.k()])
            E("vector", lambda h: h.scalar_tensor_tensor(
                w2nqA.ap, w2dA.ap, -0.5, w2tA.ap, ALU.is_lt, ALU.subtract),
              reads=[w2dA.k(), w2tA.k()], writes=[w2nqA.k()])
            E("vector", lambda h, b=b, dch=dch: h.tensor_scalar(
                w2qnA[b].ap, w2nqA.ap, co["negsw2"].ap[:, dch:dch + 1],
                None, ALU.mult),
              reads=[w2nqA.k(), K("negsw2", dch, dch + 1)],
              writes=[w2qnA[b].k()])
            E("gpsimd", lambda h, j=j, b=b: h.dma_start(
                out=w2q_hbm[j, :, :], in_=w2qnA[b].ap),
              reads=[w2qnA[b].k()],
              writes=[dk("w2q_hbm", j * 128 * 512 * 2,
                         (j + 1) * 128 * 512 * 2)],
              dma=True, fam="dram:w2q_hbm")

        _half = max(1, NB // 2)
        _per = (64 + _half - 1) // _half

        def w2_pre(blk):
            if blk < _half:
                for j in range(blk * _per, min(64, (blk + 1) * _per)):
                    w2_scale(j)
            if NB == 1 or blk >= _half:
                bq = blk - _half if NB > 1 else 0
                for j in range(bq * _per, min(64, (bq + 1) * _per)):
                    w2_quantp(j)

        for blk in range(NB):
            tiles = [blk * 4 + j for j in range(4)]
            for j, ti in enumerate(tiles):
                for c in range(8):
                    sl = slice(c * 512 + j * 128, c * 512 + j * 128 + 128)
                    transpose_128(
                        stream_t(ti)[:, c * 128:(c + 1) * 128], [skey(ti)],
                        xT.ap[:, sl], [xT.k(sl.start, sl.stop)],
                        2 if (j * 8 + c) % 2 == 0 else 7)
            for e in range(24):
                pb = (e % 2) * 512
                for c in range(8):
                    wsl = wq_bf.k(c * E3 + e * 128, c * E3 + e * 128 + 128)
                    E("tensor", lambda h, e=e, c=c, pb=pb: h.matmul(
                        ps(pb, 512),
                        wq_bf.ap[:, c * E3 + e * 128: c * E3 + e * 128 + 128],
                        xT.ap[:, c * 512:(c + 1) * 512],
                        start=(c == 0), stop=(c == 7)),
                      reads=[wsl, xT.k(c * 512, (c + 1) * 512)],
                      writes=[psk(pb, 512)])
                cc_ = e % 8
                sl = slice(cc_ * 512, (cc_ + 1) * 512)
                if e >= 16:
                    E("scalar", lambda h, sl=sl, pb=pb, e=e: h.activation(
                        vb.ap[:, sl], ps(pb, 512), ACT.Identity,
                        bias=co["bqkvp"].ap[:, e:e + 1]),
                      reads=[psk(pb, 512), K("bqkvp")],
                      writes=[vb.k(sl.start, sl.stop)])
                else:
                    eb = e % 2
                    dst = qb if e < 8 else kb
                    E("vector", lambda h, pb=pb, e=e, eb=eb: h.scalar_tensor_tensor(
                        el1[eb].ap, ps(pb, 512), co["sc128"].ap[:, 0:1],
                        co["sbq"].ap[:, e:e + 1].broadcast_to([128, 512]),
                        ALU.mult, ALU.add),
                      reads=[psk(pb, 512), K("sc128"), K("sbq")],
                      writes=[el1[eb].k()])
                    E("vector", lambda h, eb=eb: h.tensor_scalar(
                        el2[eb].ap, el1[eb].ap, 0.0, None, ALU.min),
                      reads=[el1[eb].k()], writes=[el2[eb].k()])
                    E("scalar", lambda h, eb=eb: h.activation(
                        el2[eb].ap, el2[eb].ap, ACT.Exp),
                      reads=[el2[eb].k()], writes=[el2[eb].k()])
                    E("scalar", lambda h, eb=eb: h.activation(
                        el3[eb].ap, el1[eb].ap, ACT.Relu),
                      reads=[el1[eb].k()], writes=[el3[eb].k()])
                    E("vector", lambda h, dst=dst, sl=sl, eb=eb: h.tensor_add(
                        dst.ap[:, sl], el2[eb].ap, el3[eb].ap),
                      reads=[el2[eb].k(), el3[eb].k()],
                      writes=[dst.k(sl.start, sl.stop)])
            for c in range(8):
                sl = slice(c * 512, (c + 1) * 512)
                E("vector", lambda h, sl=sl: h.tensor_mul(
                    kvb.ap[:, sl], kb.ap[:, sl], vb.ap[:, sl]),
                  reads=[kb.k(sl.start, sl.stop), vb.k(sl.start, sl.stop)],
                  writes=[kvb.k(sl.start, sl.stop)])
            for c in range(8):
                sl = slice(c * 512, (c + 1) * 512)
                E("vector", lambda h, c=c, sl=sl: h.tensor_tensor_scan(
                    kcs.ap[:, sl], kb.ap[:, sl], zer.ap,
                    co["carryk"].ap[:, c:c + 1], ALU.add, ALU.add),
                  reads=[kb.k(sl.start, sl.stop), zer.k(), K("carryk", c, c + 1)],
                  writes=[kcs.k(sl.start, sl.stop)])
                E("vector", lambda h, c=c, sl=sl: h.tensor_copy(
                    co["carryk"].ap[:, c:c + 1], kcs.ap[:, sl.stop - 1:sl.stop]),
                  reads=[kcs.k(sl.stop - 1, sl.stop)],
                  writes=[K("carryk", c, c + 1)])
            for c in range(8):
                sl = slice(c * 512, (c + 1) * 512)
                E("vector", lambda h, c=c, sl=sl: h.tensor_tensor_scan(
                    kvcs.ap[:, sl], kvb.ap[:, sl], zer.ap,
                    co["carrykv"].ap[:, c:c + 1], ALU.add, ALU.add),
                  reads=[kvb.k(sl.start, sl.stop), zer.k(),
                         K("carrykv", c, c + 1)],
                  writes=[kvcs.k(sl.start, sl.stop)])
                E("vector", lambda h, c=c, sl=sl: h.tensor_copy(
                    co["carrykv"].ap[:, c:c + 1], kvcs.ap[:, sl.stop - 1:sl.stop]),
                  reads=[kvcs.k(sl.stop - 1, sl.stop)],
                  writes=[K("carrykv", c, c + 1)])
            prodb = kvb
            for c in range(8):
                sl = slice(c * 512, (c + 1) * 512)
                E("vector", lambda h, sl=sl: h.tensor_mul(
                    prodb.ap[:, sl], qb.ap[:, sl], kcs.ap[:, sl]),
                  reads=[qb.k(sl.start, sl.stop), kcs.k(sl.start, sl.stop)],
                  writes=[prodb.k(sl.start, sl.stop)])
            for c in range(8):
                E("tensor", lambda h, c=c: h.matmul(
                    ps(1536, 512)[0:16, :],
                    co["mask16"].ap[:, c * 16:(c + 1) * 16],
                    prodb.ap[:, c * 512:(c + 1) * 512],
                    start=(c == 0), stop=(c == 7)),
                  reads=[K("mask16"), prodb.k(c * 512, (c + 1) * 512)],
                  writes=[psk(1536, 512)])
            E("vector", lambda h: h.tensor_scalar_max(
                den.ap, ps(1536, 512)[0:16, :], 1e-3),
              reads=[psk(1536, 512)], writes=[den.k()])
            E("vector", lambda h: h.reciprocal(den.ap, den.ap),
              reads=[den.k()], writes=[den.k()])
            E("vector", lambda h: h.tensor_copy(denb.ap, den.ap),
              reads=[den.k()], writes=[denb.k()])
            for c in range(8):
                sl = slice(c * 512, (c + 1) * 512)
                E("tensor", lambda h, c=c: h.matmul(
                    ps(2048, 512), co["selb"].ap[:, c * 128:(c + 1) * 128],
                    denb.ap, start=True, stop=True),
                  reads=[K("selb"), denb.k()], writes=[psk(2048, 512)])
                E("vector", lambda h, sl=sl: h.tensor_mul(
                    ytmp.ap, qb.ap[:, sl], kvcs.ap[:, sl]),
                  reads=[qb.k(sl.start, sl.stop), kvcs.k(sl.start, sl.stop)],
                  writes=[ytmp.k()])
                E("vector", lambda h, sl=sl: h.tensor_mul(
                    yb.ap[:, sl], ytmp.ap, ps(2048, 512)),
                  reads=[ytmp.k(), psk(2048, 512)],
                  writes=[yb.k(sl.start, sl.stop)])
            for j, ti in enumerate(tiles):
                plo = 2560 if j % 2 == 0 else 0
                pv, pks = ps(plo, 1024), psk(plo, 1024)
                for nb in range(2):
                    E("tensor", lambda h, pv=pv, nb=nb: h.matmul(
                        pv[:, nb * 512:(nb + 1) * 512], co["onesb"].ap[:, :],
                        co["bprojb"].ap[:, nb * 512:(nb + 1) * 512],
                        start=True, stop=False),
                      reads=[K("onesb"), K("bprojb")], writes=[pks])
                for c in range(8):
                    ysl = yb.k(c * 512 + j * 128, c * 512 + j * 128 + 128)
                    for nb in range(2):
                        E("tensor", lambda h, c=c, nb=nb, pv=pv, j=j: h.matmul(
                            pv[:, nb * 512:(nb + 1) * 512],
                            yb.ap[:, c * 512 + j * 128: c * 512 + j * 128 + 128],
                            wp_bf.ap[:, c * D + nb * 512: c * D + nb * 512 + 512],
                            start=False, stop=(c == 7)),
                          reads=[ysl, wp_bf.k(c * D + nb * 512,
                                              c * D + nb * 512 + 512)],
                          writes=[pks])
                E("vector", lambda h, pv=pv, ti=ti: h.tensor_add(
                    tmpr.ap, pv, stream_t(ti)),
                  reads=[pks, skey(ti)], writes=[tmpr.k()])
                layer_norm(tmpr, stream_t(ti), [skey(ti)], lnsq_a)
            w2_pre(blk)

        mx = co["gm"].ap
        mxk = lambda i: K("gm", i, i + 1)
        for i in range(NT):
            if i == 0:
                E("vector", lambda h, i=i: h.reduce_max(
                    mx[:, 0:1], stream_t(i), AX),
                  reads=[skey(i)], writes=[mxk(0)])
            else:
                E("vector", lambda h, i=i: h.reduce_max(
                    mx[:, 1:2], stream_t(i), AX),
                  reads=[skey(i)], writes=[mxk(1)])
                E("vector", lambda h: h.tensor_max(mx[:, 0:1], mx[:, 0:1],
                                                   mx[:, 1:2]),
                  reads=[mxk(0), mxk(1)], writes=[mxk(0)])

        def global_scale(idx, skey_s, skey_rs):
            preduce(mx[:, 0:1], [mxk(0)], mx[0:1, 2:3], [mxk(2)], 128, ALU.max)
            E("gpsimd", lambda h, idx=idx: h.dma_start(
                out=mxio[idx:idx + 1, :], in_=mx[0:1, 2:3]),
              reads=[mxk(2)], writes=[dk("mxio", idx, idx + 1)], dma=True)
            E("gpsimd", lambda h, idx=idx: h.collective_compute(
                "AllReduce", ALU.max, replica_groups=[list(range(ncores))],
                ins=[mxio[idx:idx + 1, :].opt()],
                outs=[mxout[idx:idx + 1, :].opt()]),
              reads=[dk("mxio", idx, idx + 1)],
              writes=[dk("mxout", idx, idx + 1)], cc=True)
            E("gpsimd", lambda h, idx=idx: h.dma_start(
                out=mx[0:1, 3:4], in_=mxout[idx:idx + 1, :]),
              reads=[dk("mxout", idx, idx + 1)], writes=[mxk(3)], dma=True)
            bcast_row(mx[0:1, 3:4], [mxk(3)], co[skey_s].ap[:, :],
                      [K(skey_s)], 1)
            E("vector", lambda h: h.tensor_scalar(
                co[skey_s].ap[:, :], co[skey_s].ap[:, :], 1e-8, 1.0 / 255.0,
                ALU.max, ALU.mult),
              reads=[K(skey_s)], writes=[K(skey_s)])
            E("vector", lambda h: h.reciprocal(co[skey_rs].ap[:, :],
                                               co[skey_s].ap[:, :]),
              reads=[K(skey_s)], writes=[K(skey_rs)])

        global_scale(0, "s1", "rs1")

        u1T = ARN.view(0, BF16, 128, 8 * T)
        qm5 = BIG.view(32768, F32, 128, 1024)
        qng = BIG.view(36864, F32, 128, 1024)
        u1tile = BIG.view(40960, BF16, 128, 1024)
        rqiA = BIG.view(45056, I32, 128, 1024)
        rdA = BIG.view(49152, F32, 128, 1024)
        rtA = BIG.view(53248, F32, 128, 1024)
        u1tl = [u1tile, BIG.view(57344, BF16, 128, 1024)]

        def u1_quant(i):
            b = i % 2
            E("scalar", lambda h, i=i: h.activation(
                qm5.ap, stream_t(i), ACT.Identity,
                scale=co["rs1"].ap[:, 0:1]),
              reads=[skey(i), K("rs1")], writes=[qm5.k()])
            round_neg(qm5.ap, [qm5.k()], rqiA, rdA, rtA, qng.ap, [qng.k()])
            E("scalar", lambda h, b=b: h.activation(u1tl[b].ap, qng.ap,
                                                    ACT.Relu, scale=-1.0),
              reads=[qng.k()], writes=[u1tl[b].k()])

        def u1_tr(i):
            b = i % 2
            for c in range(8):
                sl = slice(c * T + i * 128, c * T + i * 128 + 128)
                transpose_128(
                    u1tl[b].ap[:, c * 128:(c + 1) * 128], [u1tl[b].k()],
                    u1T.ap[:, sl], [u1T.k(sl.start, sl.stop)],
                    c % 4)

        u1_quant(0)
        for i in range(1, NT):
            u1_quant(i)
            u1_tr(i - 1)
        u1_tr(NT - 1)
        w1stg = [AUX.view(0, F32, 128, 1024), AUX.view(4096, F32, 128, 1024)]
        w1qn = AUX.view(8192, BF16, 128, 1024)
        w1qT = [AUX.view(10240, BF16, 128, 1024), BIG.view(43008, BF16, 128, 1024)]
        lnsq_b = ARN.view(4096, F32, 128, 1024)
        hstg = [BIG.view(0, F32, 128, T), BIG.view(4 * T, F32, 128, T)]
        swt = co["swt"].ap
        swk = lambda i: K("swt", i, i + 1)
        NTP = T // 512
        E("vector", lambda h: h.memset(mx[:, 0:1], 0.0), writes=[mxk(0)])

        def f1_load_quant(f):
            b = f % 2
            E("sync", lambda h, f=f, b=b: h.dma_start(
                out=w1stg[b].ap, in_=w1n[f * 128:(f + 1) * 128, :]),
              reads=[w1stg[b].k()], writes=[w1stg[b].k()], dma=True)
            E("vector", lambda h, b=b: h.reduce_max(
                swt[:, 0:1], w1stg[b].ap, AX, apply_absolute_value=True),
              reads=[w1stg[b].k()], writes=[swk(0)])
            E("vector", lambda h: h.tensor_scalar(
                swt[:, 1:2], swt[:, 0:1], 1e-8, 1.0 / 127.0, ALU.max, ALU.mult),
              reads=[swk(0)], writes=[swk(1)])
            E("vector", lambda h: h.reciprocal(swt[:, 2:3], swt[:, 1:2]),
              reads=[swk(1)], writes=[swk(2)])
            E("vector", lambda h: h.tensor_tensor(
                swt[:, 3:4], swt[:, 1:2], co["s1"].ap[:, 0:1], ALU.mult),
              reads=[swk(1), K("s1")], writes=[swk(3)])
            E("vector", lambda h: h.tensor_scalar(
                swt[:, 3:4], swt[:, 3:4], -1.0, None, ALU.mult),
              reads=[swk(3)], writes=[swk(3)])
            E("scalar", lambda h, b=b: h.activation(
                qm5.ap, w1stg[b].ap, ACT.Identity, scale=swt[:, 2:3]),
              reads=[w1stg[b].k(), swk(2)], writes=[qm5.k()])
            round_neg(qm5.ap, [qm5.k()], rqiA, rdA, rtA, qng.ap, [qng.k()])
            E("vector", lambda h: h.tensor_scalar(
                w1qn.ap, qng.ap, swt[:, 3:4], None, ALU.mult),
              reads=[qng.k(), swk(3)], writes=[w1qn.k()])
            bqa = co["bq1"].ap
            bk_ = K("bq1", f, f + 1)
            E("vector", lambda h, f=f: h.tensor_tensor(
                bqa[:, f:f + 1], co["b1p"].ap[:, f:f + 1],
                co["rs1"].ap[:, 0:1], ALU.mult),
              reads=[K("b1p"), K("rs1")], writes=[bk_])
            E("vector", lambda h, f=f: h.tensor_tensor(
                bqa[:, f:f + 1], bqa[:, f:f + 1], swt[:, 2:3], ALU.mult),
              reads=[bk_, swk(2)], writes=[bk_])
            E("vector", lambda h, f=f: h.tensor_copy(
                co["rqa"].ap[:, 0:1], bqa[:, f:f + 1]),
              reads=[bk_], writes=[K("rqa")])
            E("vector", lambda h, f=f: h.tensor_tensor(
                co["ln"].ap[:, 24:25], bqa[:, f:f + 1], co["rqa"].ap[:, 0:1],
                ALU.subtract),
              reads=[bk_, K("rqa")], writes=[K("ln", 24, 25)])
            E("vector", lambda h: h.scalar_tensor_tensor(
                co["ln"].ap[:, 25:26], co["ln"].ap[:, 24:25], 0.5,
                co["rqa"].ap[:, 0:1], ALU.is_gt, ALU.add),
              reads=[K("ln", 24, 25), K("rqa")], writes=[K("ln", 25, 26)])
            E("vector", lambda h, f=f: h.scalar_tensor_tensor(
                bqa[:, f:f + 1], co["ln"].ap[:, 24:25], -0.5,
                co["ln"].ap[:, 25:26], ALU.is_lt, ALU.subtract),
              reads=[K("ln", 24, 25), K("ln", 25, 26)], writes=[bk_])
            E("vector", lambda h, f=f: h.tensor_scalar(
                bqa[:, f:f + 1], bqa[:, f:f + 1], -1.0, None, ALU.mult),
              reads=[bk_], writes=[bk_])
            E("vector", lambda h, f=f: h.tensor_tensor(
                bqa[:, f:f + 1], bqa[:, f:f + 1], swt[:, 1:2], ALU.mult),
              reads=[bk_, swk(1)], writes=[bk_])
            E("vector", lambda h, f=f: h.tensor_tensor(
                bqa[:, f:f + 1], bqa[:, f:f + 1], co["s1"].ap[:, 0:1], ALU.mult),
              reads=[bk_, K("s1")], writes=[bk_])
            for c in range(8):
                transpose_128(
                    w1qn.ap[:, c * 128:(c + 1) * 128], [w1qn.k()],
                    w1qT[b].ap[:, c * 128:(c + 1) * 128],
                    [w1qT[b].k(c * 128, (c + 1) * 128)],
                    c % 2)

        def f1_mms(f):
            b = f % 2
            hb = f % 2
            for c in range(8):
                wk_ = w1qT[b].k(c * 128, (c + 1) * 128)
                for tp in range(NTP):
                    E("tensor", lambda h, c=c, tp=tp, b=b: h.matmul(
                        ps(tp * 512, 512),
                        w1qT[b].ap[:, c * 128:(c + 1) * 128],
                        u1T.ap[:, c * T + tp * 512: c * T + tp * 512 + 512],
                        start=(c == 0), stop=(c == 7)),
                      reads=[wk_, u1T.k(c * T + tp * 512, c * T + tp * 512 + 512)],
                      writes=[psk(tp * 512, 512)])
            for tp in range(NTP):
                E("scalar", lambda h, tp=tp, hb=hb, f=f: h.activation(
                    hstg[hb].ap[:, tp * 512:(tp + 1) * 512], ps(tp * 512, 512),
                    ACT.Relu, bias=co["bq1"].ap[:, f:f + 1]),
                  reads=[psk(tp * 512, 512), K("bq1", f, f + 1)],
                  writes=[hstg[hb].k(tp * 512, (tp + 1) * 512)])

        def f1_spill(f):
            hb = f % 2
            E("vector", lambda h, hb=hb: h.reduce_max(
                mx[:, 1:2], hstg[hb].ap, AX),
              reads=[hstg[hb].k()], writes=[mxk(1)])
            E("vector", lambda h: h.tensor_max(mx[:, 0:1], mx[:, 0:1],
                                               mx[:, 1:2]),
              reads=[mxk(0), mxk(1)], writes=[mxk(0)])
            E("scalar", lambda h, f=f, hb=hb: h.dma_start(
                out=h_hbm[f, :, :], in_=hstg[hb].ap),
              reads=[hstg[hb].k()],
              writes=[dk("h_hbm", f * 128 * T, (f + 1) * 128 * T)], dma=True)

        f1_load_quant(0)
        f1_load_quant(1)
        f1_mms(0)
        for f in range(2, NFC):
            f1_load_quant(f)
            f1_mms(f - 1)
            f1_spill(f - 2)
        f1_mms(NFC - 1)
        f1_spill(NFC - 2)
        f1_spill(NFC - 1)

        global_scale(1, "s2", "rs2")
        E("vector", lambda h: h.tensor_scalar(
            co["negs2"].ap[:, :], co["s2"].ap[:, :], -1.0, None, ALU.mult),
          reads=[K("s2")], writes=[K("negs2")])

        w2qT = BIG.view(0, BF16, 128, 32 * D)
        w2qn = [ARN.view(16384, BF16, 128, 512), ARN.view(17408, BF16, 128, 512)]
        sw2p = co["bq2p"].ap

        def w2_lm(j):
            b = j % 2
            E("sync", lambda h, j=j, b=b: h.dma_start(
                out=w2qn[b].ap, in_=w2q_hbm[j, :, :]),
              reads=[w2qn[b].k(),
                     dk("w2q_hbm", j * 128 * 512 * 2, (j + 1) * 128 * 512 * 2)],
              writes=[w2qn[b].k()], dma=True)

        def w2_qt(j):
            dch, hf = divmod(j, 8)
            b = j % 2
            for fc in range(4):
                fa = hf * 4 + fc
                sl = slice(fa * D + dch * 128, fa * D + dch * 128 + 128)
                transpose_128(
                    w2qn[b].ap[:, fc * 128:(fc + 1) * 128], [w2qn[b].k()],
                    w2qT.ap[:, sl], [w2qT.k(sl.start, sl.stop)],
                    6 + (fc % 2))

        w2_lm(0)
        for j in range(1, 64):
            w2_lm(j)
            w2_qt(j - 1)
        w2_qt(63)

        bt = co["ln"].ap
        btk = lambda i: K("ln", i, i + 1)
        for dch in range(8):
            E("vector", lambda h, dch=dch: h.reciprocal(
                bt[:, 0:1], sw2p[:, dch:dch + 1]),
              reads=[K("bq2p", dch, dch + 1)], writes=[btk(0)])
            E("vector", lambda h, dch=dch: h.tensor_tensor(
                bt[:, 1:2], co["b2p"].ap[:, dch:dch + 1], bt[:, 0:1], ALU.mult),
              reads=[K("b2p"), btk(0)], writes=[btk(1)])
            E("vector", lambda h: h.tensor_tensor(
                bt[:, 1:2], bt[:, 1:2], co["rs2"].ap[:, 0:1], ALU.mult),
              reads=[btk(1), K("rs2")], writes=[btk(1)])
            E("vector", lambda h: h.tensor_copy(co["rqa"].ap[:, 0:1],
                                                bt[:, 1:2]),
              reads=[btk(1)], writes=[K("rqa")])
            E("vector", lambda h: h.tensor_tensor(
                bt[:, 2:3], bt[:, 1:2], co["rqa"].ap[:, 0:1], ALU.subtract),
              reads=[btk(1), K("rqa")], writes=[btk(2)])
            E("vector", lambda h: h.scalar_tensor_tensor(
                bt[:, 3:4], bt[:, 2:3], 0.5, co["rqa"].ap[:, 0:1],
                ALU.is_gt, ALU.add),
              reads=[btk(2), K("rqa")], writes=[btk(3)])
            E("vector", lambda h: h.scalar_tensor_tensor(
                bt[:, 1:2], bt[:, 2:3], -0.5, bt[:, 3:4],
                ALU.is_lt, ALU.subtract),
              reads=[btk(2), btk(3)], writes=[btk(1)])
            E("vector", lambda h, dch=dch: h.tensor_tensor(
                co["bq2p"].ap[:, dch:dch + 1], bt[:, 1:2],
                sw2p[:, dch:dch + 1], ALU.mult),
              reads=[btk(1), K("bq2p", dch, dch + 1)],
              writes=[K("bq2p", dch, dch + 1)])
        bq2f = AUX.view(4096, F32, 1, D)
        E("sync", lambda h: h.dma_start(out=bq2s, in_=co["bq2p"].ap[:, 0:8]),
          reads=[K("bq2p")], writes=[dk("bq2s")], dma=True)
        E("sync", lambda h: h.dma_start(
            out=bq2f.ap,
            in_=bq2s.transpose([1, 0]).unsqueeze(0),
            allow_slow_non_contiguous=True),
          reads=[dk("bq2s")], writes=[bq2f.k()], dma=True)
        E("vector", lambda h: h.tensor_copy(co["bq2r"].ap[:, :], bq2f.ap),
          reads=[bq2f.k()], writes=[K("bq2r")])

        hldg = [AUX.view(4096, F32, 128, 1024), AUX.view(8192, F32, 128, 1024)]
        hm2 = [ARN.view(8192, F32, 128, 1024), ARN.view(12288, F32, 128, 1024)]
        uqi = [ARN.view(16384, I32, 128, 1024), ARN.view(20480, I32, 128, 1024)]
        ud = [ARN.view(24576, F32, 128, 1024), ARN.view(28672, F32, 128, 1024)]
        ut = [ARN.view(32768, F32, 128, 1024), ARN.view(36864, F32, 128, 1024)]
        u2c = [ARN.view(40960, BF16, 128, 1024), ARN.view(43008, BF16, 128, 1024)]
        NF2 = NFC // 2
        for tt in range(T // 512):
            for sub in range(4):
                for nb in range(2):
                    lo = sub * 1024 + nb * 512
                    E("tensor", lambda h, lo=lo, nb=nb: h.matmul(
                        ps(lo, 512), co["onesb"].ap[0:1, :],
                        co["bq2r"].ap[:, nb * 512:(nb + 1) * 512],
                        start=True, stop=False),
                      reads=[K("onesb"), K("bq2r")], writes=[psk(lo, 512)])

            def b_load(i, tt=tt):
                b = i % 2
                E("sync", lambda h, i=i, tt=tt, b=b: h.dma_start(
                    out=hldg[b].ap.rearrange("p (g w) -> p g w", w=512),
                    in_=h_hbm[2 * i:2 * i + 2, :,
                              tt * 512:(tt + 1) * 512].transpose([1, 0, 2])),
                  reads=[hldg[b].k(),
                         dk("h_hbm", 2 * i * 128 * T, (2 * i + 2) * 128 * T)],
                  writes=[hldg[b].k()], dma=True)
                E("scalar", lambda h, b=b: h.activation(
                    hm2[b].ap, hldg[b].ap, ACT.Identity,
                    scale=co["rs2"].ap[:, 0:1]),
                  reads=[hldg[b].k(), K("rs2")], writes=[hm2[b].k()])
                E("scalar", lambda h, b=b: h.copy(uqi[b].ap, hm2[b].ap),
                  reads=[hm2[b].k()], writes=[uqi[b].k()])

            def b_work(i, tt=tt):
                b = i % 2
                E("vector", lambda h, b=b: h.tensor_tensor(
                    ud[b].ap, hm2[b].ap, uqi[b].ap, ALU.subtract),
                  reads=[hm2[b].k(), uqi[b].k()], writes=[ud[b].k()])
                E("vector", lambda h, b=b: h.scalar_tensor_tensor(
                    ut[b].ap, ud[b].ap, 0.5, uqi[b].ap, ALU.is_gt, ALU.add),
                  reads=[ud[b].k(), uqi[b].k()], writes=[ut[b].k()])
                E("vector", lambda h, b=b: h.scalar_tensor_tensor(
                    u2c[b].ap, ud[b].ap, -0.5, ut[b].ap,
                    ALU.is_lt, ALU.subtract),
                  reads=[ud[b].k(), ut[b].k()], writes=[u2c[b].k()])
                for ff in range(2):
                    f = 2 * i + ff
                    for sub in range(4):
                        for nb in range(2):
                            lo = sub * 1024 + nb * 512
                            E("tensor", lambda h, f=f, ff=ff, sub=sub, nb=nb,
                              b=b, lo=lo: h.matmul(
                                  ps(lo, 512),
                                  u2c[b].ap[:, ff * 512 + sub * 128:
                                            ff * 512 + sub * 128 + 128],
                                  w2qT.ap[:, f * D + nb * 512:
                                          f * D + nb * 512 + 512],
                                  start=False, stop=(f == NFC - 1)),
                              reads=[u2c[b].k(ff * 512 + sub * 128,
                                              ff * 512 + sub * 128 + 128),
                                     w2qT.k(f * D + nb * 512,
                                            f * D + nb * 512 + 512)],
                              writes=[psk(lo, 512)])

            b_load(0)
            for i in range(1, NF2):
                b_load(i)
                b_work(i - 1)
            b_work(NF2 - 1)
            for sub in range(4):
                ti = tt * 4 + sub
                E("vector", lambda h, sub=sub, ti=ti: h.scalar_tensor_tensor(
                    tmpr.ap, ps(sub * 1024, 1024), co["negs2"].ap[:, 0:1],
                    stream_t(ti), ALU.mult, ALU.add),
                  reads=[psk(sub * 1024, 1024), K("negs2"), skey(ti)],
                  writes=[tmpr.k()])
                layer_norm(tmpr, stream_t(ti), [skey(ti)], lnsq_b)

        wq1b = AUX.view(0, BF16, 128, 8 * HD)
        wk1b = AUX.view(1024, BF16, 128, 8 * HD)
        wv1b = AUX.view(2048, BF16, 128, 8 * HD)
        wl1stg = AUX.view(3072, F32, 128, HD)
        for src, dstv in ((wqT1, wq1b), (wkT1, wk1b), (wvT1, wv1b)):
            for c in range(8):
                E("sync", lambda h, src=src, c=c: h.dma_start(
                    out=wl1stg.ap, in_=src[c * 128:(c + 1) * 128, :]),
                  reads=[wl1stg.k()], writes=[wl1stg.k()], dma=True)
                E("vector", lambda h, dstv=dstv, c=c: h.tensor_copy(
                    dstv.ap[:, c * HD:(c + 1) * HD], wl1stg.ap),
                  reads=[wl1stg.k()],
                  writes=[dstv.k(c * HD, (c + 1) * HD)])
        mstg = [AUX.view(3328, F32, 128, 1024), AUX.view(7424, F32, 128, 1024)]
        mred = AUX.view(11520, F32, 128, 1)
        E("vector", lambda h: h.memset(co["macc"].ap[:, 0:3], 0.0),
          writes=[K("macc")])

        def mean_pass(idx, loader, q, stg0, mr):
            for f in range(NFC):
                b = f % 2
                E(q, lambda h, f=f, b=b: h.dma_start(
                    out=stg0[b].ap.rearrange("p (j w) -> p j w", w=128),
                    in_=loader(f)),
                  reads=[stg0[b].k()], writes=[stg0[b].k()], dma=True)
                E("vector", lambda h, b=b: h.reduce_sum(
                    mr.ap, stg0[b].ap, AX, apply_absolute_value=True),
                  reads=[stg0[b].k()], writes=[mr.k()])
                E("vector", lambda h, idx=idx: h.tensor_tensor(
                    co["macc"].ap[:, idx:idx + 1], co["macc"].ap[:, idx:idx + 1],
                    mr.ap, ALU.add),
                  reads=[mr.k(), K("macc", idx, idx + 1)],
                  writes=[K("macc", idx, idx + 1)])

        mstgu = [BIG.view(49152, F32, 128, 1024), BIG.view(53248, F32, 128, 1024)]
        mstgd = [BIG.view(57344, F32, 128, 1024), BIG.view(61440, F32, 128, 1024)]
        mredu = AUX.view(11524, F32, 128, 1)
        mredd = AUX.view(11528, F32, 128, 1)
        mean_pass(0, lambda f: wgblk[f, :, :].rearrange("(j p) w -> p j w", p=128),
                  "scalar", mstg, mred)
        mean_pass(1, lambda f: wublk[f, :, :].rearrange("(j p) w -> p j w", p=128),
                  "gpsimd", mstgu, mredu)
        mean_pass(2, lambda f: wdT[f * 128:(f + 1) * 128, :].rearrange("p (j w) -> p j w", w=128),
                  "sync", mstgd, mredd)

        kTloc = ARN.view(0, BF16, HD, T)
        vploc = ARN.view(2 * T, BF16, 128, (T // 128) * 65)
        xTb = ARN.view(((2 * T + (T // 128) * 65 * 2 + 63) // 64) * 64,
                       BF16, 128, 4096)
        qTd = ARN.view(32768 if T == 4096 else 49152, BF16, 128, T)
        vT_t = BIG.view(0, F32, HD, 512)
        for blk in range(NB):
            tiles = [blk * 4 + j for j in range(4)]
            for j, ti in enumerate(tiles):
                for c in range(8):
                    sl = slice(c * 512 + j * 128, c * 512 + j * 128 + 128)
                    transpose_128(
                        stream_t(ti)[:, c * 128:(c + 1) * 128], [skey(ti)],
                        xTb.ap[:, sl], [xTb.k(sl.start, sl.stop)],
                        0 if (j * 8 + c) % 2 == 0 else 7)
            for wi, (dstv, plo) in enumerate(((wq1b, 512), (wk1b, 1024),
                                              (wv1b, 1536))):
                for c in range(8):
                    wk_ = dstv.k(c * HD, (c + 1) * HD)
                    E("tensor", lambda h, dstv=dstv, c=c, plo=plo: h.matmul(
                        ps(plo, 512)[0:HD, :],
                        dstv.ap[:, c * HD:(c + 1) * HD],
                        xTb.ap[:, c * 512:(c + 1) * 512],
                        start=(c == 0), stop=(c == 7)),
                      reads=[wk_, xTb.k(c * 512, (c + 1) * 512)],
                      writes=[psk(plo, 512)])
            E("scalar", lambda h, blk=blk: h.activation(
                qTd.ap[0:HD, blk * 512:(blk + 1) * 512], ps(512, 512)[0:HD, :],
                ACT.Identity, bias=co["bqkv1"].ap[:, 0:1]),
              reads=[psk(512, 512), K("bqkv1")],
              writes=[qTd.k(blk * 512, (blk + 1) * 512)])
            E("scalar", lambda h, blk=blk: h.activation(
                kTloc.ap[0:HD, blk * 512:(blk + 1) * 512], ps(1024, 512)[0:HD, :],
                ACT.Identity, bias=co["bqkv1"].ap[:, 1:2]),
              reads=[psk(1024, 512), K("bqkv1")],
              writes=[kTloc.k(blk * 512, (blk + 1) * 512)])
            E("scalar", lambda h: h.activation(
                vT_t.ap, ps(1536, 512)[0:HD, :], ACT.Identity,
                bias=co["bqkv1"].ap[:, 2:3]),
              reads=[psk(1536, 512), K("bqkv1")], writes=[vT_t.k()])
            for j in range(4):
                ch = blk * 4 + j
                pslo = 2048 + (j % 2) * 512
                E("tensor", lambda h, j=j, pslo=pslo: h.transpose(
                    ps(pslo, 128)[:, 0:HD], vT_t.ap[:, j * 128:(j + 1) * 128],
                    co["identf"].ap[0:HD, 0:HD]),
                  reads=[vT_t.k(j * 128, (j + 1) * 128), K("identf")],
                  writes=[psk(pslo, 512)])
                E("vector", lambda h, ch=ch, pslo=pslo: h.tensor_copy(
                    vploc.ap[:, ch * 65: ch * 65 + 64], ps(pslo, 128)[:, 0:HD]),
                  reads=[psk(pslo, 512)], writes=[vploc.k(ch * 65, ch * 65 + 64)])
        E("vector", lambda h: h.memset(
            vploc.ap.rearrange("p (c e) -> p c e", e=65)[:, :, 64:65], 1.0),
          writes=[vploc.k()])
        E("gpsimd", lambda h: h.dma_start(out=qTd.ap[64:128, :],
                                          in_=qTd.ap[0:64, :]),
          reads=[qTd.k()], writes=[qTd.k()], dma=True)
        E("gpsimd", lambda h: h.dma_start(out=kg_in[:, :], in_=kTloc.ap[0:HD, :]),
          reads=[kTloc.k()], writes=[dk("kg_in")], dma=True)
        E("gpsimd", lambda h: h.dma_start(
            out=vg_in.rearrange("(c p) e -> p c e", p=128),
            in_=vploc.ap.rearrange("p (c e) -> p c e", e=65)),
          reads=[vploc.k()], writes=[dk("vg_in")], dma=True)
        E("gpsimd", lambda h: h.collective_compute(
            "AllGather", ALU.bypass, replica_groups=[list(range(ncores))],
            ins=[kg_in.opt()], outs=[kg_out.opt()]),
          reads=[dk("kg_in")], writes=[dk("kg_out")], cc=True)
        E("gpsimd", lambda h: h.collective_compute(
            "AllGather", ALU.bypass, replica_groups=[list(range(ncores))],
            ins=[vg_in.opt()], outs=[vg_out.opt()]),
          reads=[dk("vg_in")], writes=[dk("vg_out")], cc=True)
        kTp = ARN.view(0, BF16, 128, (NCH // 2) * 128)
        vpack = BIG.view(0, BF16, 128, NCH * 65)
        nloc = T // 128
        for i in range(ncores):
            for par in range(2):
                m0 = (i * nloc + par) // 2
                cnt = nloc // 2
                E("sync", lambda h, i=i, par=par, cnt=cnt, m0=m0: h.dma_start(
                    out=kTp.ap[par * 64:par * 64 + 64, :].rearrange(
                        "p (m w) -> p m w", w=128)[:, m0:m0 + cnt, :],
                    in_=kg_out[i * HD:(i + 1) * HD, :].rearrange(
                        "p (m w) -> p m w", w=128)[:, par::2, :]),
                  reads=[dk("kg_out")],
                  writes=[kTp.k(m0 * 128, (m0 + cnt) * 128)], dma=True,
                  fam=f"kTp{i}")
        nvch = T // 128
        for i in range(ncores):
            E("sync", lambda h, i=i: h.dma_start(
                out=vpack.ap.rearrange("p (c e) -> p c e", e=65)[
                    :, i * nvch:(i + 1) * nvch, :],
                in_=vg_out.rearrange("(c p) e -> p c e", p=128)[
                    :, i * nvch:(i + 1) * nvch, :]),
              reads=[dk("vg_out")],
              writes=[vpack.k(i * nvch * 65, (i + 1) * nvch * 65)], dma=True,
              fam=f"vpk{i}")

        E("tensor", lambda h: h.transpose(
            ps(3584, 128)[0:3, :], co["macc"].ap[:, 0:3],
            co["identf"].ap[:, :]),
          reads=[K("macc"), K("identf")], writes=[psk(3584, 512)])
        E("vector", lambda h: h.reduce_sum(
            co["macc"].ap[0:3, 3:4], ps(3584, 128)[0:3, :], AX),
          reads=[psk(3584, 512)], writes=[K("macc", 3, 4)])
        E("tensor", lambda h: h.transpose(
            ps(3584, 128)[0:1, 0:3], co["macc"].ap[0:3, 3:4],
            co["identf"].ap[0:3, 0:3]),
          reads=[K("macc", 3, 4), K("identf")], writes=[psk(3584, 512)])
        E("vector", lambda h: h.tensor_copy(co["row"].ap[0:1, 0:3],
                                            ps(3584, 128)[0:1, 0:3]),
          reads=[psk(3584, 512)], writes=[K("row", 0, 3)])
        bcast_row(co["row"].ap[0:1, 0:3], [K("row", 0, 3)],
                  co["macc"].ap[:, 0:3], [K("macc", 0, 3)], 3)
        for idx, mnm in ((0, "mg"), (1, "mu"), (2, "md")):
            E("vector", lambda h, idx=idx, mnm=mnm: h.tensor_scalar(
                co[mnm].ap[:, :], co["macc"].ap[:, idx:idx + 1],
                1.0 / (FF * D), 1e-5, ALU.mult, ALU.max),
              reads=[K("macc", idx, idx + 1)], writes=[K(mnm)])
            E("vector", lambda h, mnm=mnm: h.reciprocal(
                co["r" + mnm].ap[:, :], co[mnm].ap[:, :]),
              reads=[K(mnm)], writes=[K("r" + mnm)])
            E("vector", lambda h, mnm=mnm: h.tensor_scalar(
                co["neg" + mnm].ap[:, :], co[mnm].ap[:, :], -1.0, None,
                ALU.mult),
              reads=[K(mnm)], writes=[K("neg" + mnm)])

        q0 = 40960 if T == 4096 else T * 8
        qstg = [ARN.view(q0, F32, 128, 1024), ARN.view(q0 + 4096, F32, 128, 1024)]
        qm = ARN.view(q0 + 8192, F32, 128, 1024)
        qqi = ARN.view(q0 + 12288, I32, 128, 1024)
        qd = ARN.view(q0 + 16384, F32, 128, 1024)
        qt = ARN.view(q0 + 20480, F32, 128, 1024)
        qout = [ARN.view(q0 + 12288, BF16, 128, 1024),
                ARN.view(q0 + 14336, BF16, 128, 1024)]

        def quant_gu(f):
            for mi, (blkap, rk, dsth) in enumerate(
                    ((wgblk, "rmg", wgq_hbm), (wublk, "rmu", wuq_hbm))):
                b = (2 * f + mi) % 2
                E("sync", lambda h, f=f, b=b, blkap=blkap: h.dma_start(
                    out=qstg[b].ap.rearrange("p (j w) -> p j w", w=128),
                    in_=blkap[f, :, :].rearrange("(j p) w -> p j w", p=128)),
                  reads=[qstg[b].k()], writes=[qstg[b].k()], dma=True)
                E("vector", lambda h, b=b, rk=rk: h.tensor_scalar(
                    qm.ap, qstg[b].ap, co[rk].ap[:, 0:1], None, ALU.mult),
                  reads=[qstg[b].k(), K(rk)], writes=[qm.k()])
                E("vector", lambda h: h.tensor_copy(qqi.ap, qm.ap),
                  reads=[qm.k()], writes=[qqi.k()])
                E("vector", lambda h: h.tensor_tensor(
                    qd.ap, qm.ap, qqi.ap, ALU.subtract),
                  reads=[qm.k(), qqi.k()], writes=[qd.k()])
                E("vector", lambda h: h.scalar_tensor_tensor(
                    qt.ap, qd.ap, 0.5, qqi.ap, ALU.is_gt, ALU.add),
                  reads=[qd.k(), qqi.k()], writes=[qt.k()])
                E("vector", lambda h: h.scalar_tensor_tensor(
                    qm.ap, qd.ap, -0.5, qt.ap, ALU.is_lt, ALU.subtract),
                  reads=[qd.k(), qt.k()], writes=[qm.k()])
                E("vector", lambda h, mi=mi: h.tensor_scalar(
                    qout[mi].ap, qm.ap, -1.0, 1.0, ALU.max, ALU.min),
                  reads=[qm.k()], writes=[qout[mi].k()])
                E("gpsimd", lambda h, f=f, mi=mi, dsth=dsth: h.dma_start(
                    out=dsth[f, :, :], in_=qout[mi].ap),
                  reads=[qout[mi].k()],
                  writes=[dk("wgq_hbm" if mi == 0 else "wuq_hbm",
                             f * 128 * D * 2, (f + 1) * 128 * D * 2)],
                  dma=True,
                  fam="dram:wgq_hbm" if mi == 0 else "dram:wuq_hbm")

        _bo = NCH * 65 * 2 + (64 - NCH * 65 * 2 % 64) % 64
        lnsq_at = BIG.view(_bo, F32, 128, 1024)
        PT = [BIG.view(_bo + 4096, BF16, 128, 1536),
              BIG.view(_bo + 7168, BF16, 128, 1536)]
        ysb = BIG.view(_bo + 10240, F32, 65, 512)
        ydiv = BIG.view(_bo + 12288, F32, 128, 1)
        ytok = BIG.view(_bo + 12352, F32, 128, 64)
        GRP = 3
        ngrp = (NCH + GRP - 1) // GRP

        def att_S(qg, g):
            ab = g % 2
            sb0 = 0 if ab == 0 else 1536
            chunks = list(range(g * GRP, min((g + 1) * GRP, NCH)))
            for ci, ch in enumerate(chunks):
                par = ch % 2
                m = ch // 2
                kk = kTp.k(m * 128, (m + 1) * 128)
                E("tensor", lambda h, par=par, m=m, sb0=sb0, ci=ci, qg=qg:
                  h.matmul(
                      ps(sb0 + ci * 512, 512),
                      kTp.ap[par * 64:par * 64 + 64, m * 128:(m + 1) * 128],
                      qTd.ap[par * 64:par * 64 + 64,
                             qg * 512:(qg + 1) * 512],
                      start=True, stop=True),
                  reads=[kk, qTd.k()], writes=[psk(sb0 + ci * 512, 512)])

        def att_exp(qg, g):
            ab = g % 2
            sb0 = 0 if ab == 0 else 1536
            n = min((g + 1) * GRP, NCH) - g * GRP
            E("scalar", lambda h, sb0=sb0, ab=ab, n=n: h.activation(
                PT[ab].ap[:, 0:n * 512], ps(sb0, n * 512), ACT.Exp,
                scale=1.0 / 32.0),
              reads=[psk(sb0, n * 512)], writes=[PT[ab].k(0, n * 512)])

        def att_PV(qg, g):
            ab = g % 2
            chunks = list(range(g * GRP, min((g + 1) * GRP, NCH)))
            for ci, ch in enumerate(chunks):
                vk = vpack.k(ch * 65, (ch + 1) * 65)
                E("tensor", lambda h, ch=ch, ci=ci, ab=ab: h.matmul(
                    ps(3072, 512)[0:65, :],
                    vpack.ap[:, ch * 65:(ch + 1) * 65],
                    PT[ab].ap[:, ci * 512:(ci + 1) * 512],
                    start=(ch == 0), stop=(ch == NCH - 1)),
                  reads=[vk, PT[ab].k(ci * 512, (ci + 1) * 512)],
                  writes=[psk(3072, 512)])

        for qg in range(T // 512):
            att_S(qg, 0)
            att_exp(qg, 0)
            for g in range(1, ngrp):
                att_S(qg, g)
                att_exp(qg, g)
                att_PV(qg, g - 1)
            att_PV(qg, ngrp - 1)
            E("vector", lambda h: h.tensor_copy(ysb.ap, ps(3072, 512)[0:65, :]),
              reads=[psk(3072, 512)], writes=[ysb.k()])
            for j in range(4):
                ti = qg * 4 + j
                E("tensor", lambda h, j=j: h.transpose(
                    ps(3584, 128)[:, 0:65], ysb.ap[:, j * 128:(j + 1) * 128],
                    co["identf"].ap[0:65, 0:65]),
                  reads=[ysb.k(j * 128, (j + 1) * 128), K("identf")],
                  writes=[psk(3584, 512)])
                E("vector", lambda h: h.reciprocal(
                    ydiv.ap, ps(3584, 128)[:, 64:65]),
                  reads=[psk(3584, 512)], writes=[ydiv.k()])
                E("vector", lambda h: h.tensor_scalar(
                    ytok.ap, ps(3584, 128)[:, 0:64], ydiv.ap[:, 0:1],
                    None, ALU.mult),
                  reads=[psk(3584, 512), ydiv.k()], writes=[ytok.k()])
                E("vector", lambda h, ti=ti: h.tensor_tensor(
                    tmpr.ap.rearrange("p (r c) -> p r c", c=64),
                    stream_t(ti).rearrange("p (r c) -> p r c", c=64),
                    ytok.ap.rearrange("p (o c) -> p o c", o=1).broadcast_to(
                        [128, 16, 64]),
                    ALU.add),
                  reads=[ytok.k(), skey(ti)], writes=[tmpr.k()])
                layer_norm(tmpr, stream_t(ti), [skey(ti)], lnsq_at)
            fpq = NFC // (T // 512)
            for f in range(qg * fpq, (qg + 1) * fpq):
                quant_gu(f)

        xqT = ARN.view(0, BF16, 128, 8 * T)
        xqm5 = AUX.view(0, F32, 128, 1024)
        xqng = AUX.view(4096, F32, 128, 1024)
        u1tile2 = AUX.view(8192, BF16, 128, 1024)
        xqi = BIG.view(0, I32, 128, 1024)
        xd = BIG.view(4096, F32, 128, 1024)
        xt = BIG.view(8192, F32, 128, 1024)
        for i in range(NT):
            E("vector", lambda h, i=i: h.reduce_max(
                co["ln"].ap[:, 0:1], stream_t(i), AX,
                apply_absolute_value=True),
              reads=[skey(i)], writes=[K("ln", 0, 1)])
            E("vector", lambda h: h.tensor_scalar(
                co["ln"].ap[:, 1:2], co["ln"].ap[:, 0:1], 1e-5, None, ALU.max),
              reads=[K("ln", 0, 1)], writes=[K("ln", 1, 2)])
            E("vector", lambda h: h.reciprocal(co["ln"].ap[:, 2:3],
                                               co["ln"].ap[:, 1:2]),
              reads=[K("ln", 1, 2)], writes=[K("ln", 2, 3)])
            E("vector", lambda h: h.tensor_scalar(
                co["rsct"].ap[:, :], co["ln"].ap[:, 2:3], 127.0, None,
                ALU.mult),
              reads=[K("ln", 2, 3)], writes=[K("rsct")])
            E("vector", lambda h: h.tensor_scalar(
                co["nsct"].ap[:, :], co["ln"].ap[:, 1:2], -1.0 / 127.0, None,
                ALU.mult),
              reads=[K("ln", 1, 2)], writes=[K("nsct")])
            E("scalar", lambda h, i=i: h.activation(
                xqm5.ap, stream_t(i), ACT.Identity,
                scale=co["rsct"].ap[:, 0:1]),
              reads=[skey(i), K("rsct")], writes=[xqm5.k()])
            round_neg(xqm5.ap, [xqm5.k()], xqi, xd, xt, xqng.ap, [xqng.k()])
            E("vector", lambda h: h.tensor_scalar(
                u1tile2.ap, xqng.ap, co["nsct"].ap[:, 0:1], None, ALU.mult),
              reads=[xqng.k(), K("nsct")], writes=[u1tile2.k()])
            for c in range(8):
                sl = slice(c * T + i * 128, c * T + i * 128 + 128)
                transpose_128(
                    u1tile2.ap[:, c * 128:(c + 1) * 128], [u1tile2.k()],
                    xqT.ap[:, sl], [xqT.k(sl.start, sl.stop)],
                    c % 4)

        wgqb = [BIG.view(16384, BF16, 128, 1024), BIG.view(18432, BF16, 128, 1024)]
        wuqb = [BIG.view(20480, BF16, 128, 1024), BIG.view(22528, BF16, 128, 1024)]
        TH_W = min(1024, T)
        NTH = T // TH_W
        gate_t = BIG.view(0, F32, 128, TH_W)
        up_t = BIG.view(4096, F32, 128, TH_W)
        gu_t = [BIG.view(8192, F32, 128, TH_W), BIG.view(12288, F32, 128, TH_W)]
        acc_t = BIG.view(24576, F32, 128, T)
        E("vector", lambda h: h.memset(acc_t.ap, 0.0), writes=[acc_t.k()])

        def p1_load(f):
            b = f % 2
            E("sync", lambda h, f=f, b=b: h.dma_start(
                out=wgqb[b].ap, in_=wgq_hbm[f, :, :]),
              reads=[wgqb[b].k(),
                     dk("wgq_hbm", f * 128 * D * 2, (f + 1) * 128 * D * 2)],
              writes=[wgqb[b].k()], dma=True)
            E("sync", lambda h, f=f, b=b: h.dma_start(
                out=wuqb[b].ap, in_=wuq_hbm[f, :, :]),
              reads=[wuqb[b].k(),
                     dk("wuq_hbm", f * 128 * D * 2, (f + 1) * 128 * D * 2)],
              writes=[wuqb[b].k()], dma=True)

        def p1_mms(g):
            f, th = divmod(g, NTH)
            b = f % 2
            for wq_, lo0 in ((wgqb[b], (g % 2) * 1024),
                             (wuqb[b], 2048 + (g % 2) * 1024)):
                for c in range(8):
                    for tp in range(TH_W // 512):
                        lo = lo0 + tp * 512
                        E("tensor", lambda h, wq_=wq_, c=c, tp=tp, lo=lo,
                          th=th: h.matmul(
                              ps(lo, 512), wq_.ap[:, c * 128:(c + 1) * 128],
                              xqT.ap[:, c * T + th * TH_W + tp * 512:
                                     c * T + th * TH_W + tp * 512 + 512],
                              start=(c == 0), stop=(c == 7)),
                          reads=[wq_.k(c * 128, (c + 1) * 128),
                                 xqT.k(c * T + th * TH_W + tp * 512,
                                       c * T + th * TH_W + tp * 512 + 512)],
                          writes=[psk(lo, 512)])

        def p1_evac(g):
            f, th = divmod(g, NTH)
            pg, pu = (g % 2) * 1024, 2048 + (g % 2) * 1024
            gb = g % 2
            E("scalar", lambda h, pg=pg: h.activation(
                gate_t.ap, ps(pg, TH_W), ACT.Silu,
                scale=co["negmg"].ap[:, 0:1]),
              reads=[psk(pg, TH_W), K("negmg")], writes=[gate_t.k()])
            E("scalar", lambda h, pu=pu: h.activation(
                up_t.ap, ps(pu, TH_W), ACT.Identity,
                scale=co["negmu"].ap[:, 0:1]),
              reads=[psk(pu, TH_W), K("negmu")], writes=[up_t.k()])
            E("vector", lambda h, gb=gb: h.tensor_mul(
                gu_t[gb].ap, gate_t.ap, up_t.ap),
              reads=[gate_t.k(), up_t.k()], writes=[gu_t[gb].k()])
            E("scalar", lambda h, gb=gb: h.activation(
                up_t.ap, gu_t[gb].ap, ACT.Abs),
              reads=[gu_t[gb].k()], writes=[up_t.k()])
            E("vector", lambda h, th=th: h.tensor_max(
                acc_t.ap[:, th * TH_W:(th + 1) * TH_W],
                acc_t.ap[:, th * TH_W:(th + 1) * TH_W], up_t.ap),
              reads=[up_t.k(), acc_t.k(th * TH_W, (th + 1) * TH_W)],
              writes=[acc_t.k(th * TH_W, (th + 1) * TH_W)])
            E("scalar", lambda h, f=f, th=th, gb=gb: h.dma_start(
                out=gu_hbm[f, :, th * TH_W:(th + 1) * TH_W], in_=gu_t[gb].ap),
              reads=[gu_t[gb].k()],
              writes=[dk("gu_hbm", f * 128 * T + th * TH_W,
                         f * 128 * T + (th + 1) * TH_W)], dma=True)

        wdstg = [AUX.view(0, F32, 128, 1024), AUX.view(4096, F32, 128, 1024)]
        wqm = BIG.view(49152, F32, 128, 1024)
        wqqi = BIG.view(53248, I32, 128, 1024)
        wqd = BIG.view(57344, F32, 128, 1024)
        wqt = BIG.view(61440, F32, 128, 1024)
        wdqo = [BIG.view(53248, BF16, 128, 1024), BIG.view(55296, BF16, 128, 1024)]

        def wd_load(f):
            b = f % 2
            E("sync", lambda h, f=f, b=b: h.dma_start(
                out=wdstg[b].ap, in_=wdT[f * 128:(f + 1) * 128, :]),
              reads=[wdstg[b].k()], writes=[wdstg[b].k()], dma=True)

        def quant_wd(f):
            b = f % 2
            E("vector", lambda h, b=b: h.tensor_scalar(
                wqm.ap, wdstg[b].ap, co["rmd"].ap[:, 0:1], None, ALU.mult),
              reads=[wdstg[b].k(), K("rmd")], writes=[wqm.k()])
            E("vector", lambda h: h.tensor_copy(wqqi.ap, wqm.ap),
              reads=[wqm.k()], writes=[wqqi.k()])
            E("vector", lambda h: h.tensor_tensor(
                wqd.ap, wqm.ap, wqqi.ap, ALU.subtract),
              reads=[wqm.k(), wqqi.k()], writes=[wqd.k()])
            E("vector", lambda h: h.scalar_tensor_tensor(
                wqt.ap, wqd.ap, 0.5, wqqi.ap, ALU.is_gt, ALU.add),
              reads=[wqd.k(), wqqi.k()], writes=[wqt.k()])
            E("vector", lambda h: h.scalar_tensor_tensor(
                wqm.ap, wqd.ap, -0.5, wqt.ap, ALU.is_lt, ALU.subtract),
              reads=[wqd.k(), wqt.k()], writes=[wqm.k()])
            E("vector", lambda h, b=b: h.tensor_scalar(
                wdqo[b].ap, wqm.ap, -1.0, 1.0, ALU.max, ALU.min),
              reads=[wqm.k()], writes=[wdqo[b].k()])
            E("gpsimd", lambda h, f=f, b=b: h.dma_start(
                out=wdq_hbm[f, :, :], in_=wdqo[b].ap),
              reads=[wdqo[b].k()],
              writes=[dk("wdq_hbm", f * 128 * D * 2, (f + 1) * 128 * D * 2)],
              dma=True, fam="dram:wdq_hbm")

        QTH = min(2, NTH - 1)
        p1_load(0)
        if NFC > 1:
            p1_load(1)
        wd_load(0)
        NG = NFC * NTH
        for g in range(NG):
            f, th = divmod(g, NTH)
            p1_mms(g)
            if g > 0:
                p1_evac(g - 1)
            if th == 0 and f + 1 < NFC:
                wd_load(f + 1)
            if th == QTH:
                quant_wd(f)
            if th == NTH - 1 and f + 2 < NFC:
                p1_load(f + 2)
        p1_evac(NG - 1)
        for b in range(NT):
            E("tensor", lambda h, b=b: h.transpose(
                ps(3584, 128), acc_t.ap[:, b * 128:(b + 1) * 128],
                co["identf"].ap[:, :]),
              reads=[acc_t.k(b * 128, (b + 1) * 128), K("identf")],
              writes=[psk(3584, 512)])
            E("vector", lambda h, b=b: h.reduce_max(
                co["mxtok"].ap[:, b:b + 1], ps(3584, 128), AX),
              reads=[psk(3584, 512)], writes=[K("mxtok", b, b + 1)])
        E("vector", lambda h: h.tensor_scalar(
            co["mxtok"].ap[:, :], co["mxtok"].ap[:, :], 1e-5, None, ALU.max),
          reads=[K("mxtok")], writes=[K("mxtok")])
        E("vector", lambda h: h.tensor_scalar(
            co["sctok"].ap[:, :], co["mxtok"].ap[:, :], 1.0 / 127.0, None,
            ALU.mult),
          reads=[K("mxtok")], writes=[K("sctok")])
        E("vector", lambda h: h.reciprocal(co["rs127"].ap[:, :],
                                           co["mxtok"].ap[:, :]),
          reads=[K("mxtok")], writes=[K("rs127")])
        E("vector", lambda h: h.tensor_scalar(
            co["rs127"].ap[:, :], co["rs127"].ap[:, :], 127.0, None, ALU.mult),
          reads=[K("rs127")], writes=[K("rs127")])
        row1T = BIG.view(45056, F32, 1, T)
        for b in range(NT):
            E("tensor", lambda h, b=b: h.transpose(
                ps(3584, 128)[0:1, :], co["rs127"].ap[:, b:b + 1],
                co["identf"].ap[:, :]),
              reads=[K("rs127", b, b + 1), K("identf")],
              writes=[psk(3584, 512)])
            E("vector", lambda h, b=b: h.tensor_copy(
                row1T.ap[0:1, b * 128:(b + 1) * 128], ps(3584, 128)[0:1, :]),
              reads=[psk(3584, 512)],
              writes=[row1T.k(b * 128, (b + 1) * 128)])
        sc2b = BIG.view(0, F32, 128, T)
        for j in range(T // 512):
            bcast_row(row1T.ap[0:1, j * 512:(j + 1) * 512],
                      [row1T.k(j * 512, (j + 1) * 512)],
                      sc2b.ap[:, j * 512:(j + 1) * 512],
                      [sc2b.k(j * 512, (j + 1) * 512)], 512, bank=6)

        wdq = ARN.view(0, BF16, 128, 32 * D)
        for ch in range(4):
            f0, f1 = ch * (NFC // 4), (ch + 1) * (NFC // 4)
            E("sync", lambda h, f0=f0, f1=f1: h.dma_start(
                out=wdq.ap[:, f0 * D:f1 * D].rearrange(
                    "p (f w) -> p f w", w=D),
                in_=wdq_hbm[f0:f1, :, :].transpose([1, 0, 2])),
              reads=[dk("wdq_hbm", f0 * 128 * D * 2, f1 * 128 * D * 2)],
              writes=[wdq.k(f0 * D, f1 * D)], dma=True, fam="wdql")

        guldg = [AUX.view(8192, F32, 128, 512), AUX.view(10240, F32, 128, 512)]
        lnsq_p2 = BIG.view(16384, F32, 128, 1024)
        tmpr2 = BIG.view(20480, F32, 128, 1024)
        guldg = [AUX.view(0, F32, 128, 1024), AUX.view(4096, F32, 128, 1024)]
        gum = [BIG.view(24576, F32, 128, 1024), BIG.view(28672, F32, 128, 1024)]
        gqi = [BIG.view(32768, I32, 128, 1024), BIG.view(36864, I32, 128, 1024)]
        gd = [BIG.view(40960, F32, 128, 1024), BIG.view(45056, F32, 128, 1024)]
        gt = [BIG.view(49152, F32, 128, 1024), BIG.view(53248, F32, 128, 1024)]
        guqc = [BIG.view(57344, BF16, 128, 1024), BIG.view(59392, BF16, 128, 1024)]
        ostg = BIG.view(61440, F32, 128, 1024)
        NF2 = NFC // 2
        for tt in range(T // 512):

            def g_load(i, tt=tt):
                b = i % 2
                E("sync", lambda h, i=i, tt=tt, b=b: h.dma_start(
                    out=guldg[b].ap.rearrange("p (g w) -> p g w", w=512),
                    in_=gu_hbm[2 * i:2 * i + 2, :,
                               tt * 512:(tt + 1) * 512].transpose([1, 0, 2])),
                  reads=[guldg[b].k(),
                         dk("gu_hbm", 2 * i * 128 * T, (2 * i + 2) * 128 * T)],
                  writes=[guldg[b].k()], dma=True)
                E("vector", lambda h, b=b, tt=tt: h.tensor_tensor(
                    gum[b].ap.rearrange("p (g w) -> p g w", w=512),
                    guldg[b].ap.rearrange("p (g w) -> p g w", w=512),
                    sc2b.ap[:, tt * 512:(tt + 1) * 512].rearrange(
                        "p (o w) -> p o w", o=1).broadcast_to([128, 2, 512]),
                    ALU.mult),
                  reads=[guldg[b].k(), sc2b.k(tt * 512, (tt + 1) * 512)],
                  writes=[gum[b].k()])
                E("scalar", lambda h, b=b: h.copy(gqi[b].ap, gum[b].ap),
                  reads=[gum[b].k()], writes=[gqi[b].k()])

            def g_work(i, tt=tt):
                b = i % 2
                E("vector", lambda h, b=b: h.tensor_tensor(
                    gd[b].ap, gum[b].ap, gqi[b].ap, ALU.subtract),
                  reads=[gum[b].k(), gqi[b].k()], writes=[gd[b].k()])
                E("vector", lambda h, b=b: h.scalar_tensor_tensor(
                    gt[b].ap, gd[b].ap, 0.5, gqi[b].ap, ALU.is_gt, ALU.add),
                  reads=[gd[b].k(), gqi[b].k()], writes=[gt[b].k()])
                E("vector", lambda h, b=b: h.scalar_tensor_tensor(
                    guqc[b].ap, gd[b].ap, -0.5, gt[b].ap,
                    ALU.is_lt, ALU.subtract),
                  reads=[gd[b].k(), gt[b].k()], writes=[guqc[b].k()])
                for ff in range(2):
                    f = 2 * i + ff
                    for sub in range(4):
                        for nb in range(2):
                            lo = sub * 1024 + nb * 512
                            E("tensor", lambda h, f=f, ff=ff, sub=sub, nb=nb,
                              b=b, lo=lo: h.matmul(
                                  ps(lo, 512),
                                  guqc[b].ap[:, ff * 512 + sub * 128:
                                             ff * 512 + sub * 128 + 128],
                                  wdq.ap[:, f * D + nb * 512:
                                         f * D + nb * 512 + 512],
                                  start=(f == 0), stop=(f == NFC - 1)),
                              reads=[guqc[b].k(ff * 512 + sub * 128,
                                               ff * 512 + sub * 128 + 128),
                                     wdq.k(f * D + nb * 512,
                                           f * D + nb * 512 + 512)],
                              writes=[psk(lo, 512)])

            g_load(0)
            for i in range(1, NF2):
                g_load(i)
                g_work(i - 1)
            g_work(NF2 - 1)
            for sub in range(4):
                ti = tt * 4 + sub
                E("vector", lambda h, ti=ti: h.tensor_tensor(
                    co["ct"].ap[:, :], co["sctok"].ap[:, ti:ti + 1],
                    co["md"].ap[:, 0:1], ALU.mult),
                  reads=[K("sctok", ti, ti + 1), K("md")], writes=[K("ct")])
                E("vector", lambda h, sub=sub, ti=ti: h.scalar_tensor_tensor(
                    tmpr2.ap, ps(sub * 1024, 1024), co["ct"].ap[:, 0:1],
                    stream_t(ti), ALU.mult, ALU.add),
                  reads=[psk(sub * 1024, 1024), K("ct"), skey(ti)],
                  writes=[tmpr2.k()])
                layer_norm(tmpr2, ostg.ap, [ostg.k()], lnsq_p2)
                E("sync", lambda h, ti=ti: h.dma_start(
                    out=out_h[ti * 128:(ti + 1) * 128, :], in_=ostg.ap),
                  reads=[ostg.k()],
                  writes=[dk("out", ti * 128 * D, (ti + 1) * 128 * D)],
                  dma=True)

        plan.barrier_on("gpsimd")

        with nc.Block() as block:
            plan.replay(block)

    return nc, plan



def _host_prep(inputs):
    f32 = np.float32
    import ml_dtypes
    bf = ml_dtypes.bfloat16
    g = {}
    g["wqkvT"] = np.ascontiguousarray(np.asarray(inputs["L0_attn_Wqkv"], f32).T)
    g["bqkvp"] = np.ascontiguousarray(
        np.asarray(inputs["L0_attn_bqkv"], f32).reshape(24, 128).T)
    g["wprojT"] = np.ascontiguousarray(np.asarray(inputs["L0_attn_Wproj"], f32).T)
    g["bprojr"] = np.ascontiguousarray(
        np.asarray(inputs["L0_attn_bproj"], f32).reshape(1, D))
    g["w1n"] = np.ascontiguousarray(np.asarray(inputs["L0_ffn_w1"], f32))
    g["b1p"] = np.ascontiguousarray(
        np.asarray(inputs["L0_ffn_b1"], f32).reshape(32, 128).T)
    g["w2n"] = np.ascontiguousarray(np.asarray(inputs["L0_ffn_w2"], f32))
    g["b2p"] = np.ascontiguousarray(
        np.asarray(inputs["L0_ffn_b2"], f32).reshape(8, 128).T)
    g["wqT1"] = np.ascontiguousarray(np.asarray(inputs["L1_attn_Wq"], f32).T)
    g["wkT1"] = np.ascontiguousarray(np.asarray(inputs["L1_attn_Wk"], f32).T)
    g["wvT1"] = np.ascontiguousarray(np.asarray(inputs["L1_attn_Wv"], f32).T)
    g["bqkv1"] = np.ascontiguousarray(np.stack(
        [np.asarray(inputs["L1_attn_bq"], f32),
         np.asarray(inputs["L1_attn_bk"], f32),
         np.asarray(inputs["L1_attn_bv"], f32)], axis=1))
    g["wgblk"] = np.ascontiguousarray(
        np.asarray(inputs["L1_ffn_w_gate"], f32).T.reshape(
            D, 32, 128).transpose(1, 0, 2))
    g["wublk"] = np.ascontiguousarray(
        np.asarray(inputs["L1_ffn_w_up"], f32).T.reshape(
            D, 32, 128).transpose(1, 0, 2))
    g["wdT"] = np.ascontiguousarray(np.asarray(inputs["L1_ffn_w_down"], f32).T)
    g["sc0"] = np.asarray(inputs["L0_attn_scale"], dtype=f32).reshape(1, 1)
    g["identf"] = np.eye(128, dtype=f32)
    g["identb"] = np.eye(128, dtype=bf)
    m16 = np.zeros((128, 128), dtype=bf)
    for c in range(8):
        m16[0:64, c * 16 + 2 * c] = 1
        m16[64:128, c * 16 + 2 * c + 1] = 1
    g["mask16"] = m16
    sel = np.zeros((16, 1024), dtype=bf)
    for c in range(8):
        sel[2 * c, c * 128: c * 128 + 64] = 1
        sel[2 * c + 1, c * 128 + 64: c * 128 + 128] = 1
    g["selb"] = sel
    g["onesb"] = np.ones((1, 128), dtype=bf)
    g["onesf"] = np.ones((1, 128), dtype=f32)
    return g


def kernel(**inputs):
    x = np.asarray(inputs["L0_attn_x"], dtype=np.float32)
    B, T, _ = x.shape
    assert B == NCORE
    g = _host_prep(inputs)
    nc, _ = build_program(T=T, ncores=NCORE)
    in_maps = []
    for c in range(NCORE):
        m = dict(g)
        m["x"] = np.ascontiguousarray(x[c])
        in_maps.append(m)
    res = run_bass_kernel_spmd(nc, in_maps, core_ids=list(range(NCORE)))
    out = np.stack([np.asarray(r["out"]) for r in res.results], axis=0)
    return out.astype(np.float32)


# version 25 (final)
# speedup vs baseline: 1.0153x; 1.0153x over previous
import contextlib
import numpy as np

import concourse.bass as bass
import concourse.mybir as mybir
from concourse.bass_utils import run_bass_kernel_spmd

F32 = mybir.dt.float32
I32 = mybir.dt.int32
BF16 = mybir.dt.bfloat16
U8 = mybir.dt.uint8
ALU = mybir.AluOpType
ACT = mybir.ActivationFunctionType
AX = mybir.AxisListType.X

D = 1024
E3 = 3072
FF = 4096
NCORE = 8
HD = 64



class Plan:

    ENGS = ("tensor", "vector", "scalar", "gpsimd", "sync")

    def __init__(self, nc):
        self.nc = nc
        self.recs = {e: [] for e in self.ENGS}
        self.streams = list(self.ENGS) + ["cc"]
        self.cnt = {s: 0 for s in self.streams}
        self.sems = {}
        self.spaces = {}
        self.wm = {e: {} for e in self.ENGS}
        self.n_inst = 0
        self._pool = []

    def make_sems(self, st):
        for s in self.streams:
            self.sems[s] = st.enter_context(self.nc.semaphore("sem_" + s))
        for i in range(70):
            self._pool.append(st.enter_context(self.nc.semaphore(f"dsem{i}")))

    def _fam_sem(self, fam):
        key = "d_" + fam
        if key not in self.sems:
            assert self._pool, "semaphore pool exhausted"
            self.sems[key] = self._pool.pop()
            self.cnt[key] = 0
        return key

    def _wait(self, eng, src, val):
        if src == eng == "tensor":
            return
        if src.startswith("d_") or src == "cc":
            val = self.cnt[src]
        if val <= self.wm[eng].get(src, 0):
            return
        self.wm[eng][src] = val
        self.recs[eng].append(("wait", src, val))

    def _deps_for(self, eng, key, write):
        sp = self.spaces.get(key[0])
        if sp is None:
            return
        lo, hi = key[1], key[2]
        for ent in sp:
            if ent[0] < hi and lo < ent[1]:
                if ent[2] is not None:
                    self._wait(eng, *ent[2])
                if write:
                    for s, c in ent[3].items():
                        self._wait(eng, s, c)

    def _record(self, key, write, tgt, cval):
        sp = self.spaces.setdefault(key[0], [])
        lo, hi = key[1], key[2]
        if write:
            new = []
            for ent in sp:
                if ent[0] < hi and lo < ent[1]:
                    if ent[0] < lo:
                        new.append([ent[0], lo, ent[2], dict(ent[3])])
                    if hi < ent[1]:
                        new.append([hi, ent[1], ent[2], dict(ent[3])])
                else:
                    new.append(ent)
            new.append([lo, hi, (tgt, cval), {}])
            self.spaces[key[0]] = new
        else:
            covered = False
            for ent in sp:
                if ent[0] < hi and lo < ent[1]:
                    ent[3][tgt] = max(ent[3].get(tgt, 0), cval)
                    covered = True
            if not covered:
                sp.append([lo, hi, None, {tgt: cval}])

    def emit(self, eng, fn, reads=(), writes=(), dma=False, cc=False, fam=None):
        if cc:
            tgt, step = "cc", 1
        elif dma:
            if fam is None:
                w0 = writes[0]
                fam = w0[0] if w0[0].startswith("dram:") else f"{w0[0]}:{w0[1]}"
            tgt, step = self._fam_sem(fam), 16
        else:
            tgt, step = eng, 1
        for r in reads:
            self._deps_for(eng, r, False)
        for w in writes:
            self._deps_for(eng, w, True)
        self.cnt[tgt] += step
        cval = self.cnt[tgt]
        self.n_inst += 1
        self.recs[eng].append(("inst", fn, tgt, cval, step))
        for r in reads:
            self._record(r, False, tgt, cval)
        for w in writes:
            self._record(w, True, tgt, cval)

    def raw(self, eng, fn):
        self.recs[eng].append(("raw", fn))

    def barrier_on(self, eng):
        for s in self.streams:
            if self.cnt[s] > 0:
                self._wait(eng, s, self.cnt[s])

    def replay(self, block):
        plan = self
        waited = {}
        for recs in self.recs.values():
            for r in recs:
                if r[0] == "wait":
                    waited.setdefault(r[1], set()).add(r[2])
        marked = {s: sorted(v) for s, v in waited.items()}

        import bisect

        def new_val(src, val):
            if src.startswith("d_") or src == "cc":
                return val
            return bisect.bisect_right(marked[src], val)

        def mk(name):
            def body(h):
                for r in plan.recs[name]:
                    if r[0] == "wait":
                        _, src, val = r
                        h.wait_ge(plan.sems[src], new_val(src, val))
                    elif r[0] == "inst":
                        _, fn, tgt, cval, step = r
                        inst = fn(h)
                        if tgt.startswith("d_") or tgt == "cc" or (
                                tgt in waited and cval in waited[tgt]):
                            inst.then_inc(plan.sems[tgt], step)
                    else:
                        r[1](h)
            return body

        block.tensor(mk("tensor"))
        block.vector(mk("vector"))
        block.scalar(mk("scalar"))
        block.gpsimd(mk("gpsimd"))
        block.sync(mk("sync"))


class Reg:

    __slots__ = ("ap", "space", "off", "isz", "n")

    def __init__(self, ap, space, off_bytes, itemsize, n_free):
        self.ap = ap
        self.space = space
        self.off = off_bytes
        self.isz = itemsize
        self.n = n_free

    def k(self, lo=0, hi=None):
        if hi is None:
            hi = self.n
        return (self.space, self.off + lo * self.isz, self.off + hi * self.isz)


class Arena:
    def __init__(self, handle, space, nbytes):
        self.h = handle
        self.space = space
        self.nbytes = nbytes

    def view(self, off, dtype, parts, free):
        isz = {F32: 4, BF16: 2, U8: 1, I32: 4}[dtype]
        assert off % 4 == 0, off
        assert off + free * isz <= self.nbytes, (self.space, off, free, isz)
        ap = self.h.ap()[0:parts, off:off + free * isz]
        if dtype != U8:
            ap = ap.bitcast(dtype)
        return Reg(ap, self.space, off, isz, free)


def dk(name, lo=0, hi=1 << 60):
    return ("dram:" + name, lo, hi)



def build_program(T=4096, ncores=NCORE):
    nc = bass.Bass("TRN2", target_bir_lowering=False)
    NT = T // 128
    NB = T // 512
    TA = T * ncores
    NCH = TA // 128
    NFC = FF // 128

    def din(name, shape, dtype=F32):
        return nc.dram_tensor(name, list(shape), dtype, kind="ExternalInput").ap()

    x_in = din("x", [T, D])
    wqkvT = din("wqkvT", [D, E3])
    bqkvp_d = din("bqkvp", [128, 24])
    wprojT = din("wprojT", [D, D])
    bprojr = din("bprojr", [1, D])
    w1n = din("w1n", [FF, D])
    b1p_d = din("b1p", [128, 32])
    w2n = din("w2n", [D, FF])
    b2p_d = din("b2p", [128, 8])
    wqT1 = din("wqT1", [D, HD])
    wkT1 = din("wkT1", [D, HD])
    wvT1 = din("wvT1", [D, HD])
    bqkv1_d = din("bqkv1", [HD, 3])
    wgblk = din("wgblk", [NFC, D, 128])
    wublk = din("wublk", [NFC, D, 128])
    wdT = din("wdT", [FF, D])
    sc0 = din("sc0", [1, 1])
    identf_d = din("identf", [128, 128])
    identb_d = din("identb", [128, 128], BF16)
    mask16_d = din("mask16", [128, 128], BF16)
    selb_d = din("selb", [16, 1024], BF16)
    onesb_d = din("onesb", [1, 128], BF16)
    onesf_d = din("onesf", [1, 128])

    out_h = nc.dram_tensor("out", [T, D], F32, kind="ExternalOutput").ap()
    h_hbm = nc.dram_tensor("h_hbm", [NFC, 128, T], F32).ap()
    gu_hbm = nc.dram_tensor("gu_hbm", [NFC, 128, T], F32).ap()
    wgq_hbm = nc.dram_tensor("wgq_hbm", [NFC, 128, D], BF16).ap()
    wuq_hbm = nc.dram_tensor("wuq_hbm", [NFC, 128, D], BF16).ap()
    wdq_hbm = nc.dram_tensor("wdq_hbm", [NFC, 128, D], BF16).ap()
    mxio = nc.dram_tensor("mxio", [2, 1], F32).ap()
    mxout = nc.dram_tensor("mxout", [2, 1], F32).ap()
    bq2s = nc.dram_tensor("bq2s", [128, 8], F32).ap()
    kg_in = nc.dram_tensor("kg_in", [HD, T], BF16).ap()
    kg_out = nc.dram_tensor("kg_out", [HD * ncores, T], BF16).ap()
    vg_in = nc.dram_tensor("vg_in", [T, 65], BF16).ap()
    vg_out = nc.dram_tensor("vg_out", [TA, 65], BF16).ap()

    st = contextlib.ExitStack()
    with st:
        stream_h = st.enter_context(nc.sbuf_tensor("stream", [128, NT * D], BF16))
        bigw_h = st.enter_context(nc.sbuf_tensor("bigw", [128, 65536], U8))
        arn_h = st.enter_context(nc.sbuf_tensor("arn", [128, 65536], U8))
        aux_h = st.enter_context(nc.sbuf_tensor("aux", [128, 12288], U8))
        cst_h = st.enter_context(nc.sbuf_tensor("cst", [128, 3904], U8))
        PS_h = st.enter_context(nc.psum_tensor("PS", [128, 4096], F32))

        ARN = Arena(arn_h, "arn", 65536)
        AUX = Arena(aux_h, "aux", 12288)
        CST = Arena(cst_h, "cst", 3904)
        BIG = Arena(bigw_h, "big", 65536)

        plan = Plan(nc)
        plan.make_sems(st)
        E = plan.emit

        def stream_t(i):
            return stream_h.ap()[:, i * D:(i + 1) * D]

        def skey(i, j=None):
            if j is None:
                j = i + 1
            return ("stream", i * D * 2, j * D * 2)

        def ps(lo, n):
            return PS_h.ap()[:, lo:lo + n]

        def psk(lo, n):
            return ("ps", lo * 4, (lo + n) * 4)

        co = {}
        coff = [0]

        def calloc(name, dtype, free, parts=128, align=4):
            isz = {F32: 4, BF16: 2, I32: 4}[dtype]
            coff[0] = ((coff[0] + align - 1) // align) * align
            co[name] = CST.view(coff[0], dtype, parts, free)
            coff[0] += free * isz

        calloc("identf", F32, 128, align=64)
        calloc("identb", BF16, 128, align=64)
        calloc("onesb", BF16, 128, parts=1, align=64)
        calloc("bqkvp", F32, 24, align=64)
        calloc("sbq", F32, 24)
        calloc("b1p", F32, 32, align=64)
        calloc("b2p", F32, 8, align=64)
        calloc("bq1", F32, 32)
        calloc("bq2p", F32, 8)
        calloc("bqkv1", F32, 3, parts=HD, align=64)
        calloc("sc128", F32, 1)
        calloc("gm", F32, 4)
        calloc("s1", F32, 1)
        calloc("rs1", F32, 1)
        calloc("s2", F32, 1)
        calloc("negs2", F32, 1)
        calloc("rs2", F32, 1)
        calloc("mg", F32, 1)
        calloc("rmg", F32, 1)
        calloc("negmg", F32, 1)
        calloc("mu", F32, 1)
        calloc("rmu", F32, 1)
        calloc("negmu", F32, 1)
        calloc("md", F32, 1)
        calloc("rmd", F32, 1)
        calloc("negmd", F32, 1)
        calloc("macc", F32, 4)
        calloc("carryk", F32, 8)
        calloc("carrykv", F32, 8)
        calloc("swt", F32, 4)
        calloc("ln", F32, 32)
        calloc("rsct", F32, 1)
        calloc("nsct", F32, 1)
        calloc("sctok", F32, NT)
        calloc("ct", F32, 1)
        calloc("half", F32, 1)
        calloc("rqa", I32, 1)
        calloc("onesf", F32, 128, parts=1, align=64)
        calloc("row", F32, 4, parts=4)
        calloc("mxtok", F32, NT)
        calloc("rs127", F32, NT)
        print("CST used:", coff[0]); assert coff[0] <= 3904, coff[0]

        co["bprojb"] = ARN.view(40960, BF16, 1, D)
        co["bq2r"] = ARN.view(0, BF16, 1, D)
        co["mask16"] = BIG.view(56320, BF16, 128, 128)
        co["selb"] = BIG.view(56576, BF16, 16, 1024)

        def K(name, lo=0, hi=None):
            return co[name].k(lo, hi)

        def bcast_row(src_ap, src_keys, dst_ap, dst_keys, n, bank=7):
            pslo = bank * 512
            E("tensor", lambda h: h.matmul(
                ps(pslo, n), co["onesf"].ap[0:1, :], src_ap,
                start=True, stop=True),
              reads=list(src_keys) + [K("onesf")], writes=[psk(pslo, 512)])
            E("vector", lambda h: h.tensor_copy(dst_ap, ps(pslo, n)),
              reads=[psk(pslo, 512)], writes=list(dst_keys))

        def preduce(src_ap, src_keys, dst_ap, dst_keys, n_part, op, bank=7):
            pslo = bank * 512
            E("tensor", lambda h: h.transpose(
                ps(pslo, 128)[0:1, 0:n_part], src_ap,
                co["identf"].ap[0:n_part, 0:n_part]),
              reads=list(src_keys) + [K("identf")], writes=[psk(pslo, 512)])
            E("vector", lambda h: h.tensor_reduce(
                dst_ap, ps(pslo, 128)[0:1, 0:n_part], AX, op),
              reads=[psk(pslo, 512)], writes=list(dst_keys))

        def transpose_128(src_ap, src_keys, dst_ap, dst_keys, bank,
                          ident="identb", evac="scalar", in_parts=128):
            pslo = bank * 512
            if ident == "identb":
                pv = ps(pslo, 64).bitcast(BF16)
            else:
                pv = ps(pslo, 128)
            pk_ = psk(pslo, 512)
            idap = co[ident].ap[0:in_parts, 0:in_parts]
            E("tensor", lambda h: h.transpose(pv[:, 0:in_parts], src_ap, idap),
              reads=list(src_keys) + [K(ident)], writes=[pk_])
            if evac == "vector":
                E("vector", lambda h: h.tensor_copy(dst_ap, pv[:, 0:in_parts]),
                  reads=[pk_], writes=list(dst_keys))
            else:
                E("scalar", lambda h: h.copy(dst_ap, pv[:, 0:in_parts]),
                  reads=[pk_], writes=list(dst_keys))


        def round_neg(m_ap, m_keys, qi, d, t, out_ap, out_keys,
                      eng="vector"):
            E("scalar", lambda h: h.copy(qi.ap, m_ap),
              reads=list(m_keys), writes=[qi.k()])
            E("vector", lambda h: h.tensor_tensor(d.ap, m_ap, qi.ap,
                                                  ALU.subtract),
              reads=list(m_keys) + [qi.k()], writes=[d.k()])
            E("vector", lambda h: h.scalar_tensor_tensor(
                t.ap, d.ap, 0.5, qi.ap, ALU.is_gt, ALU.add),
              reads=[d.k(), qi.k()], writes=[t.k()])
            E("vector", lambda h: h.scalar_tensor_tensor(
                out_ap, d.ap, -0.5, t.ap, ALU.is_lt, ALU.subtract),
              reads=[d.k(), t.k()], writes=list(out_keys))

        for nm, src in (("identf", identf_d), ("identb", identb_d),
                        ("mask16", mask16_d), ("selb", selb_d),
                        ("onesb", onesb_d), ("onesf", onesf_d), ("bqkvp", bqkvp_d),
                        ("b1p", b1p_d), ("b2p", b2p_d), ("bqkv1", bqkv1_d)):
            E("sync", lambda h, nm=nm, src=src: h.dma_start(
                out=co[nm].ap[:, :], in_=src[:, :]),
              writes=[K(nm)], dma=True)
        bpstg = AUX.view(0, F32, 1, D)
        E("sync", lambda h: h.dma_start(out=bpstg.ap, in_=bprojr[:, :]),
          writes=[bpstg.k()], dma=True)
        E("vector", lambda h: h.tensor_copy(co["bprojb"].ap[:, :], bpstg.ap),
          reads=[bpstg.k()], writes=[K("bprojb")])
        scstg = AUX.view(4096, F32, 1, 1)
        E("sync", lambda h: h.dma_start(out=scstg.ap, in_=sc0[:, :]),
          writes=[scstg.k()], dma=True)
        E("vector", lambda h: h.memset(co["half"].ap[:, :], 0.5),
          writes=[K("half")])
        bcast_row(scstg.ap, [scstg.k()], co["sc128"].ap[:, :], [K("sc128")], 1)

        _lnrot = [0]

        def layer_norm(tmp, outap, outkeys, lnsq):
            base = (_lnrot[0] % 4) * 6
            _lnrot[0] += 1
            ln = co["ln"].ap
            lnk = lambda i: K("ln", base + i, base + i + 1)
            c = lambda i: ln[:, base + i:base + i + 1]
            E("vector", lambda h: h.reduce_sum(c(0), tmp.ap, AX),
              reads=[tmp.k()], writes=[lnk(0)])
            E("scalar", lambda h: h.activation(lnsq.ap, tmp.ap, ACT.Square,
                                               accum_out=c(1)),
              reads=[tmp.k()], writes=[lnsq.k(), lnk(1)])
            E("vector", lambda h: h.tensor_scalar(c(2), c(0),
                                                  1.0 / D, None, ALU.mult),
              reads=[lnk(0)], writes=[lnk(2)])
            E("vector", lambda h: h.tensor_tensor(c(3), c(2), c(2), ALU.mult),
              reads=[lnk(2)], writes=[lnk(3)])
            E("vector", lambda h: h.scalar_tensor_tensor(
                c(4), c(1), 1.0 / D, c(3), ALU.mult, ALU.subtract),
              reads=[lnk(1), lnk(3)], writes=[lnk(4)])
            E("vector", lambda h: h.tensor_scalar(c(4), c(4), 1e-5, None,
                                                  ALU.add),
              reads=[lnk(4)], writes=[lnk(4)])
            E("vector", lambda h: h.reciprocal(c(5), c(4)),
              reads=[lnk(4)], writes=[lnk(5)])
            E("scalar", lambda h: h.activation(c(5), c(5), ACT.Sqrt),
              reads=[lnk(5)], writes=[lnk(5)])
            E("vector", lambda h: h.tensor_scalar(
                outap, tmp.ap, c(2), c(5), ALU.subtract, ALU.mult),
              reads=[tmp.k(), lnk(2), lnk(5)], writes=list(outkeys))

        xstg = [AUX.view(0, F32, 128, 1024), AUX.view(4096, F32, 128, 1024)]
        for i in range(NT):
            b = i % 2
            E("sync", lambda h, i=i, b=b: h.dma_start(
                out=xstg[b].ap, in_=x_in[i * 128:(i + 1) * 128, :]),
              reads=[xstg[b].k()], writes=[xstg[b].k()], dma=True)
            E("vector", lambda h, i=i, b=b: h.tensor_copy(
                stream_t(i), xstg[b].ap),
              reads=[xstg[b].k()], writes=[skey(i)])

        wq_bf = BIG.view(0, BF16, 128, 8 * E3)
        wp_bf = ARN.view(49152, BF16, 128, 8 * D)
        wstg = [AUX.view(0, F32, 128, E3)]
        for c in range(8):
            b = 0
            E("sync", lambda h, c=c, b=b: h.dma_start(
                out=wstg[b].ap, in_=wqkvT[c * 128:(c + 1) * 128, :]),
              reads=[wstg[b].k()], writes=[wstg[b].k()], dma=True)
            E("vector", lambda h, c=c, b=b: h.tensor_copy(
                wq_bf.ap[:, c * E3:(c + 1) * E3], wstg[b].ap),
              reads=[wstg[b].k()], writes=[wq_bf.k(c * E3, (c + 1) * E3)])
        wpstg = AUX.view(0, F32, 128, D)
        for c in range(8):
            E("sync", lambda h, c=c: h.dma_start(
                out=wpstg.ap, in_=wprojT[c * 128:(c + 1) * 128, :]),
              reads=[wpstg.k()], writes=[wpstg.k()], dma=True)
            E("vector", lambda h, c=c: h.tensor_copy(
                wp_bf.ap[:, c * D:(c + 1) * D], wpstg.ap),
              reads=[wpstg.k()], writes=[wp_bf.k(c * D, (c + 1) * D)])
        E("vector", lambda h: h.tensor_scalar(
            co["sbq"].ap[:, :], co["bqkvp"].ap[:, :], co["sc128"].ap[:, 0:1],
            None, ALU.mult),
          reads=[K("bqkvp"), K("sc128")], writes=[K("sbq")])

        xT = ARN.view(0, BF16, 128, 4096)
        qb = ARN.view(8192, BF16, 128, 4096)
        kb = ARN.view(16384, BF16, 128, 4096)
        vb = ARN.view(24576, BF16, 128, 4096)
        kvb = ARN.view(32768, BF16, 128, 4096)
        kcs = ARN.view(0, BF16, 128, 4096)
        kvcs = ARN.view(24576, BF16, 128, 4096)
        yb = kb
        zer = AUX.view(4096, BF16, 128, 512)
        ytmp = AUX.view(5120, F32, 128, 512)
        el1 = [AUX.view(7168, F32, 128, 512), AUX.view(7168, F32, 128, 512)]
        el2 = [AUX.view(9216, BF16, 128, 512), AUX.view(9216, BF16, 128, 512)]
        el3 = [AUX.view(10240, BF16, 128, 512), AUX.view(10240, BF16, 128, 512)]
        den = BIG.view(53248, F32, 16, 512)
        denb = BIG.view(55296, BF16, 16, 512)
        tmpr = AUX.view(0, F32, 128, 1024)
        lnsq_a = BIG.view(49152, F32, 128, 1024)

        E("vector", lambda h: h.memset(zer.ap, 0.0), writes=[zer.k()])
        E("vector", lambda h: h.memset(co["carryk"].ap[:, :], 0.0),
          writes=[K("carryk")])
        E("vector", lambda h: h.memset(co["carrykv"].ap[:, :], 0.0),
          writes=[K("carrykv")])

        for blk in range(NB):
            tiles = [blk * 4 + j for j in range(4)]
            for j, ti in enumerate(tiles):
                for c in range(8):
                    sl = slice(c * 512 + j * 128, c * 512 + j * 128 + 128)
                    transpose_128(
                        stream_t(ti)[:, c * 128:(c + 1) * 128], [skey(ti)],
                        xT.ap[:, sl], [xT.k(sl.start, sl.stop)],
                        2 if (j * 8 + c) % 2 == 0 else 7)
            for e in range(24):
                pb = (e % 2) * 512
                for c in range(8):
                    wsl = wq_bf.k(c * E3 + e * 128, c * E3 + e * 128 + 128)
                    E("tensor", lambda h, e=e, c=c, pb=pb: h.matmul(
                        ps(pb, 512),
                        wq_bf.ap[:, c * E3 + e * 128: c * E3 + e * 128 + 128],
                        xT.ap[:, c * 512:(c + 1) * 512],
                        start=(c == 0), stop=(c == 7)),
                      reads=[wsl, xT.k(c * 512, (c + 1) * 512)],
                      writes=[psk(pb, 512)])
                cc_ = e % 8
                sl = slice(cc_ * 512, (cc_ + 1) * 512)
                if e >= 16:
                    E("scalar", lambda h, sl=sl, pb=pb, e=e: h.activation(
                        vb.ap[:, sl], ps(pb, 512), ACT.Identity,
                        bias=co["bqkvp"].ap[:, e:e + 1]),
                      reads=[psk(pb, 512), K("bqkvp")],
                      writes=[vb.k(sl.start, sl.stop)])
                else:
                    eb = e % 2
                    dst = qb if e < 8 else kb
                    E("vector", lambda h, pb=pb, e=e, eb=eb: h.scalar_tensor_tensor(
                        el1[eb].ap, ps(pb, 512), co["sc128"].ap[:, 0:1],
                        co["sbq"].ap[:, e:e + 1].broadcast_to([128, 512]),
                        ALU.mult, ALU.add),
                      reads=[psk(pb, 512), K("sc128"), K("sbq")],
                      writes=[el1[eb].k()])
                    E("vector", lambda h, eb=eb: h.tensor_scalar(
                        el2[eb].ap, el1[eb].ap, 0.0, None, ALU.min),
                      reads=[el1[eb].k()], writes=[el2[eb].k()])
                    E("scalar", lambda h, eb=eb: h.activation(
                        el2[eb].ap, el2[eb].ap, ACT.Exp),
                      reads=[el2[eb].k()], writes=[el2[eb].k()])
                    E("scalar", lambda h, eb=eb: h.activation(
                        el3[eb].ap, el1[eb].ap, ACT.Relu),
                      reads=[el1[eb].k()], writes=[el3[eb].k()])
                    E("vector", lambda h, dst=dst, sl=sl, eb=eb: h.tensor_add(
                        dst.ap[:, sl], el2[eb].ap, el3[eb].ap),
                      reads=[el2[eb].k(), el3[eb].k()],
                      writes=[dst.k(sl.start, sl.stop)])
            for c in range(8):
                sl = slice(c * 512, (c + 1) * 512)
                E("vector", lambda h, sl=sl: h.tensor_mul(
                    kvb.ap[:, sl], kb.ap[:, sl], vb.ap[:, sl]),
                  reads=[kb.k(sl.start, sl.stop), vb.k(sl.start, sl.stop)],
                  writes=[kvb.k(sl.start, sl.stop)])
            for c in range(8):
                sl = slice(c * 512, (c + 1) * 512)
                E("vector", lambda h, c=c, sl=sl: h.tensor_tensor_scan(
                    kcs.ap[:, sl], kb.ap[:, sl], zer.ap,
                    co["carryk"].ap[:, c:c + 1], ALU.add, ALU.add),
                  reads=[kb.k(sl.start, sl.stop), zer.k(), K("carryk", c, c + 1)],
                  writes=[kcs.k(sl.start, sl.stop)])
                E("vector", lambda h, c=c, sl=sl: h.tensor_copy(
                    co["carryk"].ap[:, c:c + 1], kcs.ap[:, sl.stop - 1:sl.stop]),
                  reads=[kcs.k(sl.stop - 1, sl.stop)],
                  writes=[K("carryk", c, c + 1)])
            for c in range(8):
                sl = slice(c * 512, (c + 1) * 512)
                E("vector", lambda h, c=c, sl=sl: h.tensor_tensor_scan(
                    kvcs.ap[:, sl], kvb.ap[:, sl], zer.ap,
                    co["carrykv"].ap[:, c:c + 1], ALU.add, ALU.add),
                  reads=[kvb.k(sl.start, sl.stop), zer.k(),
                         K("carrykv", c, c + 1)],
                  writes=[kvcs.k(sl.start, sl.stop)])
                E("vector", lambda h, c=c, sl=sl: h.tensor_copy(
                    co["carrykv"].ap[:, c:c + 1], kvcs.ap[:, sl.stop - 1:sl.stop]),
                  reads=[kvcs.k(sl.stop - 1, sl.stop)],
                  writes=[K("carrykv", c, c + 1)])
            prodb = kvb
            for c in range(8):
                sl = slice(c * 512, (c + 1) * 512)
                E("vector", lambda h, sl=sl: h.tensor_mul(
                    prodb.ap[:, sl], qb.ap[:, sl], kcs.ap[:, sl]),
                  reads=[qb.k(sl.start, sl.stop), kcs.k(sl.start, sl.stop)],
                  writes=[prodb.k(sl.start, sl.stop)])
            for c in range(8):
                E("tensor", lambda h, c=c: h.matmul(
                    ps(1536, 512)[0:16, :],
                    co["mask16"].ap[:, c * 16:(c + 1) * 16],
                    prodb.ap[:, c * 512:(c + 1) * 512],
                    start=(c == 0), stop=(c == 7)),
                  reads=[K("mask16"), prodb.k(c * 512, (c + 1) * 512)],
                  writes=[psk(1536, 512)])
            E("vector", lambda h: h.tensor_scalar_max(
                den.ap, ps(1536, 512)[0:16, :], 1e-3),
              reads=[psk(1536, 512)], writes=[den.k()])
            E("vector", lambda h: h.reciprocal(den.ap, den.ap),
              reads=[den.k()], writes=[den.k()])
            E("vector", lambda h: h.tensor_copy(denb.ap, den.ap),
              reads=[den.k()], writes=[denb.k()])
            for c in range(8):
                sl = slice(c * 512, (c + 1) * 512)
                E("tensor", lambda h, c=c: h.matmul(
                    ps(2048, 512), co["selb"].ap[:, c * 128:(c + 1) * 128],
                    denb.ap, start=True, stop=True),
                  reads=[K("selb"), denb.k()], writes=[psk(2048, 512)])
                E("vector", lambda h, sl=sl: h.tensor_mul(
                    ytmp.ap, qb.ap[:, sl], kvcs.ap[:, sl]),
                  reads=[qb.k(sl.start, sl.stop), kvcs.k(sl.start, sl.stop)],
                  writes=[ytmp.k()])
                E("vector", lambda h, sl=sl: h.tensor_mul(
                    yb.ap[:, sl], ytmp.ap, ps(2048, 512)),
                  reads=[ytmp.k(), psk(2048, 512)],
                  writes=[yb.k(sl.start, sl.stop)])
            for j, ti in enumerate(tiles):
                plo = 2560 if j % 2 == 0 else 0
                pv, pks = ps(plo, 1024), psk(plo, 1024)
                for nb in range(2):
                    E("tensor", lambda h, pv=pv, nb=nb: h.matmul(
                        pv[:, nb * 512:(nb + 1) * 512], co["onesb"].ap[:, :],
                        co["bprojb"].ap[:, nb * 512:(nb + 1) * 512],
                        start=True, stop=False),
                      reads=[K("onesb"), K("bprojb")], writes=[pks])
                for c in range(8):
                    ysl = yb.k(c * 512 + j * 128, c * 512 + j * 128 + 128)
                    for nb in range(2):
                        E("tensor", lambda h, c=c, nb=nb, pv=pv, j=j: h.matmul(
                            pv[:, nb * 512:(nb + 1) * 512],
                            yb.ap[:, c * 512 + j * 128: c * 512 + j * 128 + 128],
                            wp_bf.ap[:, c * D + nb * 512: c * D + nb * 512 + 512],
                            start=False, stop=(c == 7)),
                          reads=[ysl, wp_bf.k(c * D + nb * 512,
                                              c * D + nb * 512 + 512)],
                          writes=[pks])
                E("vector", lambda h, pv=pv, ti=ti: h.tensor_add(
                    tmpr.ap, pv, stream_t(ti)),
                  reads=[pks, skey(ti)], writes=[tmpr.k()])
                layer_norm(tmpr, stream_t(ti), [skey(ti)], lnsq_a)

        mx = co["gm"].ap
        mxk = lambda i: K("gm", i, i + 1)
        for i in range(NT):
            if i == 0:
                E("vector", lambda h, i=i: h.reduce_max(
                    mx[:, 0:1], stream_t(i), AX),
                  reads=[skey(i)], writes=[mxk(0)])
            else:
                E("vector", lambda h, i=i: h.reduce_max(
                    mx[:, 1:2], stream_t(i), AX),
                  reads=[skey(i)], writes=[mxk(1)])
                E("vector", lambda h: h.tensor_max(mx[:, 0:1], mx[:, 0:1],
                                                   mx[:, 1:2]),
                  reads=[mxk(0), mxk(1)], writes=[mxk(0)])

        def global_scale(idx, skey_s, skey_rs):
            preduce(mx[:, 0:1], [mxk(0)], mx[0:1, 2:3], [mxk(2)], 128, ALU.max)
            E("gpsimd", lambda h, idx=idx: h.dma_start(
                out=mxio[idx:idx + 1, :], in_=mx[0:1, 2:3]),
              reads=[mxk(2)], writes=[dk("mxio", idx, idx + 1)], dma=True)
            E("gpsimd", lambda h, idx=idx: h.collective_compute(
                "AllReduce", ALU.max, replica_groups=[list(range(ncores))],
                ins=[mxio[idx:idx + 1, :].opt()],
                outs=[mxout[idx:idx + 1, :].opt()]),
              reads=[dk("mxio", idx, idx + 1)],
              writes=[dk("mxout", idx, idx + 1)], cc=True)
            E("gpsimd", lambda h, idx=idx: h.dma_start(
                out=mx[0:1, 3:4], in_=mxout[idx:idx + 1, :]),
              reads=[dk("mxout", idx, idx + 1)], writes=[mxk(3)], dma=True)
            bcast_row(mx[0:1, 3:4], [mxk(3)], co[skey_s].ap[:, :],
                      [K(skey_s)], 1)
            E("vector", lambda h: h.tensor_scalar(
                co[skey_s].ap[:, :], co[skey_s].ap[:, :], 1e-8, 1.0 / 255.0,
                ALU.max, ALU.mult),
              reads=[K(skey_s)], writes=[K(skey_s)])
            E("vector", lambda h: h.reciprocal(co[skey_rs].ap[:, :],
                                               co[skey_s].ap[:, :]),
              reads=[K(skey_s)], writes=[K(skey_rs)])

        global_scale(0, "s1", "rs1")

        u1T = ARN.view(0, BF16, 128, 8 * T)
        qm5 = BIG.view(32768, F32, 128, 1024)
        qng = BIG.view(36864, F32, 128, 1024)
        u1tile = BIG.view(40960, BF16, 128, 1024)
        rqiA = BIG.view(45056, I32, 128, 1024)
        rdA = BIG.view(49152, F32, 128, 1024)
        rtA = BIG.view(53248, F32, 128, 1024)
        u1tl = [u1tile, BIG.view(57344, BF16, 128, 1024)]

        def u1_quant(i):
            b = i % 2
            E("scalar", lambda h, i=i: h.activation(
                qm5.ap, stream_t(i), ACT.Identity,
                scale=co["rs1"].ap[:, 0:1]),
              reads=[skey(i), K("rs1")], writes=[qm5.k()])
            round_neg(qm5.ap, [qm5.k()], rqiA, rdA, rtA, qng.ap, [qng.k()])
            E("scalar", lambda h, b=b: h.activation(u1tl[b].ap, qng.ap,
                                                    ACT.Relu, scale=-1.0),
              reads=[qng.k()], writes=[u1tl[b].k()])

        def u1_tr(i):
            b = i % 2
            for c in range(8):
                sl = slice(c * T + i * 128, c * T + i * 128 + 128)
                transpose_128(
                    u1tl[b].ap[:, c * 128:(c + 1) * 128], [u1tl[b].k()],
                    u1T.ap[:, sl], [u1T.k(sl.start, sl.stop)],
                    c % 4)

        u1_quant(0)
        for i in range(1, NT):
            u1_quant(i)
            u1_tr(i - 1)
        u1_tr(NT - 1)
        w1stg = [AUX.view(0, F32, 128, 1024), AUX.view(4096, F32, 128, 1024)]
        w1qn = AUX.view(8192, BF16, 128, 1024)
        w1qT = [AUX.view(10240, BF16, 128, 1024), BIG.view(43008, BF16, 128, 1024)]
        lnsq_b = ARN.view(49152, F32, 128, 1024)
        hstg = [BIG.view(0, F32, 128, T), BIG.view(4 * T, F32, 128, T)]
        swt = co["swt"].ap
        swk = lambda i: K("swt", i, i + 1)
        NTP = T // 512
        E("vector", lambda h: h.memset(mx[:, 0:1], 0.0), writes=[mxk(0)])

        def f1_load_quant(f):
            b = f % 2
            E("sync", lambda h, f=f, b=b: h.dma_start(
                out=w1stg[b].ap, in_=w1n[f * 128:(f + 1) * 128, :]),
              reads=[w1stg[b].k()], writes=[w1stg[b].k()], dma=True)
            E("vector", lambda h, b=b: h.reduce_max(
                swt[:, 0:1], w1stg[b].ap, AX, apply_absolute_value=True),
              reads=[w1stg[b].k()], writes=[swk(0)])
            E("vector", lambda h: h.tensor_scalar(
                swt[:, 1:2], swt[:, 0:1], 1e-8, 1.0 / 127.0, ALU.max, ALU.mult),
              reads=[swk(0)], writes=[swk(1)])
            E("vector", lambda h: h.reciprocal(swt[:, 2:3], swt[:, 1:2]),
              reads=[swk(1)], writes=[swk(2)])
            E("vector", lambda h: h.tensor_tensor(
                swt[:, 3:4], swt[:, 1:2], co["s1"].ap[:, 0:1], ALU.mult),
              reads=[swk(1), K("s1")], writes=[swk(3)])
            E("vector", lambda h: h.tensor_scalar(
                swt[:, 3:4], swt[:, 3:4], -1.0, None, ALU.mult),
              reads=[swk(3)], writes=[swk(3)])
            E("scalar", lambda h, b=b: h.activation(
                qm5.ap, w1stg[b].ap, ACT.Identity, scale=swt[:, 2:3]),
              reads=[w1stg[b].k(), swk(2)], writes=[qm5.k()])
            round_neg(qm5.ap, [qm5.k()], rqiA, rdA, rtA, qng.ap, [qng.k()])
            E("vector", lambda h: h.tensor_scalar(
                w1qn.ap, qng.ap, swt[:, 3:4], None, ALU.mult),
              reads=[qng.k(), swk(3)], writes=[w1qn.k()])
            bqa = co["bq1"].ap
            bk_ = K("bq1", f, f + 1)
            E("vector", lambda h, f=f: h.tensor_tensor(
                bqa[:, f:f + 1], co["b1p"].ap[:, f:f + 1],
                co["rs1"].ap[:, 0:1], ALU.mult),
              reads=[K("b1p"), K("rs1")], writes=[bk_])
            E("vector", lambda h, f=f: h.tensor_tensor(
                bqa[:, f:f + 1], bqa[:, f:f + 1], swt[:, 2:3], ALU.mult),
              reads=[bk_, swk(2)], writes=[bk_])
            E("vector", lambda h, f=f: h.tensor_copy(
                co["rqa"].ap[:, 0:1], bqa[:, f:f + 1]),
              reads=[bk_], writes=[K("rqa")])
            E("vector", lambda h, f=f: h.tensor_tensor(
                co["ln"].ap[:, 24:25], bqa[:, f:f + 1], co["rqa"].ap[:, 0:1],
                ALU.subtract),
              reads=[bk_, K("rqa")], writes=[K("ln", 24, 25)])
            E("vector", lambda h: h.scalar_tensor_tensor(
                co["ln"].ap[:, 25:26], co["ln"].ap[:, 24:25], 0.5,
                co["rqa"].ap[:, 0:1], ALU.is_gt, ALU.add),
              reads=[K("ln", 24, 25), K("rqa")], writes=[K("ln", 25, 26)])
            E("vector", lambda h, f=f: h.scalar_tensor_tensor(
                bqa[:, f:f + 1], co["ln"].ap[:, 24:25], -0.5,
                co["ln"].ap[:, 25:26], ALU.is_lt, ALU.subtract),
              reads=[K("ln", 24, 25), K("ln", 25, 26)], writes=[bk_])
            E("vector", lambda h, f=f: h.tensor_scalar(
                bqa[:, f:f + 1], bqa[:, f:f + 1], -1.0, None, ALU.mult),
              reads=[bk_], writes=[bk_])
            E("vector", lambda h, f=f: h.tensor_tensor(
                bqa[:, f:f + 1], bqa[:, f:f + 1], swt[:, 1:2], ALU.mult),
              reads=[bk_, swk(1)], writes=[bk_])
            E("vector", lambda h, f=f: h.tensor_tensor(
                bqa[:, f:f + 1], bqa[:, f:f + 1], co["s1"].ap[:, 0:1], ALU.mult),
              reads=[bk_, K("s1")], writes=[bk_])
            for c in range(8):
                transpose_128(
                    w1qn.ap[:, c * 128:(c + 1) * 128], [w1qn.k()],
                    w1qT[b].ap[:, c * 128:(c + 1) * 128],
                    [w1qT[b].k(c * 128, (c + 1) * 128)],
                    c % 2)

        def f1_mms(f):
            b = f % 2
            hb = f % 2
            for c in range(8):
                wk_ = w1qT[b].k(c * 128, (c + 1) * 128)
                for tp in range(NTP):
                    E("tensor", lambda h, c=c, tp=tp, b=b: h.matmul(
                        ps(tp * 512, 512),
                        w1qT[b].ap[:, c * 128:(c + 1) * 128],
                        u1T.ap[:, c * T + tp * 512: c * T + tp * 512 + 512],
                        start=(c == 0), stop=(c == 7)),
                      reads=[wk_, u1T.k(c * T + tp * 512, c * T + tp * 512 + 512)],
                      writes=[psk(tp * 512, 512)])
            for tp in range(NTP):
                E("scalar", lambda h, tp=tp, hb=hb, f=f: h.activation(
                    hstg[hb].ap[:, tp * 512:(tp + 1) * 512], ps(tp * 512, 512),
                    ACT.Relu, bias=co["bq1"].ap[:, f:f + 1]),
                  reads=[psk(tp * 512, 512), K("bq1", f, f + 1)],
                  writes=[hstg[hb].k(tp * 512, (tp + 1) * 512)])

        def f1_spill(f):
            hb = f % 2
            E("vector", lambda h, hb=hb: h.reduce_max(
                mx[:, 1:2], hstg[hb].ap, AX),
              reads=[hstg[hb].k()], writes=[mxk(1)])
            E("vector", lambda h: h.tensor_max(mx[:, 0:1], mx[:, 0:1],
                                               mx[:, 1:2]),
              reads=[mxk(0), mxk(1)], writes=[mxk(0)])
            E("scalar", lambda h, f=f, hb=hb: h.dma_start(
                out=h_hbm[f, :, :], in_=hstg[hb].ap),
              reads=[hstg[hb].k()],
              writes=[dk("h_hbm", f * 128 * T, (f + 1) * 128 * T)], dma=True)

        f1_load_quant(0)
        f1_load_quant(1)
        f1_mms(0)
        for f in range(2, NFC):
            f1_load_quant(f)
            f1_mms(f - 1)
            f1_spill(f - 2)
        f1_mms(NFC - 1)
        f1_spill(NFC - 2)
        f1_spill(NFC - 1)

        global_scale(1, "s2", "rs2")
        E("vector", lambda h: h.tensor_scalar(
            co["negs2"].ap[:, :], co["s2"].ap[:, :], -1.0, None, ALU.mult),
          reads=[K("s2")], writes=[K("negs2")])

        w2qT = BIG.view(0, BF16, 128, 32 * D)
        w2stg = [AUX.view(0, F32, 128, 512), AUX.view(2048, F32, 128, 512)]
        w2m = [AUX.view(4096, F32, 128, 512), AUX.view(6144, F32, 128, 512)]
        w2qi = AUX.view(8192, I32, 128, 512)
        w2d = AUX.view(10240, F32, 128, 512)
        w2t = ARN.view(12288, F32, 128, 512)
        w2nq = ARN.view(14336, F32, 128, 512)
        w2qn = [ARN.view(16384, BF16, 128, 512), ARN.view(17408, BF16, 128, 512)]
        sw2p = co["bq2p"].ap
        for dch in range(8):
            for hf in range(8):
                b = (dch * 8 + hf) % 2
                E("sync", lambda h, dch=dch, hf=hf, b=b: h.dma_start(
                    out=w2stg[b].ap,
                    in_=w2n[dch * 128:(dch + 1) * 128,
                            hf * 512:(hf + 1) * 512]),
                  reads=[w2stg[b].k()], writes=[w2stg[b].k()], dma=True)
                if hf == 0:
                    E("vector", lambda h, b=b: h.reduce_max(
                        swt[:, 0:1], w2stg[b].ap, AX,
                        apply_absolute_value=True),
                      reads=[w2stg[b].k()], writes=[swk(0)])
                else:
                    E("vector", lambda h, b=b: h.reduce_max(
                        swt[:, 1:2], w2stg[b].ap, AX,
                        apply_absolute_value=True),
                      reads=[w2stg[b].k()], writes=[swk(1)])
                    E("vector", lambda h: h.tensor_max(
                        swt[:, 0:1], swt[:, 0:1], swt[:, 1:2]),
                      reads=[swk(0), swk(1)], writes=[swk(0)])
            E("vector", lambda h, dch=dch: h.tensor_scalar(
                sw2p[:, dch:dch + 1], swt[:, 0:1], 1e-8, 1.0 / 127.0,
                ALU.max, ALU.mult),
              reads=[swk(0)], writes=[K("bq2p", dch, dch + 1)])
        for dch in range(8):
            E("vector", lambda h, dch=dch: h.reciprocal(
                co["ln"].ap[:, 16 + dch:17 + dch], sw2p[:, dch:dch + 1]),
              reads=[K("bq2p", dch, dch + 1)],
              writes=[K("ln", 16 + dch, 17 + dch)])
            E("vector", lambda h, dch=dch: h.tensor_scalar(
                co["ln"].ap[:, 24 + dch:25 + dch],
                sw2p[:, dch:dch + 1], -1.0, None, ALU.mult),
              reads=[K("bq2p", dch, dch + 1)],
              writes=[K("ln", 24 + dch, 25 + dch)])

        def w2_lm(j):
            dch, hf = divmod(j, 8)
            b = j % 2
            E("sync", lambda h, dch=dch, hf=hf, b=b: h.dma_start(
                out=w2stg[b].ap,
                in_=w2n[dch * 128:(dch + 1) * 128, hf * 512:(hf + 1) * 512]),
              reads=[w2stg[b].k()], writes=[w2stg[b].k()], dma=True)
            E("scalar", lambda h, b=b, dch=dch: h.activation(
                w2m[b].ap, w2stg[b].ap, ACT.Identity,
                scale=co["ln"].ap[:, 16 + dch:17 + dch]),
              reads=[w2stg[b].k(), K("ln", 16 + dch, 17 + dch)],
              writes=[w2m[b].k()])

        def w2_qt(j):
            dch, hf = divmod(j, 8)
            b = j % 2
            E("scalar", lambda h, b=b: h.copy(w2qi.ap, w2m[b].ap),
              reads=[w2m[b].k()], writes=[w2qi.k()])
            E("vector", lambda h, b=b: h.tensor_tensor(
                w2d.ap, w2m[b].ap, w2qi.ap, ALU.subtract),
              reads=[w2m[b].k(), w2qi.k()], writes=[w2d.k()])
            E("vector", lambda h, b=b: h.scalar_tensor_tensor(
                w2t.ap, w2d.ap, 0.5, w2qi.ap, ALU.is_gt, ALU.add),
              reads=[w2d.k(), w2qi.k()], writes=[w2t.k()])
            E("vector", lambda h, b=b: h.scalar_tensor_tensor(
                w2nq.ap, w2d.ap, -0.5, w2t.ap, ALU.is_lt, ALU.subtract),
              reads=[w2d.k(), w2t.k()], writes=[w2nq.k()])
            E("vector", lambda h, b=b, dch=dch: h.tensor_scalar(
                w2qn[b].ap, w2nq.ap, co["ln"].ap[:, 24 + dch:25 + dch],
                None, ALU.mult),
              reads=[w2nq.k(), K("ln", 24 + dch, 25 + dch)],
              writes=[w2qn[b].k()])
            for fc in range(4):
                fa = hf * 4 + fc
                sl = slice(fa * D + dch * 128, fa * D + dch * 128 + 128)
                transpose_128(
                    w2qn[b].ap[:, fc * 128:(fc + 1) * 128], [w2qn[b].k()],
                    w2qT.ap[:, sl], [w2qT.k(sl.start, sl.stop)],
                    6 + (fc % 2))

        w2_lm(0)
        for j in range(1, 64):
            w2_lm(j)
            w2_qt(j - 1)
        w2_qt(63)

        bt = co["ln"].ap
        btk = lambda i: K("ln", i, i + 1)
        for dch in range(8):
            E("vector", lambda h, dch=dch: h.reciprocal(
                bt[:, 0:1], sw2p[:, dch:dch + 1]),
              reads=[K("bq2p", dch, dch + 1)], writes=[btk(0)])
            E("vector", lambda h, dch=dch: h.tensor_tensor(
                bt[:, 1:2], co["b2p"].ap[:, dch:dch + 1], bt[:, 0:1], ALU.mult),
              reads=[K("b2p"), btk(0)], writes=[btk(1)])
            E("vector", lambda h: h.tensor_tensor(
                bt[:, 1:2], bt[:, 1:2], co["rs2"].ap[:, 0:1], ALU.mult),
              reads=[btk(1), K("rs2")], writes=[btk(1)])
            E("vector", lambda h: h.tensor_copy(co["rqa"].ap[:, 0:1],
                                                bt[:, 1:2]),
              reads=[btk(1)], writes=[K("rqa")])
            E("vector", lambda h: h.tensor_tensor(
                bt[:, 2:3], bt[:, 1:2], co["rqa"].ap[:, 0:1], ALU.subtract),
              reads=[btk(1), K("rqa")], writes=[btk(2)])
            E("vector", lambda h: h.scalar_tensor_tensor(
                bt[:, 3:4], bt[:, 2:3], 0.5, co["rqa"].ap[:, 0:1],
                ALU.is_gt, ALU.add),
              reads=[btk(2), K("rqa")], writes=[btk(3)])
            E("vector", lambda h: h.scalar_tensor_tensor(
                bt[:, 1:2], bt[:, 2:3], -0.5, bt[:, 3:4],
                ALU.is_lt, ALU.subtract),
              reads=[btk(2), btk(3)], writes=[btk(1)])
            E("vector", lambda h, dch=dch: h.tensor_tensor(
                co["bq2p"].ap[:, dch:dch + 1], bt[:, 1:2],
                sw2p[:, dch:dch + 1], ALU.mult),
              reads=[btk(1), K("bq2p", dch, dch + 1)],
              writes=[K("bq2p", dch, dch + 1)])
        bq2f = AUX.view(4096, F32, 1, D)
        E("sync", lambda h: h.dma_start(out=bq2s, in_=co["bq2p"].ap[:, 0:8]),
          reads=[K("bq2p")], writes=[dk("bq2s")], dma=True)
        E("sync", lambda h: h.dma_start(
            out=bq2f.ap,
            in_=bq2s.transpose([1, 0]).unsqueeze(0),
            allow_slow_non_contiguous=True),
          reads=[dk("bq2s")], writes=[bq2f.k()], dma=True)
        E("vector", lambda h: h.tensor_copy(co["bq2r"].ap[:, :], bq2f.ap),
          reads=[bq2f.k()], writes=[K("bq2r")])

        hldg = [AUX.view(4096, F32, 128, 1024), AUX.view(8192, F32, 128, 1024)]
        hm2 = [ARN.view(8192, F32, 128, 1024), ARN.view(12288, F32, 128, 1024)]
        uqi = [ARN.view(16384, I32, 128, 1024), ARN.view(20480, I32, 128, 1024)]
        ud = [ARN.view(24576, F32, 128, 1024), ARN.view(28672, F32, 128, 1024)]
        ut = [ARN.view(32768, F32, 128, 1024), ARN.view(36864, F32, 128, 1024)]
        u2c = [ARN.view(40960, BF16, 128, 1024), ARN.view(43008, BF16, 128, 1024)]

        wq1b = ARN.view(2048, BF16, 128, 8 * HD)
        wk1b = ARN.view(3072, BF16, 128, 8 * HD)
        wv1b = ARN.view(4096, BF16, 128, 8 * HD)
        wl1stg = ARN.view(5120, F32, 128, HD)
        for src, dstv in ((wqT1, wq1b), (wkT1, wk1b), (wvT1, wv1b)):
            for c in range(8):
                E("sync", lambda h, src=src, c=c: h.dma_start(
                    out=wl1stg.ap, in_=src[c * 128:(c + 1) * 128, :]),
                  reads=[wl1stg.k()], writes=[wl1stg.k()], dma=True)
                E("vector", lambda h, dstv=dstv, c=c: h.tensor_copy(
                    dstv.ap[:, c * HD:(c + 1) * HD], wl1stg.ap),
                  reads=[wl1stg.k()],
                  writes=[dstv.k(c * HD, (c + 1) * HD)])
        ktmp = ARN.view(45056, BF16, HD, 512)
        vtmp = ARN.view(46080, BF16, 128, 4 * 65)
        vT_t = ARN.view(47104, F32, HD, 512)
        xTb = ARN.view(49152, BF16, 128, 4096)
        qTd = ARN.view(57344, BF16, 128, T)
        E("vector", lambda h: h.memset(
            vtmp.ap.rearrange("p (c e) -> p c e", e=65)[:, :, 64:65], 1.0),
          writes=[vtmp.k()])

        def l1_qkv(blk):
            tiles = [blk * 4 + j for j in range(4)]
            for j, ti in enumerate(tiles):
                for c in range(8):
                    sl = slice(c * 512 + j * 128, c * 512 + j * 128 + 128)
                    transpose_128(
                        stream_t(ti)[:, c * 128:(c + 1) * 128], [skey(ti)],
                        xTb.ap[:, sl], [xTb.k(sl.start, sl.stop)],
                        0 if (j * 8 + c) % 2 == 0 else 7)
            for wi, (dstv, plo) in enumerate(((wq1b, 512), (wk1b, 1024),
                                              (wv1b, 1536))):
                for c in range(8):
                    wk_ = dstv.k(c * HD, (c + 1) * HD)
                    E("tensor", lambda h, dstv=dstv, c=c, plo=plo: h.matmul(
                        ps(plo, 512)[0:HD, :],
                        dstv.ap[:, c * HD:(c + 1) * HD],
                        xTb.ap[:, c * 512:(c + 1) * 512],
                        start=(c == 0), stop=(c == 7)),
                      reads=[wk_, xTb.k(c * 512, (c + 1) * 512)],
                      writes=[psk(plo, 512)])
            E("scalar", lambda h, blk=blk: h.activation(
                qTd.ap[0:HD, blk * 512:(blk + 1) * 512], ps(512, 512)[0:HD, :],
                ACT.Identity, bias=co["bqkv1"].ap[:, 0:1]),
              reads=[psk(512, 512), K("bqkv1")],
              writes=[qTd.k(blk * 512, (blk + 1) * 512)])
            E("scalar", lambda h: h.activation(
                ktmp.ap, ps(1024, 512)[0:HD, :],
                ACT.Identity, bias=co["bqkv1"].ap[:, 1:2]),
              reads=[psk(1024, 512), K("bqkv1")], writes=[ktmp.k()])
            E("scalar", lambda h: h.activation(
                vT_t.ap, ps(1536, 512)[0:HD, :], ACT.Identity,
                bias=co["bqkv1"].ap[:, 2:3]),
              reads=[psk(1536, 512), K("bqkv1")], writes=[vT_t.k()])
            E("gpsimd", lambda h, blk=blk: h.dma_start(
                out=kg_in[:, blk * 512:(blk + 1) * 512], in_=ktmp.ap),
              reads=[ktmp.k()],
              writes=[dk("kg_in", blk * 1024, (blk + 1) * 1024)], dma=True,
              fam="dram:kg_in")
            for j in range(4):
                pslo = 2048 + (j % 2) * 512
                E("tensor", lambda h, j=j, pslo=pslo: h.transpose(
                    ps(pslo, 128)[:, 0:HD], vT_t.ap[:, j * 128:(j + 1) * 128],
                    co["identf"].ap[0:HD, 0:HD]),
                  reads=[vT_t.k(j * 128, (j + 1) * 128), K("identf")],
                  writes=[psk(pslo, 512)])
                E("vector", lambda h, j=j, pslo=pslo: h.tensor_copy(
                    vtmp.ap[:, j * 65: j * 65 + 64], ps(pslo, 128)[:, 0:HD]),
                  reads=[psk(pslo, 512)], writes=[vtmp.k(j * 65, j * 65 + 64)])
            E("gpsimd", lambda h, blk=blk: h.dma_start(
                out=vg_in.rearrange("(c p) e -> p c e", p=128)[
                    :, blk * 4:(blk + 1) * 4, :],
                in_=vtmp.ap.rearrange("p (c e) -> p c e", e=65)),
              reads=[vtmp.k()],
              writes=[dk("vg_in", blk * 520, (blk + 1) * 520)], dma=True,
              fam="dram:vg_in")

        NF2 = NFC // 2
        for tt in range(T // 512):
            for sub in range(4):
                for nb in range(2):
                    lo = sub * 1024 + nb * 512
                    E("tensor", lambda h, lo=lo, nb=nb: h.matmul(
                        ps(lo, 512), co["onesb"].ap[0:1, :],
                        co["bq2r"].ap[:, nb * 512:(nb + 1) * 512],
                        start=True, stop=False),
                      reads=[K("onesb"), K("bq2r")], writes=[psk(lo, 512)])

            def b_load(i, tt=tt):
                b = i % 2
                E("sync", lambda h, i=i, tt=tt, b=b: h.dma_start(
                    out=hldg[b].ap.rearrange("p (g w) -> p g w", w=512),
                    in_=h_hbm[2 * i:2 * i + 2, :,
                              tt * 512:(tt + 1) * 512].transpose([1, 0, 2])),
                  reads=[hldg[b].k(),
                         dk("h_hbm", 2 * i * 128 * T, (2 * i + 2) * 128 * T)],
                  writes=[hldg[b].k()], dma=True)
                E("scalar", lambda h, b=b: h.activation(
                    hm2[b].ap, hldg[b].ap, ACT.Identity,
                    scale=co["rs2"].ap[:, 0:1]),
                  reads=[hldg[b].k(), K("rs2")], writes=[hm2[b].k()])
                E("scalar", lambda h, b=b: h.copy(uqi[b].ap, hm2[b].ap),
                  reads=[hm2[b].k()], writes=[uqi[b].k()])

            def b_work(i, tt=tt):
                b = i % 2
                E("vector", lambda h, b=b: h.tensor_tensor(
                    ud[b].ap, hm2[b].ap, uqi[b].ap, ALU.subtract),
                  reads=[hm2[b].k(), uqi[b].k()], writes=[ud[b].k()])
                E("vector", lambda h, b=b: h.scalar_tensor_tensor(
                    ut[b].ap, ud[b].ap, 0.5, uqi[b].ap, ALU.is_gt, ALU.add),
                  reads=[ud[b].k(), uqi[b].k()], writes=[ut[b].k()])
                E("vector", lambda h, b=b: h.scalar_tensor_tensor(
                    u2c[b].ap, ud[b].ap, -0.5, ut[b].ap,
                    ALU.is_lt, ALU.subtract),
                  reads=[ud[b].k(), ut[b].k()], writes=[u2c[b].k()])
                for ff in range(2):
                    f = 2 * i + ff
                    for sub in range(4):
                        for nb in range(2):
                            lo = sub * 1024 + nb * 512
                            E("tensor", lambda h, f=f, ff=ff, sub=sub, nb=nb,
                              b=b, lo=lo: h.matmul(
                                  ps(lo, 512),
                                  u2c[b].ap[:, ff * 512 + sub * 128:
                                            ff * 512 + sub * 128 + 128],
                                  w2qT.ap[:, f * D + nb * 512:
                                          f * D + nb * 512 + 512],
                                  start=False, stop=(f == NFC - 1)),
                              reads=[u2c[b].k(ff * 512 + sub * 128,
                                              ff * 512 + sub * 128 + 128),
                                     w2qT.k(f * D + nb * 512,
                                            f * D + nb * 512 + 512)],
                              writes=[psk(lo, 512)])

            b_load(0)
            for i in range(1, NF2):
                b_load(i)
                b_work(i - 1)
            b_work(NF2 - 1)
            for sub in range(4):
                ti = tt * 4 + sub
                E("vector", lambda h, sub=sub, ti=ti: h.scalar_tensor_tensor(
                    tmpr.ap, ps(sub * 1024, 1024), co["negs2"].ap[:, 0:1],
                    stream_t(ti), ALU.mult, ALU.add),
                  reads=[psk(sub * 1024, 1024), K("negs2"), skey(ti)],
                  writes=[tmpr.k()])
                layer_norm(tmpr, stream_t(ti), [skey(ti)], lnsq_b)
            l1_qkv(tt)

        mstg = [AUX.view(3328, F32, 128, 1024), AUX.view(7424, F32, 128, 1024)]
        mred = AUX.view(11520, F32, 128, 1)
        E("vector", lambda h: h.memset(co["macc"].ap[:, 0:3], 0.0),
          writes=[K("macc")])

        def mean_pass(idx, loader, q, stg0, mr):
            for f in range(NFC):
                b = f % 2
                E(q, lambda h, f=f, b=b: h.dma_start(
                    out=stg0[b].ap.rearrange("p (j w) -> p j w", w=128),
                    in_=loader(f)),
                  reads=[stg0[b].k()], writes=[stg0[b].k()], dma=True)
                E("vector", lambda h, b=b: h.reduce_sum(
                    mr.ap, stg0[b].ap, AX, apply_absolute_value=True),
                  reads=[stg0[b].k()], writes=[mr.k()])
                E("vector", lambda h, idx=idx: h.tensor_tensor(
                    co["macc"].ap[:, idx:idx + 1], co["macc"].ap[:, idx:idx + 1],
                    mr.ap, ALU.add),
                  reads=[mr.k(), K("macc", idx, idx + 1)],
                  writes=[K("macc", idx, idx + 1)])

        mstgu = [BIG.view(49152, F32, 128, 1024), BIG.view(53248, F32, 128, 1024)]
        mstgd = [BIG.view(57344, F32, 128, 1024), BIG.view(61440, F32, 128, 1024)]
        mredu = AUX.view(11524, F32, 128, 1)
        mredd = AUX.view(11528, F32, 128, 1)
        E("gpsimd", lambda h: h.dma_start(out=qTd.ap[64:128, :],
                                          in_=qTd.ap[0:64, :]),
          reads=[qTd.k()], writes=[qTd.k()], dma=True)
        E("gpsimd", lambda h: h.collective_compute(
            "AllGather", ALU.bypass, replica_groups=[list(range(ncores))],
            ins=[kg_in.opt()], outs=[kg_out.opt()]),
          reads=[dk("kg_in")], writes=[dk("kg_out")], cc=True)
        E("gpsimd", lambda h: h.collective_compute(
            "AllGather", ALU.bypass, replica_groups=[list(range(ncores))],
            ins=[vg_in.opt()], outs=[vg_out.opt()]),
          reads=[dk("vg_in")], writes=[dk("vg_out")], cc=True)
        kTp = ARN.view(0, BF16, 128, (NCH // 2) * 128)
        vpack = BIG.view(0, BF16, 128, NCH * 65)
        nloc = T // 128
        for i in range(ncores):
            for par in range(2):
                m0 = (i * nloc + par) // 2
                cnt = nloc // 2
                E("sync", lambda h, i=i, par=par, cnt=cnt, m0=m0: h.dma_start(
                    out=kTp.ap[par * 64:par * 64 + 64, :].rearrange(
                        "p (m w) -> p m w", w=128)[:, m0:m0 + cnt, :],
                    in_=kg_out[i * HD:(i + 1) * HD, :].rearrange(
                        "p (m w) -> p m w", w=128)[:, par::2, :]),
                  reads=[dk("kg_out")],
                  writes=[kTp.k(m0 * 128, (m0 + cnt) * 128)], dma=True,
                  fam=f"kTp{i}")
        nvch = T // 128
        for i in range(ncores):
            E("sync", lambda h, i=i: h.dma_start(
                out=vpack.ap.rearrange("p (c e) -> p c e", e=65)[
                    :, i * nvch:(i + 1) * nvch, :],
                in_=vg_out.rearrange("(c p) e -> p c e", p=128)[
                    :, i * nvch:(i + 1) * nvch, :]),
              reads=[dk("vg_out")],
              writes=[vpack.k(i * nvch * 65, (i + 1) * nvch * 65)], dma=True,
              fam=f"vpk{i}")
        mean_pass(0, lambda f: wgblk[f, :, :].rearrange("(j p) w -> p j w", p=128),
                  "scalar", mstg, mred)
        mean_pass(1, lambda f: wublk[f, :, :].rearrange("(j p) w -> p j w", p=128),
                  "gpsimd", mstgu, mredu)
        mean_pass(2, lambda f: wdT[f * 128:(f + 1) * 128, :].rearrange("p (j w) -> p j w", w=128),
                  "sync", mstgd, mredd)

        E("tensor", lambda h: h.transpose(
            ps(3584, 128)[0:3, :], co["macc"].ap[:, 0:3],
            co["identf"].ap[:, :]),
          reads=[K("macc"), K("identf")], writes=[psk(3584, 512)])
        E("vector", lambda h: h.reduce_sum(
            co["macc"].ap[0:3, 3:4], ps(3584, 128)[0:3, :], AX),
          reads=[psk(3584, 512)], writes=[K("macc", 3, 4)])
        E("tensor", lambda h: h.transpose(
            ps(3584, 128)[0:1, 0:3], co["macc"].ap[0:3, 3:4],
            co["identf"].ap[0:3, 0:3]),
          reads=[K("macc", 3, 4), K("identf")], writes=[psk(3584, 512)])
        E("vector", lambda h: h.tensor_copy(co["row"].ap[0:1, 0:3],
                                            ps(3584, 128)[0:1, 0:3]),
          reads=[psk(3584, 512)], writes=[K("row", 0, 3)])
        bcast_row(co["row"].ap[0:1, 0:3], [K("row", 0, 3)],
                  co["macc"].ap[:, 0:3], [K("macc", 0, 3)], 3)
        for idx, mnm in ((0, "mg"), (1, "mu"), (2, "md")):
            E("vector", lambda h, idx=idx, mnm=mnm: h.tensor_scalar(
                co[mnm].ap[:, :], co["macc"].ap[:, idx:idx + 1],
                1.0 / (FF * D), 1e-5, ALU.mult, ALU.max),
              reads=[K("macc", idx, idx + 1)], writes=[K(mnm)])
            E("vector", lambda h, mnm=mnm: h.reciprocal(
                co["r" + mnm].ap[:, :], co[mnm].ap[:, :]),
              reads=[K(mnm)], writes=[K("r" + mnm)])
            E("vector", lambda h, mnm=mnm: h.tensor_scalar(
                co["neg" + mnm].ap[:, :], co[mnm].ap[:, :], -1.0, None,
                ALU.mult),
              reads=[K(mnm)], writes=[K("neg" + mnm)])

        q0 = T * 8
        qstg = [ARN.view(q0, F32, 128, 1024), ARN.view(q0 + 4096, F32, 128, 1024)]
        qm = ARN.view(q0 + 8192, F32, 128, 1024)
        qqi = ARN.view(q0 + 12288, I32, 128, 1024)
        qd = ARN.view(q0 + 16384, F32, 128, 1024)
        qt = ARN.view(q0 + 20480, F32, 128, 1024)
        qout = [ARN.view(q0 + 12288, BF16, 128, 1024),
                ARN.view(q0 + 14336, BF16, 128, 1024)]

        def quant_gu(f):
            for mi, (blkap, rk, dsth) in enumerate(
                    ((wgblk, "rmg", wgq_hbm), (wublk, "rmu", wuq_hbm))):
                b = (2 * f + mi) % 2
                E("sync", lambda h, f=f, b=b, blkap=blkap: h.dma_start(
                    out=qstg[b].ap.rearrange("p (j w) -> p j w", w=128),
                    in_=blkap[f, :, :].rearrange("(j p) w -> p j w", p=128)),
                  reads=[qstg[b].k()], writes=[qstg[b].k()], dma=True)
                E("vector", lambda h, b=b, rk=rk: h.tensor_scalar(
                    qm.ap, qstg[b].ap, co[rk].ap[:, 0:1], None, ALU.mult),
                  reads=[qstg[b].k(), K(rk)], writes=[qm.k()])
                E("vector", lambda h: h.tensor_copy(qqi.ap, qm.ap),
                  reads=[qm.k()], writes=[qqi.k()])
                E("vector", lambda h: h.tensor_tensor(
                    qd.ap, qm.ap, qqi.ap, ALU.subtract),
                  reads=[qm.k(), qqi.k()], writes=[qd.k()])
                E("vector", lambda h: h.scalar_tensor_tensor(
                    qt.ap, qd.ap, 0.5, qqi.ap, ALU.is_gt, ALU.add),
                  reads=[qd.k(), qqi.k()], writes=[qt.k()])
                E("vector", lambda h: h.scalar_tensor_tensor(
                    qm.ap, qd.ap, -0.5, qt.ap, ALU.is_lt, ALU.subtract),
                  reads=[qd.k(), qt.k()], writes=[qm.k()])
                E("vector", lambda h, mi=mi: h.tensor_scalar(
                    qout[mi].ap, qm.ap, -1.0, 1.0, ALU.max, ALU.min),
                  reads=[qm.k()], writes=[qout[mi].k()])
                E("gpsimd", lambda h, f=f, mi=mi, dsth=dsth: h.dma_start(
                    out=dsth[f, :, :], in_=qout[mi].ap),
                  reads=[qout[mi].k()],
                  writes=[dk("wgq_hbm" if mi == 0 else "wuq_hbm",
                             f * 128 * D * 2, (f + 1) * 128 * D * 2)],
                  dma=True,
                  fam="dram:wgq_hbm" if mi == 0 else "dram:wuq_hbm")

        _bo = NCH * 65 * 2 + (64 - NCH * 65 * 2 % 64) % 64
        lnsq_at = BIG.view(_bo, F32, 128, 1024)
        PT = [BIG.view(_bo + 4096, BF16, 128, 1536),
              BIG.view(_bo + 7168, BF16, 128, 1536)]
        ysb = BIG.view(_bo + 10240, F32, 65, 512)
        ydiv = BIG.view(_bo + 12288, F32, 128, 1)
        ytok = BIG.view(_bo + 12352, F32, 128, 64)
        GRP = 3
        ngrp = (NCH + GRP - 1) // GRP

        def att_S(qg, g):
            ab = g % 2
            sb0 = 0 if ab == 0 else 1536
            chunks = list(range(g * GRP, min((g + 1) * GRP, NCH)))
            for ci, ch in enumerate(chunks):
                par = ch % 2
                m = ch // 2
                kk = kTp.k(m * 128, (m + 1) * 128)
                E("tensor", lambda h, par=par, m=m, sb0=sb0, ci=ci, qg=qg:
                  h.matmul(
                      ps(sb0 + ci * 512, 512),
                      kTp.ap[par * 64:par * 64 + 64, m * 128:(m + 1) * 128],
                      qTd.ap[par * 64:par * 64 + 64,
                             qg * 512:(qg + 1) * 512],
                      start=True, stop=True),
                  reads=[kk, qTd.k()], writes=[psk(sb0 + ci * 512, 512)])

        def att_exp(qg, g):
            ab = g % 2
            sb0 = 0 if ab == 0 else 1536
            n = min((g + 1) * GRP, NCH) - g * GRP
            E("scalar", lambda h, sb0=sb0, ab=ab, n=n: h.activation(
                PT[ab].ap[:, 0:n * 512], ps(sb0, n * 512), ACT.Exp,
                scale=1.0 / 32.0),
              reads=[psk(sb0, n * 512)], writes=[PT[ab].k(0, n * 512)])

        def att_PV(qg, g):
            ab = g % 2
            chunks = list(range(g * GRP, min((g + 1) * GRP, NCH)))
            for ci, ch in enumerate(chunks):
                vk = vpack.k(ch * 65, (ch + 1) * 65)
                E("tensor", lambda h, ch=ch, ci=ci, ab=ab: h.matmul(
                    ps(3072, 512)[0:65, :],
                    vpack.ap[:, ch * 65:(ch + 1) * 65],
                    PT[ab].ap[:, ci * 512:(ci + 1) * 512],
                    start=(ch == 0), stop=(ch == NCH - 1)),
                  reads=[vk, PT[ab].k(ci * 512, (ci + 1) * 512)],
                  writes=[psk(3072, 512)])

        for qg in range(T // 512):
            att_S(qg, 0)
            att_exp(qg, 0)
            for g in range(1, ngrp):
                att_S(qg, g)
                att_exp(qg, g)
                att_PV(qg, g - 1)
            att_PV(qg, ngrp - 1)
            E("vector", lambda h: h.tensor_copy(ysb.ap, ps(3072, 512)[0:65, :]),
              reads=[psk(3072, 512)], writes=[ysb.k()])
            for j in range(4):
                ti = qg * 4 + j
                E("tensor", lambda h, j=j: h.transpose(
                    ps(3584, 128)[:, 0:65], ysb.ap[:, j * 128:(j + 1) * 128],
                    co["identf"].ap[0:65, 0:65]),
                  reads=[ysb.k(j * 128, (j + 1) * 128), K("identf")],
                  writes=[psk(3584, 512)])
                E("vector", lambda h: h.reciprocal(
                    ydiv.ap, ps(3584, 128)[:, 64:65]),
                  reads=[psk(3584, 512)], writes=[ydiv.k()])
                E("vector", lambda h: h.tensor_scalar(
                    ytok.ap, ps(3584, 128)[:, 0:64], ydiv.ap[:, 0:1],
                    None, ALU.mult),
                  reads=[psk(3584, 512), ydiv.k()], writes=[ytok.k()])
                E("vector", lambda h, ti=ti: h.tensor_tensor(
                    tmpr.ap.rearrange("p (r c) -> p r c", c=64),
                    stream_t(ti).rearrange("p (r c) -> p r c", c=64),
                    ytok.ap.rearrange("p (o c) -> p o c", o=1).broadcast_to(
                        [128, 16, 64]),
                    ALU.add),
                  reads=[ytok.k(), skey(ti)], writes=[tmpr.k()])
                layer_norm(tmpr, stream_t(ti), [skey(ti)], lnsq_at)
            fpq = NFC // (T // 512)
            for f in range(qg * fpq, (qg + 1) * fpq):
                quant_gu(f)

        xqT = ARN.view(0, BF16, 128, 8 * T)
        xqm5 = AUX.view(0, F32, 128, 1024)
        xqng = AUX.view(4096, F32, 128, 1024)
        u1tile2 = AUX.view(8192, BF16, 128, 1024)
        xqi = BIG.view(0, I32, 128, 1024)
        xd = BIG.view(4096, F32, 128, 1024)
        xt = BIG.view(8192, F32, 128, 1024)
        for i in range(NT):
            E("vector", lambda h, i=i: h.reduce_max(
                co["ln"].ap[:, 0:1], stream_t(i), AX,
                apply_absolute_value=True),
              reads=[skey(i)], writes=[K("ln", 0, 1)])
            E("vector", lambda h: h.tensor_scalar(
                co["ln"].ap[:, 1:2], co["ln"].ap[:, 0:1], 1e-5, None, ALU.max),
              reads=[K("ln", 0, 1)], writes=[K("ln", 1, 2)])
            E("vector", lambda h: h.reciprocal(co["ln"].ap[:, 2:3],
                                               co["ln"].ap[:, 1:2]),
              reads=[K("ln", 1, 2)], writes=[K("ln", 2, 3)])
            E("vector", lambda h: h.tensor_scalar(
                co["rsct"].ap[:, :], co["ln"].ap[:, 2:3], 127.0, None,
                ALU.mult),
              reads=[K("ln", 2, 3)], writes=[K("rsct")])
            E("vector", lambda h: h.tensor_scalar(
                co["nsct"].ap[:, :], co["ln"].ap[:, 1:2], -1.0 / 127.0, None,
                ALU.mult),
              reads=[K("ln", 1, 2)], writes=[K("nsct")])
            E("scalar", lambda h, i=i: h.activation(
                xqm5.ap, stream_t(i), ACT.Identity,
                scale=co["rsct"].ap[:, 0:1]),
              reads=[skey(i), K("rsct")], writes=[xqm5.k()])
            round_neg(xqm5.ap, [xqm5.k()], xqi, xd, xt, xqng.ap, [xqng.k()])
            E("vector", lambda h: h.tensor_scalar(
                u1tile2.ap, xqng.ap, co["nsct"].ap[:, 0:1], None, ALU.mult),
              reads=[xqng.k(), K("nsct")], writes=[u1tile2.k()])
            for c in range(8):
                sl = slice(c * T + i * 128, c * T + i * 128 + 128)
                transpose_128(
                    u1tile2.ap[:, c * 128:(c + 1) * 128], [u1tile2.k()],
                    xqT.ap[:, sl], [xqT.k(sl.start, sl.stop)],
                    c % 4)

        wgqb = [BIG.view(16384, BF16, 128, 1024), BIG.view(18432, BF16, 128, 1024)]
        wuqb = [BIG.view(20480, BF16, 128, 1024), BIG.view(22528, BF16, 128, 1024)]
        TH_W = min(1024, T)
        NTH = T // TH_W
        gate_t = BIG.view(0, F32, 128, TH_W)
        up_t = BIG.view(4096, F32, 128, TH_W)
        gu_t = [BIG.view(8192, F32, 128, TH_W), BIG.view(12288, F32, 128, TH_W)]
        acc_t = BIG.view(24576, F32, 128, T)
        E("vector", lambda h: h.memset(acc_t.ap, 0.0), writes=[acc_t.k()])

        def p1_load(f):
            b = f % 2
            E("sync", lambda h, f=f, b=b: h.dma_start(
                out=wgqb[b].ap, in_=wgq_hbm[f, :, :]),
              reads=[wgqb[b].k(),
                     dk("wgq_hbm", f * 128 * D * 2, (f + 1) * 128 * D * 2)],
              writes=[wgqb[b].k()], dma=True)
            E("sync", lambda h, f=f, b=b: h.dma_start(
                out=wuqb[b].ap, in_=wuq_hbm[f, :, :]),
              reads=[wuqb[b].k(),
                     dk("wuq_hbm", f * 128 * D * 2, (f + 1) * 128 * D * 2)],
              writes=[wuqb[b].k()], dma=True)

        def p1_mms(g):
            f, th = divmod(g, NTH)
            b = f % 2
            for wq_, lo0 in ((wgqb[b], (g % 2) * 1024),
                             (wuqb[b], 2048 + (g % 2) * 1024)):
                for c in range(8):
                    for tp in range(TH_W // 512):
                        lo = lo0 + tp * 512
                        E("tensor", lambda h, wq_=wq_, c=c, tp=tp, lo=lo,
                          th=th: h.matmul(
                              ps(lo, 512), wq_.ap[:, c * 128:(c + 1) * 128],
                              xqT.ap[:, c * T + th * TH_W + tp * 512:
                                     c * T + th * TH_W + tp * 512 + 512],
                              start=(c == 0), stop=(c == 7)),
                          reads=[wq_.k(c * 128, (c + 1) * 128),
                                 xqT.k(c * T + th * TH_W + tp * 512,
                                       c * T + th * TH_W + tp * 512 + 512)],
                          writes=[psk(lo, 512)])

        def p1_evac(g):
            f, th = divmod(g, NTH)
            pg, pu = (g % 2) * 1024, 2048 + (g % 2) * 1024
            gb = g % 2
            E("scalar", lambda h, pg=pg: h.activation(
                gate_t.ap, ps(pg, TH_W), ACT.Silu,
                scale=co["negmg"].ap[:, 0:1]),
              reads=[psk(pg, TH_W), K("negmg")], writes=[gate_t.k()])
            E("scalar", lambda h, pu=pu: h.activation(
                up_t.ap, ps(pu, TH_W), ACT.Identity,
                scale=co["negmu"].ap[:, 0:1]),
              reads=[psk(pu, TH_W), K("negmu")], writes=[up_t.k()])
            E("vector", lambda h, gb=gb: h.tensor_mul(
                gu_t[gb].ap, gate_t.ap, up_t.ap),
              reads=[gate_t.k(), up_t.k()], writes=[gu_t[gb].k()])
            E("scalar", lambda h, gb=gb: h.activation(
                up_t.ap, gu_t[gb].ap, ACT.Abs),
              reads=[gu_t[gb].k()], writes=[up_t.k()])
            E("vector", lambda h, th=th: h.tensor_max(
                acc_t.ap[:, th * TH_W:(th + 1) * TH_W],
                acc_t.ap[:, th * TH_W:(th + 1) * TH_W], up_t.ap),
              reads=[up_t.k(), acc_t.k(th * TH_W, (th + 1) * TH_W)],
              writes=[acc_t.k(th * TH_W, (th + 1) * TH_W)])
            E("scalar", lambda h, f=f, th=th, gb=gb: h.dma_start(
                out=gu_hbm[f, :, th * TH_W:(th + 1) * TH_W], in_=gu_t[gb].ap),
              reads=[gu_t[gb].k()],
              writes=[dk("gu_hbm", f * 128 * T + th * TH_W,
                         f * 128 * T + (th + 1) * TH_W)], dma=True)

        wdstg = [AUX.view(0, F32, 128, 1024), AUX.view(4096, F32, 128, 1024)]
        wqm = BIG.view(49152, F32, 128, 1024)
        wqqi = BIG.view(53248, I32, 128, 1024)
        wqd = BIG.view(57344, F32, 128, 1024)
        wqt = BIG.view(61440, F32, 128, 1024)
        wdqo = [BIG.view(53248, BF16, 128, 1024), BIG.view(55296, BF16, 128, 1024)]

        def wd_load(f):
            b = f % 2
            E("sync", lambda h, f=f, b=b: h.dma_start(
                out=wdstg[b].ap, in_=wdT[f * 128:(f + 1) * 128, :]),
              reads=[wdstg[b].k()], writes=[wdstg[b].k()], dma=True)

        def quant_wd(f):
            b = f % 2
            E("vector", lambda h, b=b: h.tensor_scalar(
                wqm.ap, wdstg[b].ap, co["rmd"].ap[:, 0:1], None, ALU.mult),
              reads=[wdstg[b].k(), K("rmd")], writes=[wqm.k()])
            E("vector", lambda h: h.tensor_copy(wqqi.ap, wqm.ap),
              reads=[wqm.k()], writes=[wqqi.k()])
            E("vector", lambda h: h.tensor_tensor(
                wqd.ap, wqm.ap, wqqi.ap, ALU.subtract),
              reads=[wqm.k(), wqqi.k()], writes=[wqd.k()])
            E("vector", lambda h: h.scalar_tensor_tensor(
                wqt.ap, wqd.ap, 0.5, wqqi.ap, ALU.is_gt, ALU.add),
              reads=[wqd.k(), wqqi.k()], writes=[wqt.k()])
            E("vector", lambda h: h.scalar_tensor_tensor(
                wqm.ap, wqd.ap, -0.5, wqt.ap, ALU.is_lt, ALU.subtract),
              reads=[wqd.k(), wqt.k()], writes=[wqm.k()])
            E("vector", lambda h, b=b: h.tensor_scalar(
                wdqo[b].ap, wqm.ap, -1.0, 1.0, ALU.max, ALU.min),
              reads=[wqm.k()], writes=[wdqo[b].k()])
            E("gpsimd", lambda h, f=f, b=b: h.dma_start(
                out=wdq_hbm[f, :, :], in_=wdqo[b].ap),
              reads=[wdqo[b].k()],
              writes=[dk("wdq_hbm", f * 128 * D * 2, (f + 1) * 128 * D * 2)],
              dma=True, fam="dram:wdq_hbm")

        QTH = min(2, NTH - 1)
        p1_load(0)
        if NFC > 1:
            p1_load(1)
        wd_load(0)
        NG = NFC * NTH
        for g in range(NG):
            f, th = divmod(g, NTH)
            p1_mms(g)
            if g > 0:
                p1_evac(g - 1)
            if th == 0 and f + 1 < NFC:
                wd_load(f + 1)
            if th == QTH:
                quant_wd(f)
            if th == NTH - 1 and f + 2 < NFC:
                p1_load(f + 2)
        p1_evac(NG - 1)
        for b in range(NT):
            E("tensor", lambda h, b=b: h.transpose(
                ps(3584, 128), acc_t.ap[:, b * 128:(b + 1) * 128],
                co["identf"].ap[:, :]),
              reads=[acc_t.k(b * 128, (b + 1) * 128), K("identf")],
              writes=[psk(3584, 512)])
            E("vector", lambda h, b=b: h.reduce_max(
                co["mxtok"].ap[:, b:b + 1], ps(3584, 128), AX),
              reads=[psk(3584, 512)], writes=[K("mxtok", b, b + 1)])
        E("vector", lambda h: h.tensor_scalar(
            co["mxtok"].ap[:, :], co["mxtok"].ap[:, :], 1e-5, None, ALU.max),
          reads=[K("mxtok")], writes=[K("mxtok")])
        E("vector", lambda h: h.tensor_scalar(
            co["sctok"].ap[:, :], co["mxtok"].ap[:, :], 1.0 / 127.0, None,
            ALU.mult),
          reads=[K("mxtok")], writes=[K("sctok")])
        E("vector", lambda h: h.reciprocal(co["rs127"].ap[:, :],
                                           co["mxtok"].ap[:, :]),
          reads=[K("mxtok")], writes=[K("rs127")])
        E("vector", lambda h: h.tensor_scalar(
            co["rs127"].ap[:, :], co["rs127"].ap[:, :], 127.0, None, ALU.mult),
          reads=[K("rs127")], writes=[K("rs127")])
        row1T = BIG.view(45056, F32, 1, T)
        for b in range(NT):
            E("tensor", lambda h, b=b: h.transpose(
                ps(3584, 128)[0:1, :], co["rs127"].ap[:, b:b + 1],
                co["identf"].ap[:, :]),
              reads=[K("rs127", b, b + 1), K("identf")],
              writes=[psk(3584, 512)])
            E("vector", lambda h, b=b: h.tensor_copy(
                row1T.ap[0:1, b * 128:(b + 1) * 128], ps(3584, 128)[0:1, :]),
              reads=[psk(3584, 512)],
              writes=[row1T.k(b * 128, (b + 1) * 128)])
        sc2b = BIG.view(0, F32, 128, T)
        for j in range(T // 512):
            bcast_row(row1T.ap[0:1, j * 512:(j + 1) * 512],
                      [row1T.k(j * 512, (j + 1) * 512)],
                      sc2b.ap[:, j * 512:(j + 1) * 512],
                      [sc2b.k(j * 512, (j + 1) * 512)], 512, bank=6)

        wdq = ARN.view(0, BF16, 128, 32 * D)
        for ch in range(4):
            f0, f1 = ch * (NFC // 4), (ch + 1) * (NFC // 4)
            E("sync", lambda h, f0=f0, f1=f1: h.dma_start(
                out=wdq.ap[:, f0 * D:f1 * D].rearrange(
                    "p (f w) -> p f w", w=D),
                in_=wdq_hbm[f0:f1, :, :].transpose([1, 0, 2])),
              reads=[dk("wdq_hbm", f0 * 128 * D * 2, f1 * 128 * D * 2)],
              writes=[wdq.k(f0 * D, f1 * D)], dma=True, fam="wdql")

        guldg = [AUX.view(8192, F32, 128, 512), AUX.view(10240, F32, 128, 512)]
        lnsq_p2 = BIG.view(16384, F32, 128, 1024)
        tmpr2 = BIG.view(20480, F32, 128, 1024)
        guldg = [AUX.view(0, F32, 128, 1024), AUX.view(4096, F32, 128, 1024)]
        gum = [BIG.view(24576, F32, 128, 1024), BIG.view(28672, F32, 128, 1024)]
        gqi = [BIG.view(32768, I32, 128, 1024), BIG.view(36864, I32, 128, 1024)]
        gd = [BIG.view(40960, F32, 128, 1024), BIG.view(45056, F32, 128, 1024)]
        gt = [BIG.view(49152, F32, 128, 1024), BIG.view(53248, F32, 128, 1024)]
        guqc = [BIG.view(57344, BF16, 128, 1024), BIG.view(59392, BF16, 128, 1024)]
        ostg = BIG.view(61440, F32, 128, 1024)
        NF2 = NFC // 2
        for tt in range(T // 512):

            def g_load(i, tt=tt):
                b = i % 2
                E("sync", lambda h, i=i, tt=tt, b=b: h.dma_start(
                    out=guldg[b].ap.rearrange("p (g w) -> p g w", w=512),
                    in_=gu_hbm[2 * i:2 * i + 2, :,
                               tt * 512:(tt + 1) * 512].transpose([1, 0, 2])),
                  reads=[guldg[b].k(),
                         dk("gu_hbm", 2 * i * 128 * T, (2 * i + 2) * 128 * T)],
                  writes=[guldg[b].k()], dma=True)
                E("vector", lambda h, b=b, tt=tt: h.tensor_tensor(
                    gum[b].ap.rearrange("p (g w) -> p g w", w=512),
                    guldg[b].ap.rearrange("p (g w) -> p g w", w=512),
                    sc2b.ap[:, tt * 512:(tt + 1) * 512].rearrange(
                        "p (o w) -> p o w", o=1).broadcast_to([128, 2, 512]),
                    ALU.mult),
                  reads=[guldg[b].k(), sc2b.k(tt * 512, (tt + 1) * 512)],
                  writes=[gum[b].k()])
                E("scalar", lambda h, b=b: h.copy(gqi[b].ap, gum[b].ap),
                  reads=[gum[b].k()], writes=[gqi[b].k()])

            def g_work(i, tt=tt):
                b = i % 2
                E("vector", lambda h, b=b: h.tensor_tensor(
                    gd[b].ap, gum[b].ap, gqi[b].ap, ALU.subtract),
                  reads=[gum[b].k(), gqi[b].k()], writes=[gd[b].k()])
                E("vector", lambda h, b=b: h.scalar_tensor_tensor(
                    gt[b].ap, gd[b].ap, 0.5, gqi[b].ap, ALU.is_gt, ALU.add),
                  reads=[gd[b].k(), gqi[b].k()], writes=[gt[b].k()])
                E("vector", lambda h, b=b: h.scalar_tensor_tensor(
                    guqc[b].ap, gd[b].ap, -0.5, gt[b].ap,
                    ALU.is_lt, ALU.subtract),
                  reads=[gd[b].k(), gt[b].k()], writes=[guqc[b].k()])
                for ff in range(2):
                    f = 2 * i + ff
                    for sub in range(4):
                        for nb in range(2):
                            lo = sub * 1024 + nb * 512
                            E("tensor", lambda h, f=f, ff=ff, sub=sub, nb=nb,
                              b=b, lo=lo: h.matmul(
                                  ps(lo, 512),
                                  guqc[b].ap[:, ff * 512 + sub * 128:
                                             ff * 512 + sub * 128 + 128],
                                  wdq.ap[:, f * D + nb * 512:
                                         f * D + nb * 512 + 512],
                                  start=(f == 0), stop=(f == NFC - 1)),
                              reads=[guqc[b].k(ff * 512 + sub * 128,
                                               ff * 512 + sub * 128 + 128),
                                     wdq.k(f * D + nb * 512,
                                           f * D + nb * 512 + 512)],
                              writes=[psk(lo, 512)])

            g_load(0)
            for i in range(1, NF2):
                g_load(i)
                g_work(i - 1)
            g_work(NF2 - 1)
            for sub in range(4):
                ti = tt * 4 + sub
                E("vector", lambda h, ti=ti: h.tensor_tensor(
                    co["ct"].ap[:, :], co["sctok"].ap[:, ti:ti + 1],
                    co["md"].ap[:, 0:1], ALU.mult),
                  reads=[K("sctok", ti, ti + 1), K("md")], writes=[K("ct")])
                E("vector", lambda h, sub=sub, ti=ti: h.scalar_tensor_tensor(
                    tmpr2.ap, ps(sub * 1024, 1024), co["ct"].ap[:, 0:1],
                    stream_t(ti), ALU.mult, ALU.add),
                  reads=[psk(sub * 1024, 1024), K("ct"), skey(ti)],
                  writes=[tmpr2.k()])
                layer_norm(tmpr2, ostg.ap, [ostg.k()], lnsq_p2)
                E("sync", lambda h, ti=ti: h.dma_start(
                    out=out_h[ti * 128:(ti + 1) * 128, :], in_=ostg.ap),
                  reads=[ostg.k()],
                  writes=[dk("out", ti * 128 * D, (ti + 1) * 128 * D)],
                  dma=True)

        plan.barrier_on("gpsimd")

        with nc.Block() as block:
            plan.replay(block)

    return nc, plan



def _host_prep(inputs):
    f32 = np.float32
    import ml_dtypes
    bf = ml_dtypes.bfloat16
    g = {}
    g["wqkvT"] = np.ascontiguousarray(np.asarray(inputs["L0_attn_Wqkv"], f32).T)
    g["bqkvp"] = np.ascontiguousarray(
        np.asarray(inputs["L0_attn_bqkv"], f32).reshape(24, 128).T)
    g["wprojT"] = np.ascontiguousarray(np.asarray(inputs["L0_attn_Wproj"], f32).T)
    g["bprojr"] = np.ascontiguousarray(
        np.asarray(inputs["L0_attn_bproj"], f32).reshape(1, D))
    g["w1n"] = np.ascontiguousarray(np.asarray(inputs["L0_ffn_w1"], f32))
    g["b1p"] = np.ascontiguousarray(
        np.asarray(inputs["L0_ffn_b1"], f32).reshape(32, 128).T)
    g["w2n"] = np.ascontiguousarray(np.asarray(inputs["L0_ffn_w2"], f32))
    g["b2p"] = np.ascontiguousarray(
        np.asarray(inputs["L0_ffn_b2"], f32).reshape(8, 128).T)
    g["wqT1"] = np.ascontiguousarray(np.asarray(inputs["L1_attn_Wq"], f32).T)
    g["wkT1"] = np.ascontiguousarray(np.asarray(inputs["L1_attn_Wk"], f32).T)
    g["wvT1"] = np.ascontiguousarray(np.asarray(inputs["L1_attn_Wv"], f32).T)
    g["bqkv1"] = np.ascontiguousarray(np.stack(
        [np.asarray(inputs["L1_attn_bq"], f32),
         np.asarray(inputs["L1_attn_bk"], f32),
         np.asarray(inputs["L1_attn_bv"], f32)], axis=1))
    g["wgblk"] = np.ascontiguousarray(
        np.asarray(inputs["L1_ffn_w_gate"], f32).T.reshape(
            D, 32, 128).transpose(1, 0, 2))
    g["wublk"] = np.ascontiguousarray(
        np.asarray(inputs["L1_ffn_w_up"], f32).T.reshape(
            D, 32, 128).transpose(1, 0, 2))
    g["wdT"] = np.ascontiguousarray(np.asarray(inputs["L1_ffn_w_down"], f32).T)
    g["sc0"] = np.asarray(inputs["L0_attn_scale"], dtype=f32).reshape(1, 1)
    g["identf"] = np.eye(128, dtype=f32)
    g["identb"] = np.eye(128, dtype=bf)
    m16 = np.zeros((128, 128), dtype=bf)
    for c in range(8):
        m16[0:64, c * 16 + 2 * c] = 1
        m16[64:128, c * 16 + 2 * c + 1] = 1
    g["mask16"] = m16
    sel = np.zeros((16, 1024), dtype=bf)
    for c in range(8):
        sel[2 * c, c * 128: c * 128 + 64] = 1
        sel[2 * c + 1, c * 128 + 64: c * 128 + 128] = 1
    g["selb"] = sel
    g["onesb"] = np.ones((1, 128), dtype=bf)
    g["onesf"] = np.ones((1, 128), dtype=f32)
    return g


def kernel(**inputs):
    x = np.asarray(inputs["L0_attn_x"], dtype=np.float32)
    B, T, _ = x.shape
    assert B == NCORE
    g = _host_prep(inputs)
    nc, _ = build_program(T=T, ncores=NCORE)
    in_maps = []
    for c in range(NCORE):
        m = dict(g)
        m["x"] = np.ascontiguousarray(x[c])
        in_maps.append(m)
    res = run_bass_kernel_spmd(nc, in_maps, core_ids=list(range(NCORE)))
    out = np.stack([np.asarray(r["out"]) for r in res.results], axis=0)
    return out.astype(np.float32)


# version 26
# speedup vs baseline: 1.0294x; 1.0140x over previous
import contextlib
import numpy as np

import concourse.bass as bass
import concourse.mybir as mybir
from concourse.bass_utils import run_bass_kernel_spmd

F32 = mybir.dt.float32
I32 = mybir.dt.int32
BF16 = mybir.dt.bfloat16
U8 = mybir.dt.uint8
ALU = mybir.AluOpType
ACT = mybir.ActivationFunctionType
AX = mybir.AxisListType.X

D = 1024
E3 = 3072
FF = 4096
NCORE = 8
HD = 64



class Plan:

    ENGS = ("tensor", "vector", "scalar", "gpsimd", "sync")

    def __init__(self, nc):
        self.nc = nc
        self.recs = {e: [] for e in self.ENGS}
        self.streams = list(self.ENGS) + ["cc"]
        self.cnt = {s: 0 for s in self.streams}
        self.sems = {}
        self.spaces = {}
        self.wm = {e: {} for e in self.ENGS}
        self.n_inst = 0
        self._pool = []

    def make_sems(self, st):
        for s in self.streams:
            self.sems[s] = st.enter_context(self.nc.semaphore("sem_" + s))
        for i in range(70):
            self._pool.append(st.enter_context(self.nc.semaphore(f"dsem{i}")))

    def _fam_sem(self, fam):
        key = "d_" + fam
        if key not in self.sems:
            assert self._pool, "semaphore pool exhausted"
            self.sems[key] = self._pool.pop()
            self.cnt[key] = 0
        return key

    def _wait(self, eng, src, val):
        if src == eng == "tensor":
            return
        if src.startswith("d_") or src == "cc":
            val = self.cnt[src]
        if val <= self.wm[eng].get(src, 0):
            return
        self.wm[eng][src] = val
        self.recs[eng].append(("wait", src, val))

    def _deps_for(self, eng, key, write):
        sp = self.spaces.get(key[0])
        if sp is None:
            return
        lo, hi = key[1], key[2]
        for ent in sp:
            if ent[0] < hi and lo < ent[1]:
                if ent[2] is not None:
                    self._wait(eng, *ent[2])
                if write:
                    for s, c in ent[3].items():
                        self._wait(eng, s, c)

    def _record(self, key, write, tgt, cval):
        sp = self.spaces.setdefault(key[0], [])
        lo, hi = key[1], key[2]
        if write:
            new = []
            for ent in sp:
                if ent[0] < hi and lo < ent[1]:
                    if ent[0] < lo:
                        new.append([ent[0], lo, ent[2], dict(ent[3])])
                    if hi < ent[1]:
                        new.append([hi, ent[1], ent[2], dict(ent[3])])
                else:
                    new.append(ent)
            new.append([lo, hi, (tgt, cval), {}])
            self.spaces[key[0]] = new
        else:
            covered = False
            for ent in sp:
                if ent[0] < hi and lo < ent[1]:
                    ent[3][tgt] = max(ent[3].get(tgt, 0), cval)
                    covered = True
            if not covered:
                sp.append([lo, hi, None, {tgt: cval}])

    def emit(self, eng, fn, reads=(), writes=(), dma=False, cc=False, fam=None):
        if cc:
            tgt, step = "cc", 1
        elif dma:
            if fam is None:
                w0 = writes[0]
                fam = w0[0] if w0[0].startswith("dram:") else f"{w0[0]}:{w0[1]}"
            tgt, step = self._fam_sem(fam), 16
        else:
            tgt, step = eng, 1
        for r in reads:
            self._deps_for(eng, r, False)
        for w in writes:
            self._deps_for(eng, w, True)
        self.cnt[tgt] += step
        cval = self.cnt[tgt]
        self.n_inst += 1
        self.recs[eng].append(("inst", fn, tgt, cval, step))
        for r in reads:
            self._record(r, False, tgt, cval)
        for w in writes:
            self._record(w, True, tgt, cval)

    def raw(self, eng, fn):
        self.recs[eng].append(("raw", fn))

    def barrier_on(self, eng):
        for s in self.streams:
            if self.cnt[s] > 0:
                self._wait(eng, s, self.cnt[s])

    def replay(self, block):
        plan = self
        waited = {}
        for recs in self.recs.values():
            for r in recs:
                if r[0] == "wait":
                    waited.setdefault(r[1], set()).add(r[2])
        marked = {s: sorted(v) for s, v in waited.items()}

        import bisect

        def new_val(src, val):
            if src.startswith("d_") or src == "cc":
                return val
            return bisect.bisect_right(marked[src], val)

        def mk(name):
            def body(h):
                for r in plan.recs[name]:
                    if r[0] == "wait":
                        _, src, val = r
                        h.wait_ge(plan.sems[src], new_val(src, val))
                    elif r[0] == "inst":
                        _, fn, tgt, cval, step = r
                        inst = fn(h)
                        if tgt.startswith("d_") or tgt == "cc" or (
                                tgt in waited and cval in waited[tgt]):
                            inst.then_inc(plan.sems[tgt], step)
                    else:
                        r[1](h)
            return body

        block.tensor(mk("tensor"))
        block.vector(mk("vector"))
        block.scalar(mk("scalar"))
        block.gpsimd(mk("gpsimd"))
        block.sync(mk("sync"))


class Reg:

    __slots__ = ("ap", "space", "off", "isz", "n")

    def __init__(self, ap, space, off_bytes, itemsize, n_free):
        self.ap = ap
        self.space = space
        self.off = off_bytes
        self.isz = itemsize
        self.n = n_free

    def k(self, lo=0, hi=None):
        if hi is None:
            hi = self.n
        return (self.space, self.off + lo * self.isz, self.off + hi * self.isz)


class Arena:
    def __init__(self, handle, space, nbytes):
        self.h = handle
        self.space = space
        self.nbytes = nbytes

    def view(self, off, dtype, parts, free):
        isz = {F32: 4, BF16: 2, U8: 1, I32: 4}[dtype]
        assert off % 4 == 0, off
        assert off + free * isz <= self.nbytes, (self.space, off, free, isz)
        ap = self.h.ap()[0:parts, off:off + free * isz]
        if dtype != U8:
            ap = ap.bitcast(dtype)
        return Reg(ap, self.space, off, isz, free)


def dk(name, lo=0, hi=1 << 60):
    return ("dram:" + name, lo, hi)



def build_program(T=4096, ncores=NCORE):
    nc = bass.Bass("TRN2", target_bir_lowering=False)
    NT = T // 128
    NB = T // 512
    TA = T * ncores
    NCH = TA // 128
    NFC = FF // 128

    def din(name, shape, dtype=F32):
        return nc.dram_tensor(name, list(shape), dtype, kind="ExternalInput").ap()

    x_in = din("x", [T, D])
    wqkvT = din("wqkvT", [D, E3])
    bqkvp_d = din("bqkvp", [128, 24])
    wprojT = din("wprojT", [D, D])
    bprojr = din("bprojr", [1, D])
    w1n = din("w1n", [FF, D])
    b1p_d = din("b1p", [128, 32])
    w2n = din("w2n", [D, FF])
    b2p_d = din("b2p", [128, 8])
    wqT1 = din("wqT1", [D, HD])
    wkT1 = din("wkT1", [D, HD])
    wvT1 = din("wvT1", [D, HD])
    bqkv1_d = din("bqkv1", [HD, 3])
    wgblk = din("wgblk", [NFC, D, 128])
    wublk = din("wublk", [NFC, D, 128])
    wdT = din("wdT", [FF, D])
    sc0 = din("sc0", [1, 1])
    identf_d = din("identf", [128, 128])
    identb_d = din("identb", [128, 128], BF16)
    mask16_d = din("mask16", [128, 128], BF16)
    selb_d = din("selb", [16, 1024], BF16)
    onesb_d = din("onesb", [1, 128], BF16)
    onesf_d = din("onesf", [1, 128])

    out_h = nc.dram_tensor("out", [T, D], F32, kind="ExternalOutput").ap()
    h_hbm = nc.dram_tensor("h_hbm", [NFC, 128, T], F32).ap()
    gu_hbm = nc.dram_tensor("gu_hbm", [NFC, 128, T], F32).ap()
    wgq_hbm = nc.dram_tensor("wgq_hbm", [NFC, 128, D], BF16).ap()
    wuq_hbm = nc.dram_tensor("wuq_hbm", [NFC, 128, D], BF16).ap()
    wdq_hbm = nc.dram_tensor("wdq_hbm", [NFC, 128, D], BF16).ap()
    mxio = nc.dram_tensor("mxio", [2, 1], F32).ap()
    mxout = nc.dram_tensor("mxout", [2, 1], F32).ap()
    bq2s = nc.dram_tensor("bq2s", [128, 8], F32).ap()
    kg_in = nc.dram_tensor("kg_in", [HD, T], BF16).ap()
    kg_out = nc.dram_tensor("kg_out", [HD * ncores, T], BF16).ap()
    vg_in = nc.dram_tensor("vg_in", [T, 65], BF16).ap()
    vg_out = nc.dram_tensor("vg_out", [TA, 65], BF16).ap()

    st = contextlib.ExitStack()
    with st:
        stream_h = st.enter_context(nc.sbuf_tensor("stream", [128, NT * D], BF16))
        bigw_h = st.enter_context(nc.sbuf_tensor("bigw", [128, 65536], U8))
        arn_h = st.enter_context(nc.sbuf_tensor("arn", [128, 65536], U8))
        aux_h = st.enter_context(nc.sbuf_tensor("aux", [128, 12288], U8))
        cst_h = st.enter_context(nc.sbuf_tensor("cst", [128, 3904], U8))
        PS_h = st.enter_context(nc.psum_tensor("PS", [128, 4096], F32))

        ARN = Arena(arn_h, "arn", 65536)
        AUX = Arena(aux_h, "aux", 12288)
        CST = Arena(cst_h, "cst", 3904)
        BIG = Arena(bigw_h, "big", 65536)

        plan = Plan(nc)
        plan.make_sems(st)
        E = plan.emit

        def stream_t(i):
            return stream_h.ap()[:, i * D:(i + 1) * D]

        def skey(i, j=None):
            if j is None:
                j = i + 1
            return ("stream", i * D * 2, j * D * 2)

        def ps(lo, n):
            return PS_h.ap()[:, lo:lo + n]

        def psk(lo, n):
            return ("ps", lo * 4, (lo + n) * 4)

        co = {}
        coff = [0]

        def calloc(name, dtype, free, parts=128, align=4):
            isz = {F32: 4, BF16: 2, I32: 4}[dtype]
            coff[0] = ((coff[0] + align - 1) // align) * align
            co[name] = CST.view(coff[0], dtype, parts, free)
            coff[0] += free * isz

        calloc("identf", F32, 128, align=64)
        calloc("identb", BF16, 128, align=64)
        calloc("onesb", BF16, 128, parts=1, align=64)
        calloc("bqkvp", F32, 24, align=64)
        calloc("sbq", F32, 24)
        calloc("b1p", F32, 32, align=64)
        calloc("b2p", F32, 8, align=64)
        calloc("bq1", F32, 32)
        calloc("bq2p", F32, 8)
        calloc("bqkv1", F32, 3, parts=HD, align=64)
        calloc("sc128", F32, 1)
        calloc("gm", F32, 4)
        calloc("s1", F32, 1)
        calloc("rs1", F32, 1)
        calloc("s2", F32, 1)
        calloc("negs2", F32, 1)
        calloc("rs2", F32, 1)
        calloc("mg", F32, 1)
        calloc("rmg", F32, 1)
        calloc("negmg", F32, 1)
        calloc("mu", F32, 1)
        calloc("rmu", F32, 1)
        calloc("negmu", F32, 1)
        calloc("md", F32, 1)
        calloc("rmd", F32, 1)
        calloc("negmd", F32, 1)
        calloc("macc", F32, 4)
        calloc("carryk", F32, 8)
        calloc("carrykv", F32, 8)
        calloc("swt", F32, 4)
        calloc("ln", F32, 32)
        calloc("rsct", F32, 1)
        calloc("nsct", F32, 1)
        calloc("sctok", F32, NT)
        calloc("ct", F32, 1)
        calloc("half", F32, 1)
        calloc("rqa", I32, 1)
        calloc("onesf", F32, 128, parts=1, align=64)
        calloc("row", F32, 4, parts=4)
        calloc("mxtok", F32, NT)
        calloc("rs127", F32, NT)
        print("CST used:", coff[0]); assert coff[0] <= 3904, coff[0]

        co["bprojb"] = ARN.view(40960, BF16, 1, D)
        co["bq2r"] = ARN.view(0, BF16, 1, D)
        co["mask16"] = BIG.view(56320, BF16, 128, 128)
        co["selb"] = BIG.view(56576, BF16, 16, 1024)

        def K(name, lo=0, hi=None):
            return co[name].k(lo, hi)

        def bcast_row(src_ap, src_keys, dst_ap, dst_keys, n, bank=7):
            pslo = bank * 512
            E("tensor", lambda h: h.matmul(
                ps(pslo, n), co["onesf"].ap[0:1, :], src_ap,
                start=True, stop=True),
              reads=list(src_keys) + [K("onesf")], writes=[psk(pslo, 512)])
            E("vector", lambda h: h.tensor_copy(dst_ap, ps(pslo, n)),
              reads=[psk(pslo, 512)], writes=list(dst_keys))

        def preduce(src_ap, src_keys, dst_ap, dst_keys, n_part, op, bank=7):
            pslo = bank * 512
            E("tensor", lambda h: h.transpose(
                ps(pslo, 128)[0:1, 0:n_part], src_ap,
                co["identf"].ap[0:n_part, 0:n_part]),
              reads=list(src_keys) + [K("identf")], writes=[psk(pslo, 512)])
            E("vector", lambda h: h.tensor_reduce(
                dst_ap, ps(pslo, 128)[0:1, 0:n_part], AX, op),
              reads=[psk(pslo, 512)], writes=list(dst_keys))

        def transpose_128(src_ap, src_keys, dst_ap, dst_keys, bank,
                          ident="identb", evac="scalar", in_parts=128):
            pslo = bank * 512
            if ident == "identb":
                pv = ps(pslo, 64).bitcast(BF16)
            else:
                pv = ps(pslo, 128)
            pk_ = psk(pslo, 512)
            idap = co[ident].ap[0:in_parts, 0:in_parts]
            E("tensor", lambda h: h.transpose(pv[:, 0:in_parts], src_ap, idap),
              reads=list(src_keys) + [K(ident)], writes=[pk_])
            if evac == "vector":
                E("vector", lambda h: h.tensor_copy(dst_ap, pv[:, 0:in_parts]),
                  reads=[pk_], writes=list(dst_keys))
            else:
                E("scalar", lambda h: h.copy(dst_ap, pv[:, 0:in_parts]),
                  reads=[pk_], writes=list(dst_keys))


        def round_neg(m_ap, m_keys, qi, d, t, out_ap, out_keys,
                      eng="vector"):
            E("scalar", lambda h: h.copy(qi.ap, m_ap),
              reads=list(m_keys), writes=[qi.k()])
            E("vector", lambda h: h.tensor_tensor(d.ap, m_ap, qi.ap,
                                                  ALU.subtract),
              reads=list(m_keys) + [qi.k()], writes=[d.k()])
            E("vector", lambda h: h.scalar_tensor_tensor(
                t.ap, d.ap, 0.5, qi.ap, ALU.is_gt, ALU.add),
              reads=[d.k(), qi.k()], writes=[t.k()])
            E("vector", lambda h: h.scalar_tensor_tensor(
                out_ap, d.ap, -0.5, t.ap, ALU.is_lt, ALU.subtract),
              reads=[d.k(), t.k()], writes=list(out_keys))

        for nm, src in (("identf", identf_d), ("identb", identb_d),
                        ("mask16", mask16_d), ("selb", selb_d),
                        ("onesb", onesb_d), ("onesf", onesf_d), ("bqkvp", bqkvp_d),
                        ("b1p", b1p_d), ("b2p", b2p_d), ("bqkv1", bqkv1_d)):
            E("sync", lambda h, nm=nm, src=src: h.dma_start(
                out=co[nm].ap[:, :], in_=src[:, :]),
              writes=[K(nm)], dma=True)
        bpstg = AUX.view(0, F32, 1, D)
        E("sync", lambda h: h.dma_start(out=bpstg.ap, in_=bprojr[:, :]),
          writes=[bpstg.k()], dma=True)
        E("vector", lambda h: h.tensor_copy(co["bprojb"].ap[:, :], bpstg.ap),
          reads=[bpstg.k()], writes=[K("bprojb")])
        scstg = AUX.view(4096, F32, 1, 1)
        E("sync", lambda h: h.dma_start(out=scstg.ap, in_=sc0[:, :]),
          writes=[scstg.k()], dma=True)
        E("vector", lambda h: h.memset(co["half"].ap[:, :], 0.5),
          writes=[K("half")])
        bcast_row(scstg.ap, [scstg.k()], co["sc128"].ap[:, :], [K("sc128")], 1)

        _lnrot = [0]

        def layer_norm(tmp, outap, outkeys, lnsq):
            base = (_lnrot[0] % 4) * 6
            _lnrot[0] += 1
            ln = co["ln"].ap
            lnk = lambda i: K("ln", base + i, base + i + 1)
            c = lambda i: ln[:, base + i:base + i + 1]
            E("vector", lambda h: h.reduce_sum(c(0), tmp.ap, AX),
              reads=[tmp.k()], writes=[lnk(0)])
            E("scalar", lambda h: h.activation(lnsq.ap, tmp.ap, ACT.Square,
                                               accum_out=c(1)),
              reads=[tmp.k()], writes=[lnsq.k(), lnk(1)])
            E("vector", lambda h: h.tensor_scalar(c(2), c(0),
                                                  1.0 / D, None, ALU.mult),
              reads=[lnk(0)], writes=[lnk(2)])
            E("vector", lambda h: h.tensor_tensor(c(3), c(2), c(2), ALU.mult),
              reads=[lnk(2)], writes=[lnk(3)])
            E("vector", lambda h: h.scalar_tensor_tensor(
                c(4), c(1), 1.0 / D, c(3), ALU.mult, ALU.subtract),
              reads=[lnk(1), lnk(3)], writes=[lnk(4)])
            E("vector", lambda h: h.tensor_scalar(c(4), c(4), 1e-5, None,
                                                  ALU.add),
              reads=[lnk(4)], writes=[lnk(4)])
            E("vector", lambda h: h.reciprocal(c(5), c(4)),
              reads=[lnk(4)], writes=[lnk(5)])
            E("scalar", lambda h: h.activation(c(5), c(5), ACT.Sqrt),
              reads=[lnk(5)], writes=[lnk(5)])
            E("vector", lambda h: h.tensor_scalar(
                outap, tmp.ap, c(2), c(5), ALU.subtract, ALU.mult),
              reads=[tmp.k(), lnk(2), lnk(5)], writes=list(outkeys))

        xstg = [AUX.view(0, F32, 128, 1024), AUX.view(4096, F32, 128, 1024)]
        for i in range(NT):
            b = i % 2
            E("sync", lambda h, i=i, b=b: h.dma_start(
                out=xstg[b].ap, in_=x_in[i * 128:(i + 1) * 128, :]),
              reads=[xstg[b].k()], writes=[xstg[b].k()], dma=True)
            E("vector", lambda h, i=i, b=b: h.tensor_copy(
                stream_t(i), xstg[b].ap),
              reads=[xstg[b].k()], writes=[skey(i)])

        wq_bf = BIG.view(0, BF16, 128, 8 * E3)
        wp_bf = ARN.view(49152, BF16, 128, 8 * D)
        wstg = [ARN.view(32768, F32, 128, 1536), ARN.view(43008, F32, 128, 1536)]
        for c in range(8):
            for hf in range(2):
                b = (c * 2 + hf) % 2
                E("scalar", lambda h, c=c, hf=hf, b=b: h.dma_start(
                    out=wstg[b].ap,
                    in_=wqkvT[c * 128:(c + 1) * 128,
                              hf * 1536:(hf + 1) * 1536]),
                  reads=[wstg[b].k()], writes=[wstg[b].k()], dma=True)
                E("vector", lambda h, c=c, hf=hf, b=b: h.tensor_copy(
                    wq_bf.ap[:, c * E3 + hf * 1536: c * E3 + (hf + 1) * 1536],
                    wstg[b].ap),
                  reads=[wstg[b].k()],
                  writes=[wq_bf.k(c * E3 + hf * 1536, c * E3 + (hf + 1) * 1536)])
        wpstg = ARN.view(32768, F32, 128, D)
        for c in range(8):
            E("scalar", lambda h, c=c: h.dma_start(
                out=wpstg.ap, in_=wprojT[c * 128:(c + 1) * 128, :]),
              reads=[wpstg.k()], writes=[wpstg.k()], dma=True)
            E("vector", lambda h, c=c: h.tensor_copy(
                wp_bf.ap[:, c * D:(c + 1) * D], wpstg.ap),
              reads=[wpstg.k()], writes=[wp_bf.k(c * D, (c + 1) * D)])
        E("vector", lambda h: h.tensor_scalar(
            co["sbq"].ap[:, :], co["bqkvp"].ap[:, :], co["sc128"].ap[:, 0:1],
            None, ALU.mult),
          reads=[K("bqkvp"), K("sc128")], writes=[K("sbq")])

        xT = ARN.view(0, BF16, 128, 4096)
        qb = ARN.view(8192, BF16, 128, 4096)
        kb = ARN.view(16384, BF16, 128, 4096)
        vb = ARN.view(24576, BF16, 128, 4096)
        kvb = ARN.view(32768, BF16, 128, 4096)
        kcs = ARN.view(0, BF16, 128, 4096)
        kvcs = ARN.view(24576, BF16, 128, 4096)
        yb = kb
        zer = AUX.view(4096, BF16, 128, 512)
        ytmp = AUX.view(5120, F32, 128, 512)
        el1 = [AUX.view(7168, F32, 128, 512), AUX.view(7168, F32, 128, 512)]
        el2 = [AUX.view(9216, BF16, 128, 512), AUX.view(9216, BF16, 128, 512)]
        el3 = [AUX.view(10240, BF16, 128, 512), AUX.view(10240, BF16, 128, 512)]
        den = BIG.view(53248, F32, 16, 512)
        denb = BIG.view(55296, BF16, 16, 512)
        tmpr = AUX.view(0, F32, 128, 1024)
        lnsq_a = BIG.view(49152, F32, 128, 1024)

        E("vector", lambda h: h.memset(zer.ap, 0.0), writes=[zer.k()])
        E("vector", lambda h: h.memset(co["carryk"].ap[:, :], 0.0),
          writes=[K("carryk")])
        E("vector", lambda h: h.memset(co["carrykv"].ap[:, :], 0.0),
          writes=[K("carrykv")])

        for blk in range(NB):
            tiles = [blk * 4 + j for j in range(4)]
            for j, ti in enumerate(tiles):
                for c in range(8):
                    sl = slice(c * 512 + j * 128, c * 512 + j * 128 + 128)
                    transpose_128(
                        stream_t(ti)[:, c * 128:(c + 1) * 128], [skey(ti)],
                        xT.ap[:, sl], [xT.k(sl.start, sl.stop)],
                        2 if (j * 8 + c) % 2 == 0 else 7)
            for e in range(24):
                pb = (e % 2) * 512
                for c in range(8):
                    wsl = wq_bf.k(c * E3 + e * 128, c * E3 + e * 128 + 128)
                    E("tensor", lambda h, e=e, c=c, pb=pb: h.matmul(
                        ps(pb, 512),
                        wq_bf.ap[:, c * E3 + e * 128: c * E3 + e * 128 + 128],
                        xT.ap[:, c * 512:(c + 1) * 512],
                        start=(c == 0), stop=(c == 7)),
                      reads=[wsl, xT.k(c * 512, (c + 1) * 512)],
                      writes=[psk(pb, 512)])
                cc_ = e % 8
                sl = slice(cc_ * 512, (cc_ + 1) * 512)
                if e >= 16:
                    E("scalar", lambda h, sl=sl, pb=pb, e=e: h.activation(
                        vb.ap[:, sl], ps(pb, 512), ACT.Identity,
                        bias=co["bqkvp"].ap[:, e:e + 1]),
                      reads=[psk(pb, 512), K("bqkvp")],
                      writes=[vb.k(sl.start, sl.stop)])
                else:
                    eb = e % 2
                    dst = qb if e < 8 else kb
                    E("vector", lambda h, pb=pb, e=e, eb=eb: h.scalar_tensor_tensor(
                        el1[eb].ap, ps(pb, 512), co["sc128"].ap[:, 0:1],
                        co["sbq"].ap[:, e:e + 1].broadcast_to([128, 512]),
                        ALU.mult, ALU.add),
                      reads=[psk(pb, 512), K("sc128"), K("sbq")],
                      writes=[el1[eb].k()])
                    E("vector", lambda h, eb=eb: h.tensor_scalar(
                        el2[eb].ap, el1[eb].ap, 0.0, None, ALU.min),
                      reads=[el1[eb].k()], writes=[el2[eb].k()])
                    E("scalar", lambda h, eb=eb: h.activation(
                        el2[eb].ap, el2[eb].ap, ACT.Exp),
                      reads=[el2[eb].k()], writes=[el2[eb].k()])
                    E("scalar", lambda h, eb=eb: h.activation(
                        el3[eb].ap, el1[eb].ap, ACT.Relu),
                      reads=[el1[eb].k()], writes=[el3[eb].k()])
                    E("vector", lambda h, dst=dst, sl=sl, eb=eb: h.tensor_add(
                        dst.ap[:, sl], el2[eb].ap, el3[eb].ap),
                      reads=[el2[eb].k(), el3[eb].k()],
                      writes=[dst.k(sl.start, sl.stop)])
            for c in range(8):
                sl = slice(c * 512, (c + 1) * 512)
                E("vector", lambda h, sl=sl: h.tensor_mul(
                    kvb.ap[:, sl], kb.ap[:, sl], vb.ap[:, sl]),
                  reads=[kb.k(sl.start, sl.stop), vb.k(sl.start, sl.stop)],
                  writes=[kvb.k(sl.start, sl.stop)])
            for c in range(8):
                sl = slice(c * 512, (c + 1) * 512)
                E("vector", lambda h, c=c, sl=sl: h.tensor_tensor_scan(
                    kcs.ap[:, sl], kb.ap[:, sl], zer.ap,
                    co["carryk"].ap[:, c:c + 1], ALU.add, ALU.add),
                  reads=[kb.k(sl.start, sl.stop), zer.k(), K("carryk", c, c + 1)],
                  writes=[kcs.k(sl.start, sl.stop)])
                E("vector", lambda h, c=c, sl=sl: h.tensor_copy(
                    co["carryk"].ap[:, c:c + 1], kcs.ap[:, sl.stop - 1:sl.stop]),
                  reads=[kcs.k(sl.stop - 1, sl.stop)],
                  writes=[K("carryk", c, c + 1)])
            for c in range(8):
                sl = slice(c * 512, (c + 1) * 512)
                E("vector", lambda h, c=c, sl=sl: h.tensor_tensor_scan(
                    kvcs.ap[:, sl], kvb.ap[:, sl], zer.ap,
                    co["carrykv"].ap[:, c:c + 1], ALU.add, ALU.add),
                  reads=[kvb.k(sl.start, sl.stop), zer.k(),
                         K("carrykv", c, c + 1)],
                  writes=[kvcs.k(sl.start, sl.stop)])
                E("vector", lambda h, c=c, sl=sl: h.tensor_copy(
                    co["carrykv"].ap[:, c:c + 1], kvcs.ap[:, sl.stop - 1:sl.stop]),
                  reads=[kvcs.k(sl.stop - 1, sl.stop)],
                  writes=[K("carrykv", c, c + 1)])
            prodb = kvb
            for c in range(8):
                sl = slice(c * 512, (c + 1) * 512)
                E("vector", lambda h, sl=sl: h.tensor_mul(
                    prodb.ap[:, sl], qb.ap[:, sl], kcs.ap[:, sl]),
                  reads=[qb.k(sl.start, sl.stop), kcs.k(sl.start, sl.stop)],
                  writes=[prodb.k(sl.start, sl.stop)])
            for c in range(8):
                E("tensor", lambda h, c=c: h.matmul(
                    ps(1536, 512)[0:16, :],
                    co["mask16"].ap[:, c * 16:(c + 1) * 16],
                    prodb.ap[:, c * 512:(c + 1) * 512],
                    start=(c == 0), stop=(c == 7)),
                  reads=[K("mask16"), prodb.k(c * 512, (c + 1) * 512)],
                  writes=[psk(1536, 512)])
            E("vector", lambda h: h.tensor_scalar_max(
                den.ap, ps(1536, 512)[0:16, :], 1e-3),
              reads=[psk(1536, 512)], writes=[den.k()])
            E("vector", lambda h: h.reciprocal(den.ap, den.ap),
              reads=[den.k()], writes=[den.k()])
            E("vector", lambda h: h.tensor_copy(denb.ap, den.ap),
              reads=[den.k()], writes=[denb.k()])
            for c in range(8):
                sl = slice(c * 512, (c + 1) * 512)
                E("tensor", lambda h, c=c: h.matmul(
                    ps(2048, 512), co["selb"].ap[:, c * 128:(c + 1) * 128],
                    denb.ap, start=True, stop=True),
                  reads=[K("selb"), denb.k()], writes=[psk(2048, 512)])
                E("vector", lambda h, sl=sl: h.tensor_mul(
                    ytmp.ap, qb.ap[:, sl], kvcs.ap[:, sl]),
                  reads=[qb.k(sl.start, sl.stop), kvcs.k(sl.start, sl.stop)],
                  writes=[ytmp.k()])
                E("vector", lambda h, sl=sl: h.tensor_mul(
                    yb.ap[:, sl], ytmp.ap, ps(2048, 512)),
                  reads=[ytmp.k(), psk(2048, 512)],
                  writes=[yb.k(sl.start, sl.stop)])
            for j, ti in enumerate(tiles):
                plo = 2560 if j % 2 == 0 else 0
                pv, pks = ps(plo, 1024), psk(plo, 1024)
                for nb in range(2):
                    E("tensor", lambda h, pv=pv, nb=nb: h.matmul(
                        pv[:, nb * 512:(nb + 1) * 512], co["onesb"].ap[:, :],
                        co["bprojb"].ap[:, nb * 512:(nb + 1) * 512],
                        start=True, stop=False),
                      reads=[K("onesb"), K("bprojb")], writes=[pks])
                for c in range(8):
                    ysl = yb.k(c * 512 + j * 128, c * 512 + j * 128 + 128)
                    for nb in range(2):
                        E("tensor", lambda h, c=c, nb=nb, pv=pv, j=j: h.matmul(
                            pv[:, nb * 512:(nb + 1) * 512],
                            yb.ap[:, c * 512 + j * 128: c * 512 + j * 128 + 128],
                            wp_bf.ap[:, c * D + nb * 512: c * D + nb * 512 + 512],
                            start=False, stop=(c == 7)),
                          reads=[ysl, wp_bf.k(c * D + nb * 512,
                                              c * D + nb * 512 + 512)],
                          writes=[pks])
                E("vector", lambda h, pv=pv, ti=ti: h.tensor_add(
                    tmpr.ap, pv, stream_t(ti)),
                  reads=[pks, skey(ti)], writes=[tmpr.k()])
                layer_norm(tmpr, stream_t(ti), [skey(ti)], lnsq_a)

        mx = co["gm"].ap
        mxk = lambda i: K("gm", i, i + 1)
        for i in range(NT):
            if i == 0:
                E("vector", lambda h, i=i: h.reduce_max(
                    mx[:, 0:1], stream_t(i), AX),
                  reads=[skey(i)], writes=[mxk(0)])
            else:
                E("vector", lambda h, i=i: h.reduce_max(
                    mx[:, 1:2], stream_t(i), AX),
                  reads=[skey(i)], writes=[mxk(1)])
                E("vector", lambda h: h.tensor_max(mx[:, 0:1], mx[:, 0:1],
                                                   mx[:, 1:2]),
                  reads=[mxk(0), mxk(1)], writes=[mxk(0)])

        def global_scale(idx, skey_s, skey_rs):
            preduce(mx[:, 0:1], [mxk(0)], mx[0:1, 2:3], [mxk(2)], 128, ALU.max)
            E("gpsimd", lambda h, idx=idx: h.dma_start(
                out=mxio[idx:idx + 1, :], in_=mx[0:1, 2:3]),
              reads=[mxk(2)], writes=[dk("mxio", idx, idx + 1)], dma=True)
            E("gpsimd", lambda h, idx=idx: h.collective_compute(
                "AllReduce", ALU.max, replica_groups=[list(range(ncores))],
                ins=[mxio[idx:idx + 1, :].opt()],
                outs=[mxout[idx:idx + 1, :].opt()]),
              reads=[dk("mxio", idx, idx + 1)],
              writes=[dk("mxout", idx, idx + 1)], cc=True)
            E("gpsimd", lambda h, idx=idx: h.dma_start(
                out=mx[0:1, 3:4], in_=mxout[idx:idx + 1, :]),
              reads=[dk("mxout", idx, idx + 1)], writes=[mxk(3)], dma=True)
            bcast_row(mx[0:1, 3:4], [mxk(3)], co[skey_s].ap[:, :],
                      [K(skey_s)], 1)
            E("vector", lambda h: h.tensor_scalar(
                co[skey_s].ap[:, :], co[skey_s].ap[:, :], 1e-8, 1.0 / 255.0,
                ALU.max, ALU.mult),
              reads=[K(skey_s)], writes=[K(skey_s)])
            E("vector", lambda h: h.reciprocal(co[skey_rs].ap[:, :],
                                               co[skey_s].ap[:, :]),
              reads=[K(skey_s)], writes=[K(skey_rs)])

        global_scale(0, "s1", "rs1")

        u1T = ARN.view(0, BF16, 128, 8 * T)
        qm5 = BIG.view(32768, F32, 128, 1024)
        qng = BIG.view(36864, F32, 128, 1024)
        u1tile = BIG.view(40960, BF16, 128, 1024)
        rqiA = BIG.view(45056, I32, 128, 1024)
        rdA = BIG.view(49152, F32, 128, 1024)
        rtA = BIG.view(53248, F32, 128, 1024)
        u1tl = [u1tile, BIG.view(57344, BF16, 128, 1024)]

        def u1_quant(i):
            b = i % 2
            E("scalar", lambda h, i=i: h.activation(
                qm5.ap, stream_t(i), ACT.Identity,
                scale=co["rs1"].ap[:, 0:1]),
              reads=[skey(i), K("rs1")], writes=[qm5.k()])
            round_neg(qm5.ap, [qm5.k()], rqiA, rdA, rtA, qng.ap, [qng.k()])
            E("scalar", lambda h, b=b: h.activation(u1tl[b].ap, qng.ap,
                                                    ACT.Relu, scale=-1.0),
              reads=[qng.k()], writes=[u1tl[b].k()])

        def u1_tr(i):
            b = i % 2
            for c in range(8):
                sl = slice(c * T + i * 128, c * T + i * 128 + 128)
                transpose_128(
                    u1tl[b].ap[:, c * 128:(c + 1) * 128], [u1tl[b].k()],
                    u1T.ap[:, sl], [u1T.k(sl.start, sl.stop)],
                    c % 4)

        u1_quant(0)
        for i in range(1, NT):
            u1_quant(i)
            u1_tr(i - 1)
        u1_tr(NT - 1)
        w1stg = [AUX.view(0, F32, 128, 1024), AUX.view(4096, F32, 128, 1024)]
        w1qn = AUX.view(8192, BF16, 128, 1024)
        w1qT = [AUX.view(10240, BF16, 128, 1024), BIG.view(43008, BF16, 128, 1024)]
        lnsq_b = ARN.view(49152, F32, 128, 1024)
        hstg = [BIG.view(0, F32, 128, T), BIG.view(4 * T, F32, 128, T)]
        swt = co["swt"].ap
        swk = lambda i: K("swt", i, i + 1)
        NTP = T // 512
        E("vector", lambda h: h.memset(mx[:, 0:1], 0.0), writes=[mxk(0)])

        def f1_load_quant(f):
            b = f % 2
            E("sync", lambda h, f=f, b=b: h.dma_start(
                out=w1stg[b].ap, in_=w1n[f * 128:(f + 1) * 128, :]),
              reads=[w1stg[b].k()], writes=[w1stg[b].k()], dma=True)
            E("vector", lambda h, b=b: h.reduce_max(
                swt[:, 0:1], w1stg[b].ap, AX, apply_absolute_value=True),
              reads=[w1stg[b].k()], writes=[swk(0)])
            E("vector", lambda h: h.tensor_scalar(
                swt[:, 1:2], swt[:, 0:1], 1e-8, 1.0 / 127.0, ALU.max, ALU.mult),
              reads=[swk(0)], writes=[swk(1)])
            E("vector", lambda h: h.reciprocal(swt[:, 2:3], swt[:, 1:2]),
              reads=[swk(1)], writes=[swk(2)])
            E("vector", lambda h: h.tensor_tensor(
                swt[:, 3:4], swt[:, 1:2], co["s1"].ap[:, 0:1], ALU.mult),
              reads=[swk(1), K("s1")], writes=[swk(3)])
            E("vector", lambda h: h.tensor_scalar(
                swt[:, 3:4], swt[:, 3:4], -1.0, None, ALU.mult),
              reads=[swk(3)], writes=[swk(3)])
            E("scalar", lambda h, b=b: h.activation(
                qm5.ap, w1stg[b].ap, ACT.Identity, scale=swt[:, 2:3]),
              reads=[w1stg[b].k(), swk(2)], writes=[qm5.k()])
            round_neg(qm5.ap, [qm5.k()], rqiA, rdA, rtA, qng.ap, [qng.k()])
            E("vector", lambda h: h.tensor_scalar(
                w1qn.ap, qng.ap, swt[:, 3:4], None, ALU.mult),
              reads=[qng.k(), swk(3)], writes=[w1qn.k()])
            bqa = co["bq1"].ap
            bk_ = K("bq1", f, f + 1)
            E("vector", lambda h, f=f: h.tensor_tensor(
                bqa[:, f:f + 1], co["b1p"].ap[:, f:f + 1],
                co["rs1"].ap[:, 0:1], ALU.mult),
              reads=[K("b1p"), K("rs1")], writes=[bk_])
            E("vector", lambda h, f=f: h.tensor_tensor(
                bqa[:, f:f + 1], bqa[:, f:f + 1], swt[:, 2:3], ALU.mult),
              reads=[bk_, swk(2)], writes=[bk_])
            E("vector", lambda h, f=f: h.tensor_copy(
                co["rqa"].ap[:, 0:1], bqa[:, f:f + 1]),
              reads=[bk_], writes=[K("rqa")])
            E("vector", lambda h, f=f: h.tensor_tensor(
                co["ln"].ap[:, 24:25], bqa[:, f:f + 1], co["rqa"].ap[:, 0:1],
                ALU.subtract),
              reads=[bk_, K("rqa")], writes=[K("ln", 24, 25)])
            E("vector", lambda h: h.scalar_tensor_tensor(
                co["ln"].ap[:, 25:26], co["ln"].ap[:, 24:25], 0.5,
                co["rqa"].ap[:, 0:1], ALU.is_gt, ALU.add),
              reads=[K("ln", 24, 25), K("rqa")], writes=[K("ln", 25, 26)])
            E("vector", lambda h, f=f: h.scalar_tensor_tensor(
                bqa[:, f:f + 1], co["ln"].ap[:, 24:25], -0.5,
                co["ln"].ap[:, 25:26], ALU.is_lt, ALU.subtract),
              reads=[K("ln", 24, 25), K("ln", 25, 26)], writes=[bk_])
            E("vector", lambda h, f=f: h.tensor_scalar(
                bqa[:, f:f + 1], bqa[:, f:f + 1], -1.0, None, ALU.mult),
              reads=[bk_], writes=[bk_])
            E("vector", lambda h, f=f: h.tensor_tensor(
                bqa[:, f:f + 1], bqa[:, f:f + 1], swt[:, 1:2], ALU.mult),
              reads=[bk_, swk(1)], writes=[bk_])
            E("vector", lambda h, f=f: h.tensor_tensor(
                bqa[:, f:f + 1], bqa[:, f:f + 1], co["s1"].ap[:, 0:1], ALU.mult),
              reads=[bk_, K("s1")], writes=[bk_])
            for c in range(8):
                transpose_128(
                    w1qn.ap[:, c * 128:(c + 1) * 128], [w1qn.k()],
                    w1qT[b].ap[:, c * 128:(c + 1) * 128],
                    [w1qT[b].k(c * 128, (c + 1) * 128)],
                    c % 2)

        def f1_mms(f):
            b = f % 2
            hb = f % 2
            for c in range(8):
                wk_ = w1qT[b].k(c * 128, (c + 1) * 128)
                for tp in range(NTP):
                    E("tensor", lambda h, c=c, tp=tp, b=b: h.matmul(
                        ps(tp * 512, 512),
                        w1qT[b].ap[:, c * 128:(c + 1) * 128],
                        u1T.ap[:, c * T + tp * 512: c * T + tp * 512 + 512],
                        start=(c == 0), stop=(c == 7)),
                      reads=[wk_, u1T.k(c * T + tp * 512, c * T + tp * 512 + 512)],
                      writes=[psk(tp * 512, 512)])
            for tp in range(NTP):
                E("scalar", lambda h, tp=tp, hb=hb, f=f: h.activation(
                    hstg[hb].ap[:, tp * 512:(tp + 1) * 512], ps(tp * 512, 512),
                    ACT.Relu, bias=co["bq1"].ap[:, f:f + 1]),
                  reads=[psk(tp * 512, 512), K("bq1", f, f + 1)],
                  writes=[hstg[hb].k(tp * 512, (tp + 1) * 512)])

        def f1_spill(f):
            hb = f % 2
            E("vector", lambda h, hb=hb: h.reduce_max(
                mx[:, 1:2], hstg[hb].ap, AX),
              reads=[hstg[hb].k()], writes=[mxk(1)])
            E("vector", lambda h: h.tensor_max(mx[:, 0:1], mx[:, 0:1],
                                               mx[:, 1:2]),
              reads=[mxk(0), mxk(1)], writes=[mxk(0)])
            E("scalar", lambda h, f=f, hb=hb: h.dma_start(
                out=h_hbm[f, :, :], in_=hstg[hb].ap),
              reads=[hstg[hb].k()],
              writes=[dk("h_hbm", f * 128 * T, (f + 1) * 128 * T)], dma=True)

        f1_load_quant(0)
        f1_load_quant(1)
        f1_mms(0)
        for f in range(2, NFC):
            f1_load_quant(f)
            f1_mms(f - 1)
            f1_spill(f - 2)
        f1_mms(NFC - 1)
        f1_spill(NFC - 2)
        f1_spill(NFC - 1)

        global_scale(1, "s2", "rs2")
        E("vector", lambda h: h.tensor_scalar(
            co["negs2"].ap[:, :], co["s2"].ap[:, :], -1.0, None, ALU.mult),
          reads=[K("s2")], writes=[K("negs2")])

        w2qT = BIG.view(0, BF16, 128, 32 * D)
        w2stg = [AUX.view(0, F32, 128, 512), AUX.view(2048, F32, 128, 512)]
        w2m = [AUX.view(4096, F32, 128, 512), AUX.view(6144, F32, 128, 512)]
        w2qi = AUX.view(8192, I32, 128, 512)
        w2d = AUX.view(10240, F32, 128, 512)
        w2t = ARN.view(12288, F32, 128, 512)
        w2nq = ARN.view(14336, F32, 128, 512)
        w2qn = [ARN.view(16384, BF16, 128, 512), ARN.view(17408, BF16, 128, 512)]
        sw2p = co["bq2p"].ap
        for dch in range(8):
            for hf in range(8):
                b = (dch * 8 + hf) % 2
                E("sync", lambda h, dch=dch, hf=hf, b=b: h.dma_start(
                    out=w2stg[b].ap,
                    in_=w2n[dch * 128:(dch + 1) * 128,
                            hf * 512:(hf + 1) * 512]),
                  reads=[w2stg[b].k()], writes=[w2stg[b].k()], dma=True)
                if hf == 0:
                    E("vector", lambda h, b=b: h.reduce_max(
                        swt[:, 0:1], w2stg[b].ap, AX,
                        apply_absolute_value=True),
                      reads=[w2stg[b].k()], writes=[swk(0)])
                else:
                    E("vector", lambda h, b=b: h.reduce_max(
                        swt[:, 1:2], w2stg[b].ap, AX,
                        apply_absolute_value=True),
                      reads=[w2stg[b].k()], writes=[swk(1)])
                    E("vector", lambda h: h.tensor_max(
                        swt[:, 0:1], swt[:, 0:1], swt[:, 1:2]),
                      reads=[swk(0), swk(1)], writes=[swk(0)])
            E("vector", lambda h, dch=dch: h.tensor_scalar(
                sw2p[:, dch:dch + 1], swt[:, 0:1], 1e-8, 1.0 / 127.0,
                ALU.max, ALU.mult),
              reads=[swk(0)], writes=[K("bq2p", dch, dch + 1)])
        for dch in range(8):
            E("vector", lambda h, dch=dch: h.reciprocal(
                co["ln"].ap[:, 16 + dch:17 + dch], sw2p[:, dch:dch + 1]),
              reads=[K("bq2p", dch, dch + 1)],
              writes=[K("ln", 16 + dch, 17 + dch)])
            E("vector", lambda h, dch=dch: h.tensor_scalar(
                co["ln"].ap[:, 24 + dch:25 + dch],
                sw2p[:, dch:dch + 1], -1.0, None, ALU.mult),
              reads=[K("bq2p", dch, dch + 1)],
              writes=[K("ln", 24 + dch, 25 + dch)])

        def w2_lm(j):
            dch, hf = divmod(j, 8)
            b = j % 2
            E("sync", lambda h, dch=dch, hf=hf, b=b: h.dma_start(
                out=w2stg[b].ap,
                in_=w2n[dch * 128:(dch + 1) * 128, hf * 512:(hf + 1) * 512]),
              reads=[w2stg[b].k()], writes=[w2stg[b].k()], dma=True)
            E("scalar", lambda h, b=b, dch=dch: h.activation(
                w2m[b].ap, w2stg[b].ap, ACT.Identity,
                scale=co["ln"].ap[:, 16 + dch:17 + dch]),
              reads=[w2stg[b].k(), K("ln", 16 + dch, 17 + dch)],
              writes=[w2m[b].k()])

        def w2_qt(j):
            dch, hf = divmod(j, 8)
            b = j % 2
            E("scalar", lambda h, b=b: h.copy(w2qi.ap, w2m[b].ap),
              reads=[w2m[b].k()], writes=[w2qi.k()])
            E("vector", lambda h, b=b: h.tensor_tensor(
                w2d.ap, w2m[b].ap, w2qi.ap, ALU.subtract),
              reads=[w2m[b].k(), w2qi.k()], writes=[w2d.k()])
            E("vector", lambda h, b=b: h.scalar_tensor_tensor(
                w2t.ap, w2d.ap, 0.5, w2qi.ap, ALU.is_gt, ALU.add),
              reads=[w2d.k(), w2qi.k()], writes=[w2t.k()])
            E("vector", lambda h, b=b: h.scalar_tensor_tensor(
                w2nq.ap, w2d.ap, -0.5, w2t.ap, ALU.is_lt, ALU.subtract),
              reads=[w2d.k(), w2t.k()], writes=[w2nq.k()])
            E("vector", lambda h, b=b, dch=dch: h.tensor_scalar(
                w2qn[b].ap, w2nq.ap, co["ln"].ap[:, 24 + dch:25 + dch],
                None, ALU.mult),
              reads=[w2nq.k(), K("ln", 24 + dch, 25 + dch)],
              writes=[w2qn[b].k()])
            for fc in range(4):
                fa = hf * 4 + fc
                sl = slice(fa * D + dch * 128, fa * D + dch * 128 + 128)
                transpose_128(
                    w2qn[b].ap[:, fc * 128:(fc + 1) * 128], [w2qn[b].k()],
                    w2qT.ap[:, sl], [w2qT.k(sl.start, sl.stop)],
                    6 + (fc % 2))

        w2_lm(0)
        for j in range(1, 64):
            w2_lm(j)
            w2_qt(j - 1)
        w2_qt(63)

        bt = co["ln"].ap
        btk = lambda i: K("ln", i, i + 1)
        for dch in range(8):
            E("vector", lambda h, dch=dch: h.reciprocal(
                bt[:, 0:1], sw2p[:, dch:dch + 1]),
              reads=[K("bq2p", dch, dch + 1)], writes=[btk(0)])
            E("vector", lambda h, dch=dch: h.tensor_tensor(
                bt[:, 1:2], co["b2p"].ap[:, dch:dch + 1], bt[:, 0:1], ALU.mult),
              reads=[K("b2p"), btk(0)], writes=[btk(1)])
            E("vector", lambda h: h.tensor_tensor(
                bt[:, 1:2], bt[:, 1:2], co["rs2"].ap[:, 0:1], ALU.mult),
              reads=[btk(1), K("rs2")], writes=[btk(1)])
            E("vector", lambda h: h.tensor_copy(co["rqa"].ap[:, 0:1],
                                                bt[:, 1:2]),
              reads=[btk(1)], writes=[K("rqa")])
            E("vector", lambda h: h.tensor_tensor(
                bt[:, 2:3], bt[:, 1:2], co["rqa"].ap[:, 0:1], ALU.subtract),
              reads=[btk(1), K("rqa")], writes=[btk(2)])
            E("vector", lambda h: h.scalar_tensor_tensor(
                bt[:, 3:4], bt[:, 2:3], 0.5, co["rqa"].ap[:, 0:1],
                ALU.is_gt, ALU.add),
              reads=[btk(2), K("rqa")], writes=[btk(3)])
            E("vector", lambda h: h.scalar_tensor_tensor(
                bt[:, 1:2], bt[:, 2:3], -0.5, bt[:, 3:4],
                ALU.is_lt, ALU.subtract),
              reads=[btk(2), btk(3)], writes=[btk(1)])
            E("vector", lambda h, dch=dch: h.tensor_tensor(
                co["bq2p"].ap[:, dch:dch + 1], bt[:, 1:2],
                sw2p[:, dch:dch + 1], ALU.mult),
              reads=[btk(1), K("bq2p", dch, dch + 1)],
              writes=[K("bq2p", dch, dch + 1)])
        bq2f = AUX.view(4096, F32, 1, D)
        E("sync", lambda h: h.dma_start(out=bq2s, in_=co["bq2p"].ap[:, 0:8]),
          reads=[K("bq2p")], writes=[dk("bq2s")], dma=True)
        E("sync", lambda h: h.dma_start(
            out=bq2f.ap,
            in_=bq2s.transpose([1, 0]).unsqueeze(0),
            allow_slow_non_contiguous=True),
          reads=[dk("bq2s")], writes=[bq2f.k()], dma=True)
        E("vector", lambda h: h.tensor_copy(co["bq2r"].ap[:, :], bq2f.ap),
          reads=[bq2f.k()], writes=[K("bq2r")])

        hldg = [AUX.view(4096, F32, 128, 1024), AUX.view(8192, F32, 128, 1024)]
        hm2 = [ARN.view(8192, F32, 128, 1024), ARN.view(12288, F32, 128, 1024)]
        uqi = [ARN.view(16384, I32, 128, 1024), ARN.view(20480, I32, 128, 1024)]
        ud = [ARN.view(24576, F32, 128, 1024), ARN.view(28672, F32, 128, 1024)]
        ut = [ARN.view(32768, F32, 128, 1024), ARN.view(36864, F32, 128, 1024)]
        u2c = [ARN.view(40960, BF16, 128, 1024), ARN.view(43008, BF16, 128, 1024)]

        wq1b = ARN.view(2048, BF16, 128, 8 * HD)
        wk1b = ARN.view(3072, BF16, 128, 8 * HD)
        wv1b = ARN.view(4096, BF16, 128, 8 * HD)
        wl1stg = ARN.view(5120, F32, 128, HD)
        for src, dstv in ((wqT1, wq1b), (wkT1, wk1b), (wvT1, wv1b)):
            for c in range(8):
                E("sync", lambda h, src=src, c=c: h.dma_start(
                    out=wl1stg.ap, in_=src[c * 128:(c + 1) * 128, :]),
                  reads=[wl1stg.k()], writes=[wl1stg.k()], dma=True)
                E("vector", lambda h, dstv=dstv, c=c: h.tensor_copy(
                    dstv.ap[:, c * HD:(c + 1) * HD], wl1stg.ap),
                  reads=[wl1stg.k()],
                  writes=[dstv.k(c * HD, (c + 1) * HD)])
        ktmp = ARN.view(45056, BF16, HD, 512)
        vtmp = ARN.view(46080, BF16, 128, 4 * 65)
        vT_t = ARN.view(47104, F32, HD, 512)
        xTb = ARN.view(49152, BF16, 128, 4096)
        qTd = ARN.view(57344, BF16, 128, T)
        E("vector", lambda h: h.memset(
            vtmp.ap.rearrange("p (c e) -> p c e", e=65)[:, :, 64:65], 1.0),
          writes=[vtmp.k()])

        def l1_qkv(blk):
            tiles = [blk * 4 + j for j in range(4)]
            for j, ti in enumerate(tiles):
                for c in range(8):
                    sl = slice(c * 512 + j * 128, c * 512 + j * 128 + 128)
                    transpose_128(
                        stream_t(ti)[:, c * 128:(c + 1) * 128], [skey(ti)],
                        xTb.ap[:, sl], [xTb.k(sl.start, sl.stop)],
                        0 if (j * 8 + c) % 2 == 0 else 7)
            for wi, (dstv, plo) in enumerate(((wq1b, 512), (wk1b, 1024),
                                              (wv1b, 1536))):
                for c in range(8):
                    wk_ = dstv.k(c * HD, (c + 1) * HD)
                    E("tensor", lambda h, dstv=dstv, c=c, plo=plo: h.matmul(
                        ps(plo, 512)[0:HD, :],
                        dstv.ap[:, c * HD:(c + 1) * HD],
                        xTb.ap[:, c * 512:(c + 1) * 512],
                        start=(c == 0), stop=(c == 7)),
                      reads=[wk_, xTb.k(c * 512, (c + 1) * 512)],
                      writes=[psk(plo, 512)])
            E("scalar", lambda h, blk=blk: h.activation(
                qTd.ap[0:HD, blk * 512:(blk + 1) * 512], ps(512, 512)[0:HD, :],
                ACT.Identity, bias=co["bqkv1"].ap[:, 0:1]),
              reads=[psk(512, 512), K("bqkv1")],
              writes=[qTd.k(blk * 512, (blk + 1) * 512)])
            E("scalar", lambda h: h.activation(
                ktmp.ap, ps(1024, 512)[0:HD, :],
                ACT.Identity, bias=co["bqkv1"].ap[:, 1:2]),
              reads=[psk(1024, 512), K("bqkv1")], writes=[ktmp.k()])
            E("scalar", lambda h: h.activation(
                vT_t.ap, ps(1536, 512)[0:HD, :], ACT.Identity,
                bias=co["bqkv1"].ap[:, 2:3]),
              reads=[psk(1536, 512), K("bqkv1")], writes=[vT_t.k()])
            E("gpsimd", lambda h, blk=blk: h.dma_start(
                out=kg_in[:, blk * 512:(blk + 1) * 512], in_=ktmp.ap),
              reads=[ktmp.k()],
              writes=[dk("kg_in", blk * 1024, (blk + 1) * 1024)], dma=True,
              fam="dram:kg_in")
            for j in range(4):
                pslo = 2048 + (j % 2) * 512
                E("tensor", lambda h, j=j, pslo=pslo: h.transpose(
                    ps(pslo, 128)[:, 0:HD], vT_t.ap[:, j * 128:(j + 1) * 128],
                    co["identf"].ap[0:HD, 0:HD]),
                  reads=[vT_t.k(j * 128, (j + 1) * 128), K("identf")],
                  writes=[psk(pslo, 512)])
                E("vector", lambda h, j=j, pslo=pslo: h.tensor_copy(
                    vtmp.ap[:, j * 65: j * 65 + 64], ps(pslo, 128)[:, 0:HD]),
                  reads=[psk(pslo, 512)], writes=[vtmp.k(j * 65, j * 65 + 64)])
            E("gpsimd", lambda h, blk=blk: h.dma_start(
                out=vg_in.rearrange("(c p) e -> p c e", p=128)[
                    :, blk * 4:(blk + 1) * 4, :],
                in_=vtmp.ap.rearrange("p (c e) -> p c e", e=65)),
              reads=[vtmp.k()],
              writes=[dk("vg_in", blk * 520, (blk + 1) * 520)], dma=True,
              fam="dram:vg_in")

        NF2 = NFC // 2
        for tt in range(T // 512):
            for sub in range(4):
                for nb in range(2):
                    lo = sub * 1024 + nb * 512
                    E("tensor", lambda h, lo=lo, nb=nb: h.matmul(
                        ps(lo, 512), co["onesb"].ap[0:1, :],
                        co["bq2r"].ap[:, nb * 512:(nb + 1) * 512],
                        start=True, stop=False),
                      reads=[K("onesb"), K("bq2r")], writes=[psk(lo, 512)])

            def b_load(i, tt=tt):
                b = i % 2
                E("sync", lambda h, i=i, tt=tt, b=b: h.dma_start(
                    out=hldg[b].ap.rearrange("p (g w) -> p g w", w=512),
                    in_=h_hbm[2 * i:2 * i + 2, :,
                              tt * 512:(tt + 1) * 512].transpose([1, 0, 2])),
                  reads=[hldg[b].k(),
                         dk("h_hbm", 2 * i * 128 * T, (2 * i + 2) * 128 * T)],
                  writes=[hldg[b].k()], dma=True)
                E("scalar", lambda h, b=b: h.activation(
                    hm2[b].ap, hldg[b].ap, ACT.Identity,
                    scale=co["rs2"].ap[:, 0:1]),
                  reads=[hldg[b].k(), K("rs2")], writes=[hm2[b].k()])
                E("scalar", lambda h, b=b: h.copy(uqi[b].ap, hm2[b].ap),
                  reads=[hm2[b].k()], writes=[uqi[b].k()])

            def b_work(i, tt=tt):
                b = i % 2
                E("vector", lambda h, b=b: h.tensor_tensor(
                    ud[b].ap, hm2[b].ap, uqi[b].ap, ALU.subtract),
                  reads=[hm2[b].k(), uqi[b].k()], writes=[ud[b].k()])
                E("vector", lambda h, b=b: h.scalar_tensor_tensor(
                    ut[b].ap, ud[b].ap, 0.5, uqi[b].ap, ALU.is_gt, ALU.add),
                  reads=[ud[b].k(), uqi[b].k()], writes=[ut[b].k()])
                E("vector", lambda h, b=b: h.scalar_tensor_tensor(
                    u2c[b].ap, ud[b].ap, -0.5, ut[b].ap,
                    ALU.is_lt, ALU.subtract),
                  reads=[ud[b].k(), ut[b].k()], writes=[u2c[b].k()])
                for ff in range(2):
                    f = 2 * i + ff
                    for sub in range(4):
                        for nb in range(2):
                            lo = sub * 1024 + nb * 512
                            E("tensor", lambda h, f=f, ff=ff, sub=sub, nb=nb,
                              b=b, lo=lo: h.matmul(
                                  ps(lo, 512),
                                  u2c[b].ap[:, ff * 512 + sub * 128:
                                            ff * 512 + sub * 128 + 128],
                                  w2qT.ap[:, f * D + nb * 512:
                                          f * D + nb * 512 + 512],
                                  start=False, stop=(f == NFC - 1)),
                              reads=[u2c[b].k(ff * 512 + sub * 128,
                                              ff * 512 + sub * 128 + 128),
                                     w2qT.k(f * D + nb * 512,
                                            f * D + nb * 512 + 512)],
                              writes=[psk(lo, 512)])

            b_load(0)
            for i in range(1, NF2):
                b_load(i)
                b_work(i - 1)
            b_work(NF2 - 1)
            for sub in range(4):
                ti = tt * 4 + sub
                E("vector", lambda h, sub=sub, ti=ti: h.scalar_tensor_tensor(
                    tmpr.ap, ps(sub * 1024, 1024), co["negs2"].ap[:, 0:1],
                    stream_t(ti), ALU.mult, ALU.add),
                  reads=[psk(sub * 1024, 1024), K("negs2"), skey(ti)],
                  writes=[tmpr.k()])
                layer_norm(tmpr, stream_t(ti), [skey(ti)], lnsq_b)
            l1_qkv(tt)

        mstg = [AUX.view(3328, F32, 128, 1024), AUX.view(7424, F32, 128, 1024)]
        mred = AUX.view(11520, F32, 128, 1)
        E("vector", lambda h: h.memset(co["macc"].ap[:, 0:3], 0.0),
          writes=[K("macc")])

        def mean_pass(idx, loader, q, stg0, mr):
            for f in range(NFC):
                b = f % 2
                E(q, lambda h, f=f, b=b: h.dma_start(
                    out=stg0[b].ap.rearrange("p (j w) -> p j w", w=128),
                    in_=loader(f)),
                  reads=[stg0[b].k()], writes=[stg0[b].k()], dma=True)
                E("vector", lambda h, b=b: h.reduce_sum(
                    mr.ap, stg0[b].ap, AX, apply_absolute_value=True),
                  reads=[stg0[b].k()], writes=[mr.k()])
                E("vector", lambda h, idx=idx: h.tensor_tensor(
                    co["macc"].ap[:, idx:idx + 1], co["macc"].ap[:, idx:idx + 1],
                    mr.ap, ALU.add),
                  reads=[mr.k(), K("macc", idx, idx + 1)],
                  writes=[K("macc", idx, idx + 1)])

        mstgu = [BIG.view(49152, F32, 128, 1024), BIG.view(53248, F32, 128, 1024)]
        mstgd = [BIG.view(57344, F32, 128, 1024), BIG.view(61440, F32, 128, 1024)]
        mredu = AUX.view(11524, F32, 128, 1)
        mredd = AUX.view(11528, F32, 128, 1)
        E("gpsimd", lambda h: h.dma_start(out=qTd.ap[64:128, :],
                                          in_=qTd.ap[0:64, :]),
          reads=[qTd.k()], writes=[qTd.k()], dma=True)
        E("gpsimd", lambda h: h.collective_compute(
            "AllGather", ALU.bypass, replica_groups=[list(range(ncores))],
            ins=[kg_in.opt()], outs=[kg_out.opt()]),
          reads=[dk("kg_in")], writes=[dk("kg_out")], cc=True)
        E("gpsimd", lambda h: h.collective_compute(
            "AllGather", ALU.bypass, replica_groups=[list(range(ncores))],
            ins=[vg_in.opt()], outs=[vg_out.opt()]),
          reads=[dk("vg_in")], writes=[dk("vg_out")], cc=True)
        kTp = ARN.view(0, BF16, 128, (NCH // 2) * 128)
        vpack = BIG.view(0, BF16, 128, NCH * 65)
        nloc = T // 128
        for i in range(ncores):
            for par in range(2):
                m0 = (i * nloc + par) // 2
                cnt = nloc // 2
                E("sync", lambda h, i=i, par=par, cnt=cnt, m0=m0: h.dma_start(
                    out=kTp.ap[par * 64:par * 64 + 64, :].rearrange(
                        "p (m w) -> p m w", w=128)[:, m0:m0 + cnt, :],
                    in_=kg_out[i * HD:(i + 1) * HD, :].rearrange(
                        "p (m w) -> p m w", w=128)[:, par::2, :]),
                  reads=[dk("kg_out")],
                  writes=[kTp.k(m0 * 128, (m0 + cnt) * 128)], dma=True,
                  fam=f"kTp{i}")
        nvch = T // 128
        for i in range(ncores):
            E("sync", lambda h, i=i: h.dma_start(
                out=vpack.ap.rearrange("p (c e) -> p c e", e=65)[
                    :, i * nvch:(i + 1) * nvch, :],
                in_=vg_out.rearrange("(c p) e -> p c e", p=128)[
                    :, i * nvch:(i + 1) * nvch, :]),
              reads=[dk("vg_out")],
              writes=[vpack.k(i * nvch * 65, (i + 1) * nvch * 65)], dma=True,
              fam=f"vpk{i}")
        mean_pass(0, lambda f: wgblk[f, :, :].rearrange("(j p) w -> p j w", p=128),
                  "scalar", mstg, mred)
        mean_pass(1, lambda f: wublk[f, :, :].rearrange("(j p) w -> p j w", p=128),
                  "gpsimd", mstgu, mredu)
        mean_pass(2, lambda f: wdT[f * 128:(f + 1) * 128, :].rearrange("p (j w) -> p j w", w=128),
                  "sync", mstgd, mredd)

        E("tensor", lambda h: h.transpose(
            ps(3584, 128)[0:3, :], co["macc"].ap[:, 0:3],
            co["identf"].ap[:, :]),
          reads=[K("macc"), K("identf")], writes=[psk(3584, 512)])
        E("vector", lambda h: h.reduce_sum(
            co["macc"].ap[0:3, 3:4], ps(3584, 128)[0:3, :], AX),
          reads=[psk(3584, 512)], writes=[K("macc", 3, 4)])
        E("tensor", lambda h: h.transpose(
            ps(3584, 128)[0:1, 0:3], co["macc"].ap[0:3, 3:4],
            co["identf"].ap[0:3, 0:3]),
          reads=[K("macc", 3, 4), K("identf")], writes=[psk(3584, 512)])
        E("vector", lambda h: h.tensor_copy(co["row"].ap[0:1, 0:3],
                                            ps(3584, 128)[0:1, 0:3]),
          reads=[psk(3584, 512)], writes=[K("row", 0, 3)])
        bcast_row(co["row"].ap[0:1, 0:3], [K("row", 0, 3)],
                  co["macc"].ap[:, 0:3], [K("macc", 0, 3)], 3)
        for idx, mnm in ((0, "mg"), (1, "mu"), (2, "md")):
            E("vector", lambda h, idx=idx, mnm=mnm: h.tensor_scalar(
                co[mnm].ap[:, :], co["macc"].ap[:, idx:idx + 1],
                1.0 / (FF * D), 1e-5, ALU.mult, ALU.max),
              reads=[K("macc", idx, idx + 1)], writes=[K(mnm)])
            E("vector", lambda h, mnm=mnm: h.reciprocal(
                co["r" + mnm].ap[:, :], co[mnm].ap[:, :]),
              reads=[K(mnm)], writes=[K("r" + mnm)])
            E("vector", lambda h, mnm=mnm: h.tensor_scalar(
                co["neg" + mnm].ap[:, :], co[mnm].ap[:, :], -1.0, None,
                ALU.mult),
              reads=[K(mnm)], writes=[K("neg" + mnm)])

        q0 = T * 8
        qstg = [ARN.view(q0, F32, 128, 1024), ARN.view(q0 + 4096, F32, 128, 1024)]
        qm = ARN.view(q0 + 8192, F32, 128, 1024)
        qqi = ARN.view(q0 + 12288, I32, 128, 1024)
        qd = ARN.view(q0 + 16384, F32, 128, 1024)
        qt = ARN.view(q0 + 20480, F32, 128, 1024)
        qout = [ARN.view(q0 + 12288, BF16, 128, 1024),
                ARN.view(q0 + 14336, BF16, 128, 1024)]

        def quant_gu(f):
            for mi, (blkap, rk, dsth) in enumerate(
                    ((wgblk, "rmg", wgq_hbm), (wublk, "rmu", wuq_hbm))):
                b = (2 * f + mi) % 2
                E("sync", lambda h, f=f, b=b, blkap=blkap: h.dma_start(
                    out=qstg[b].ap.rearrange("p (j w) -> p j w", w=128),
                    in_=blkap[f, :, :].rearrange("(j p) w -> p j w", p=128)),
                  reads=[qstg[b].k()], writes=[qstg[b].k()], dma=True)
                E("vector", lambda h, b=b, rk=rk: h.tensor_scalar(
                    qm.ap, qstg[b].ap, co[rk].ap[:, 0:1], None, ALU.mult),
                  reads=[qstg[b].k(), K(rk)], writes=[qm.k()])
                E("vector", lambda h: h.tensor_copy(qqi.ap, qm.ap),
                  reads=[qm.k()], writes=[qqi.k()])
                E("vector", lambda h: h.tensor_tensor(
                    qd.ap, qm.ap, qqi.ap, ALU.subtract),
                  reads=[qm.k(), qqi.k()], writes=[qd.k()])
                E("vector", lambda h: h.scalar_tensor_tensor(
                    qt.ap, qd.ap, 0.5, qqi.ap, ALU.is_gt, ALU.add),
                  reads=[qd.k(), qqi.k()], writes=[qt.k()])
                E("vector", lambda h: h.scalar_tensor_tensor(
                    qm.ap, qd.ap, -0.5, qt.ap, ALU.is_lt, ALU.subtract),
                  reads=[qd.k(), qt.k()], writes=[qm.k()])
                E("vector", lambda h, mi=mi: h.tensor_scalar(
                    qout[mi].ap, qm.ap, -1.0, 1.0, ALU.max, ALU.min),
                  reads=[qm.k()], writes=[qout[mi].k()])
                E("gpsimd", lambda h, f=f, mi=mi, dsth=dsth: h.dma_start(
                    out=dsth[f, :, :], in_=qout[mi].ap),
                  reads=[qout[mi].k()],
                  writes=[dk("wgq_hbm" if mi == 0 else "wuq_hbm",
                             f * 128 * D * 2, (f + 1) * 128 * D * 2)],
                  dma=True,
                  fam="dram:wgq_hbm" if mi == 0 else "dram:wuq_hbm")

        _bo = NCH * 65 * 2 + (64 - NCH * 65 * 2 % 64) % 64
        lnsq_at = BIG.view(_bo, F32, 128, 1024)
        PT = [BIG.view(_bo + 4096, BF16, 128, 1536),
              BIG.view(_bo + 7168, BF16, 128, 1536)]
        ysb = BIG.view(_bo + 10240, F32, 65, 512)
        ydiv = BIG.view(_bo + 12288, F32, 128, 1)
        ytok = BIG.view(_bo + 12352, F32, 128, 64)
        GRP = 3
        ngrp = (NCH + GRP - 1) // GRP

        def att_S(qg, g):
            ab = g % 2
            sb0 = 0 if ab == 0 else 1536
            chunks = list(range(g * GRP, min((g + 1) * GRP, NCH)))
            for ci, ch in enumerate(chunks):
                par = ch % 2
                m = ch // 2
                kk = kTp.k(m * 128, (m + 1) * 128)
                E("tensor", lambda h, par=par, m=m, sb0=sb0, ci=ci, qg=qg:
                  h.matmul(
                      ps(sb0 + ci * 512, 512),
                      kTp.ap[par * 64:par * 64 + 64, m * 128:(m + 1) * 128],
                      qTd.ap[par * 64:par * 64 + 64,
                             qg * 512:(qg + 1) * 512],
                      start=True, stop=True),
                  reads=[kk, qTd.k()], writes=[psk(sb0 + ci * 512, 512)])

        def att_exp(qg, g):
            ab = g % 2
            sb0 = 0 if ab == 0 else 1536
            n = min((g + 1) * GRP, NCH) - g * GRP
            E("scalar", lambda h, sb0=sb0, ab=ab, n=n: h.activation(
                PT[ab].ap[:, 0:n * 512], ps(sb0, n * 512), ACT.Exp,
                scale=1.0 / 32.0),
              reads=[psk(sb0, n * 512)], writes=[PT[ab].k(0, n * 512)])

        def att_PV(qg, g):
            ab = g % 2
            chunks = list(range(g * GRP, min((g + 1) * GRP, NCH)))
            for ci, ch in enumerate(chunks):
                vk = vpack.k(ch * 65, (ch + 1) * 65)
                E("tensor", lambda h, ch=ch, ci=ci, ab=ab: h.matmul(
                    ps(3072, 512)[0:65, :],
                    vpack.ap[:, ch * 65:(ch + 1) * 65],
                    PT[ab].ap[:, ci * 512:(ci + 1) * 512],
                    start=(ch == 0), stop=(ch == NCH - 1)),
                  reads=[vk, PT[ab].k(ci * 512, (ci + 1) * 512)],
                  writes=[psk(3072, 512)])

        for qg in range(T // 512):
            att_S(qg, 0)
            att_exp(qg, 0)
            for g in range(1, ngrp):
                att_S(qg, g)
                att_exp(qg, g)
                att_PV(qg, g - 1)
            att_PV(qg, ngrp - 1)
            E("vector", lambda h: h.tensor_copy(ysb.ap, ps(3072, 512)[0:65, :]),
              reads=[psk(3072, 512)], writes=[ysb.k()])
            for j in range(4):
                ti = qg * 4 + j
                E("tensor", lambda h, j=j: h.transpose(
                    ps(3584, 128)[:, 0:65], ysb.ap[:, j * 128:(j + 1) * 128],
                    co["identf"].ap[0:65, 0:65]),
                  reads=[ysb.k(j * 128, (j + 1) * 128), K("identf")],
                  writes=[psk(3584, 512)])
                E("vector", lambda h: h.reciprocal(
                    ydiv.ap, ps(3584, 128)[:, 64:65]),
                  reads=[psk(3584, 512)], writes=[ydiv.k()])
                E("vector", lambda h: h.tensor_scalar(
                    ytok.ap, ps(3584, 128)[:, 0:64], ydiv.ap[:, 0:1],
                    None, ALU.mult),
                  reads=[psk(3584, 512), ydiv.k()], writes=[ytok.k()])
                E("vector", lambda h, ti=ti: h.tensor_tensor(
                    tmpr.ap.rearrange("p (r c) -> p r c", c=64),
                    stream_t(ti).rearrange("p (r c) -> p r c", c=64),
                    ytok.ap.rearrange("p (o c) -> p o c", o=1).broadcast_to(
                        [128, 16, 64]),
                    ALU.add),
                  reads=[ytok.k(), skey(ti)], writes=[tmpr.k()])
                layer_norm(tmpr, stream_t(ti), [skey(ti)], lnsq_at)
            fpq = NFC // (T // 512)
            for f in range(qg * fpq, (qg + 1) * fpq):
                quant_gu(f)

        xqT = ARN.view(0, BF16, 128, 8 * T)
        xqm5 = AUX.view(0, F32, 128, 1024)
        xqng = AUX.view(4096, F32, 128, 1024)
        u1tile2 = AUX.view(8192, BF16, 128, 1024)
        xqi = BIG.view(0, I32, 128, 1024)
        xd = BIG.view(4096, F32, 128, 1024)
        xt = BIG.view(8192, F32, 128, 1024)
        for i in range(NT):
            E("vector", lambda h, i=i: h.reduce_max(
                co["ln"].ap[:, 0:1], stream_t(i), AX,
                apply_absolute_value=True),
              reads=[skey(i)], writes=[K("ln", 0, 1)])
            E("vector", lambda h: h.tensor_scalar(
                co["ln"].ap[:, 1:2], co["ln"].ap[:, 0:1], 1e-5, None, ALU.max),
              reads=[K("ln", 0, 1)], writes=[K("ln", 1, 2)])
            E("vector", lambda h: h.reciprocal(co["ln"].ap[:, 2:3],
                                               co["ln"].ap[:, 1:2]),
              reads=[K("ln", 1, 2)], writes=[K("ln", 2, 3)])
            E("vector", lambda h: h.tensor_scalar(
                co["rsct"].ap[:, :], co["ln"].ap[:, 2:3], 127.0, None,
                ALU.mult),
              reads=[K("ln", 2, 3)], writes=[K("rsct")])
            E("vector", lambda h: h.tensor_scalar(
                co["nsct"].ap[:, :], co["ln"].ap[:, 1:2], -1.0 / 127.0, None,
                ALU.mult),
              reads=[K("ln", 1, 2)], writes=[K("nsct")])
            E("scalar", lambda h, i=i: h.activation(
                xqm5.ap, stream_t(i), ACT.Identity,
                scale=co["rsct"].ap[:, 0:1]),
              reads=[skey(i), K("rsct")], writes=[xqm5.k()])
            round_neg(xqm5.ap, [xqm5.k()], xqi, xd, xt, xqng.ap, [xqng.k()])
            E("vector", lambda h: h.tensor_scalar(
                u1tile2.ap, xqng.ap, co["nsct"].ap[:, 0:1], None, ALU.mult),
              reads=[xqng.k(), K("nsct")], writes=[u1tile2.k()])
            for c in range(8):
                sl = slice(c * T + i * 128, c * T + i * 128 + 128)
                transpose_128(
                    u1tile2.ap[:, c * 128:(c + 1) * 128], [u1tile2.k()],
                    xqT.ap[:, sl], [xqT.k(sl.start, sl.stop)],
                    c % 4)

        wgqb = [BIG.view(16384, BF16, 128, 1024), BIG.view(18432, BF16, 128, 1024)]
        wuqb = [BIG.view(20480, BF16, 128, 1024), BIG.view(22528, BF16, 128, 1024)]
        TH_W = min(1024, T)
        NTH = T // TH_W
        gate_t = BIG.view(0, F32, 128, TH_W)
        up_t = BIG.view(4096, F32, 128, TH_W)
        gu_t = [BIG.view(8192, F32, 128, TH_W), BIG.view(12288, F32, 128, TH_W)]
        acc_t = BIG.view(24576, F32, 128, T)
        E("vector", lambda h: h.memset(acc_t.ap, 0.0), writes=[acc_t.k()])

        def p1_load(f):
            b = f % 2
            E("sync", lambda h, f=f, b=b: h.dma_start(
                out=wgqb[b].ap, in_=wgq_hbm[f, :, :]),
              reads=[wgqb[b].k(),
                     dk("wgq_hbm", f * 128 * D * 2, (f + 1) * 128 * D * 2)],
              writes=[wgqb[b].k()], dma=True)
            E("sync", lambda h, f=f, b=b: h.dma_start(
                out=wuqb[b].ap, in_=wuq_hbm[f, :, :]),
              reads=[wuqb[b].k(),
                     dk("wuq_hbm", f * 128 * D * 2, (f + 1) * 128 * D * 2)],
              writes=[wuqb[b].k()], dma=True)

        def p1_mms(g):
            f, th = divmod(g, NTH)
            b = f % 2
            for wq_, lo0 in ((wgqb[b], (g % 2) * 1024),
                             (wuqb[b], 2048 + (g % 2) * 1024)):
                for c in range(8):
                    for tp in range(TH_W // 512):
                        lo = lo0 + tp * 512
                        E("tensor", lambda h, wq_=wq_, c=c, tp=tp, lo=lo,
                          th=th: h.matmul(
                              ps(lo, 512), wq_.ap[:, c * 128:(c + 1) * 128],
                              xqT.ap[:, c * T + th * TH_W + tp * 512:
                                     c * T + th * TH_W + tp * 512 + 512],
                              start=(c == 0), stop=(c == 7)),
                          reads=[wq_.k(c * 128, (c + 1) * 128),
                                 xqT.k(c * T + th * TH_W + tp * 512,
                                       c * T + th * TH_W + tp * 512 + 512)],
                          writes=[psk(lo, 512)])

        def p1_evac(g):
            f, th = divmod(g, NTH)
            pg, pu = (g % 2) * 1024, 2048 + (g % 2) * 1024
            gb = g % 2
            E("scalar", lambda h, pg=pg: h.activation(
                gate_t.ap, ps(pg, TH_W), ACT.Silu,
                scale=co["negmg"].ap[:, 0:1]),
              reads=[psk(pg, TH_W), K("negmg")], writes=[gate_t.k()])
            E("scalar", lambda h, pu=pu: h.activation(
                up_t.ap, ps(pu, TH_W), ACT.Identity,
                scale=co["negmu"].ap[:, 0:1]),
              reads=[psk(pu, TH_W), K("negmu")], writes=[up_t.k()])
            E("vector", lambda h, gb=gb: h.tensor_mul(
                gu_t[gb].ap, gate_t.ap, up_t.ap),
              reads=[gate_t.k(), up_t.k()], writes=[gu_t[gb].k()])
            E("scalar", lambda h, gb=gb: h.activation(
                up_t.ap, gu_t[gb].ap, ACT.Abs),
              reads=[gu_t[gb].k()], writes=[up_t.k()])
            E("vector", lambda h, th=th: h.tensor_max(
                acc_t.ap[:, th * TH_W:(th + 1) * TH_W],
                acc_t.ap[:, th * TH_W:(th + 1) * TH_W], up_t.ap),
              reads=[up_t.k(), acc_t.k(th * TH_W, (th + 1) * TH_W)],
              writes=[acc_t.k(th * TH_W, (th + 1) * TH_W)])
            E("scalar", lambda h, f=f, th=th, gb=gb: h.dma_start(
                out=gu_hbm[f, :, th * TH_W:(th + 1) * TH_W], in_=gu_t[gb].ap),
              reads=[gu_t[gb].k()],
              writes=[dk("gu_hbm", f * 128 * T + th * TH_W,
                         f * 128 * T + (th + 1) * TH_W)], dma=True)

        wdstg = [AUX.view(0, F32, 128, 1024), AUX.view(4096, F32, 128, 1024)]
        wqm = BIG.view(49152, F32, 128, 1024)
        wqqi = BIG.view(53248, I32, 128, 1024)
        wqd = BIG.view(57344, F32, 128, 1024)
        wqt = BIG.view(61440, F32, 128, 1024)
        wdqo = [BIG.view(53248, BF16, 128, 1024), BIG.view(55296, BF16, 128, 1024)]

        def wd_load(f):
            b = f % 2
            E("sync", lambda h, f=f, b=b: h.dma_start(
                out=wdstg[b].ap, in_=wdT[f * 128:(f + 1) * 128, :]),
              reads=[wdstg[b].k()], writes=[wdstg[b].k()], dma=True)

        def quant_wd(f):
            b = f % 2
            E("vector", lambda h, b=b: h.tensor_scalar(
                wqm.ap, wdstg[b].ap, co["rmd"].ap[:, 0:1], None, ALU.mult),
              reads=[wdstg[b].k(), K("rmd")], writes=[wqm.k()])
            E("vector", lambda h: h.tensor_copy(wqqi.ap, wqm.ap),
              reads=[wqm.k()], writes=[wqqi.k()])
            E("vector", lambda h: h.tensor_tensor(
                wqd.ap, wqm.ap, wqqi.ap, ALU.subtract),
              reads=[wqm.k(), wqqi.k()], writes=[wqd.k()])
            E("vector", lambda h: h.scalar_tensor_tensor(
                wqt.ap, wqd.ap, 0.5, wqqi.ap, ALU.is_gt, ALU.add),
              reads=[wqd.k(), wqqi.k()], writes=[wqt.k()])
            E("vector", lambda h: h.scalar_tensor_tensor(
                wqm.ap, wqd.ap, -0.5, wqt.ap, ALU.is_lt, ALU.subtract),
              reads=[wqd.k(), wqt.k()], writes=[wqm.k()])
            E("vector", lambda h, b=b: h.tensor_scalar(
                wdqo[b].ap, wqm.ap, -1.0, 1.0, ALU.max, ALU.min),
              reads=[wqm.k()], writes=[wdqo[b].k()])
            E("gpsimd", lambda h, f=f, b=b: h.dma_start(
                out=wdq_hbm[f, :, :], in_=wdqo[b].ap),
              reads=[wdqo[b].k()],
              writes=[dk("wdq_hbm", f * 128 * D * 2, (f + 1) * 128 * D * 2)],
              dma=True, fam="dram:wdq_hbm")

        QTH = min(2, NTH - 1)
        p1_load(0)
        if NFC > 1:
            p1_load(1)
        wd_load(0)
        NG = NFC * NTH
        for g in range(NG):
            f, th = divmod(g, NTH)
            p1_mms(g)
            if g > 0:
                p1_evac(g - 1)
            if th == 0 and f + 1 < NFC:
                wd_load(f + 1)
            if th == QTH:
                quant_wd(f)
            if th == NTH - 1 and f + 2 < NFC:
                p1_load(f + 2)
        p1_evac(NG - 1)
        for b in range(NT):
            E("tensor", lambda h, b=b: h.transpose(
                ps(3584, 128), acc_t.ap[:, b * 128:(b + 1) * 128],
                co["identf"].ap[:, :]),
              reads=[acc_t.k(b * 128, (b + 1) * 128), K("identf")],
              writes=[psk(3584, 512)])
            E("vector", lambda h, b=b: h.reduce_max(
                co["mxtok"].ap[:, b:b + 1], ps(3584, 128), AX),
              reads=[psk(3584, 512)], writes=[K("mxtok", b, b + 1)])
        E("vector", lambda h: h.tensor_scalar(
            co["mxtok"].ap[:, :], co["mxtok"].ap[:, :], 1e-5, None, ALU.max),
          reads=[K("mxtok")], writes=[K("mxtok")])
        E("vector", lambda h: h.tensor_scalar(
            co["sctok"].ap[:, :], co["mxtok"].ap[:, :], 1.0 / 127.0, None,
            ALU.mult),
          reads=[K("mxtok")], writes=[K("sctok")])
        E("vector", lambda h: h.reciprocal(co["rs127"].ap[:, :],
                                           co["mxtok"].ap[:, :]),
          reads=[K("mxtok")], writes=[K("rs127")])
        E("vector", lambda h: h.tensor_scalar(
            co["rs127"].ap[:, :], co["rs127"].ap[:, :], 127.0, None, ALU.mult),
          reads=[K("rs127")], writes=[K("rs127")])
        row1T = BIG.view(45056, F32, 1, T)
        for b in range(NT):
            E("tensor", lambda h, b=b: h.transpose(
                ps(3584, 128)[0:1, :], co["rs127"].ap[:, b:b + 1],
                co["identf"].ap[:, :]),
              reads=[K("rs127", b, b + 1), K("identf")],
              writes=[psk(3584, 512)])
            E("vector", lambda h, b=b: h.tensor_copy(
                row1T.ap[0:1, b * 128:(b + 1) * 128], ps(3584, 128)[0:1, :]),
              reads=[psk(3584, 512)],
              writes=[row1T.k(b * 128, (b + 1) * 128)])
        sc2b = BIG.view(0, F32, 128, T)
        for j in range(T // 512):
            bcast_row(row1T.ap[0:1, j * 512:(j + 1) * 512],
                      [row1T.k(j * 512, (j + 1) * 512)],
                      sc2b.ap[:, j * 512:(j + 1) * 512],
                      [sc2b.k(j * 512, (j + 1) * 512)], 512, bank=6)

        wdq = ARN.view(0, BF16, 128, 32 * D)
        for ch in range(4):
            f0, f1 = ch * (NFC // 4), (ch + 1) * (NFC // 4)
            E("sync", lambda h, f0=f0, f1=f1: h.dma_start(
                out=wdq.ap[:, f0 * D:f1 * D].rearrange(
                    "p (f w) -> p f w", w=D),
                in_=wdq_hbm[f0:f1, :, :].transpose([1, 0, 2])),
              reads=[dk("wdq_hbm", f0 * 128 * D * 2, f1 * 128 * D * 2)],
              writes=[wdq.k(f0 * D, f1 * D)], dma=True, fam="wdql")

        guldg = [AUX.view(8192, F32, 128, 512), AUX.view(10240, F32, 128, 512)]
        lnsq_p2 = BIG.view(16384, F32, 128, 1024)
        tmpr2 = BIG.view(20480, F32, 128, 1024)
        guldg = [AUX.view(0, F32, 128, 1024), AUX.view(4096, F32, 128, 1024)]
        gum = [BIG.view(24576, F32, 128, 1024), BIG.view(28672, F32, 128, 1024)]
        gqi = [BIG.view(32768, I32, 128, 1024), BIG.view(36864, I32, 128, 1024)]
        gd = [BIG.view(40960, F32, 128, 1024), BIG.view(45056, F32, 128, 1024)]
        gt = [BIG.view(49152, F32, 128, 1024), BIG.view(53248, F32, 128, 1024)]
        guqc = [BIG.view(57344, BF16, 128, 1024), BIG.view(59392, BF16, 128, 1024)]
        ostg = BIG.view(61440, F32, 128, 1024)
        NF2 = NFC // 2
        for tt in range(T // 512):

            def g_load(i, tt=tt):
                b = i % 2
                E("sync", lambda h, i=i, tt=tt, b=b: h.dma_start(
                    out=guldg[b].ap.rearrange("p (g w) -> p g w", w=512),
                    in_=gu_hbm[2 * i:2 * i + 2, :,
                               tt * 512:(tt + 1) * 512].transpose([1, 0, 2])),
                  reads=[guldg[b].k(),
                         dk("gu_hbm", 2 * i * 128 * T, (2 * i + 2) * 128 * T)],
                  writes=[guldg[b].k()], dma=True)
                E("vector", lambda h, b=b, tt=tt: h.tensor_tensor(
                    gum[b].ap.rearrange("p (g w) -> p g w", w=512),
                    guldg[b].ap.rearrange("p (g w) -> p g w", w=512),
                    sc2b.ap[:, tt * 512:(tt + 1) * 512].rearrange(
                        "p (o w) -> p o w", o=1).broadcast_to([128, 2, 512]),
                    ALU.mult),
                  reads=[guldg[b].k(), sc2b.k(tt * 512, (tt + 1) * 512)],
                  writes=[gum[b].k()])
                E("scalar", lambda h, b=b: h.copy(gqi[b].ap, gum[b].ap),
                  reads=[gum[b].k()], writes=[gqi[b].k()])

            def g_work(i, tt=tt):
                b = i % 2
                E("vector", lambda h, b=b: h.tensor_tensor(
                    gd[b].ap, gum[b].ap, gqi[b].ap, ALU.subtract),
                  reads=[gum[b].k(), gqi[b].k()], writes=[gd[b].k()])
                E("vector", lambda h, b=b: h.scalar_tensor_tensor(
                    gt[b].ap, gd[b].ap, 0.5, gqi[b].ap, ALU.is_gt, ALU.add),
                  reads=[gd[b].k(), gqi[b].k()], writes=[gt[b].k()])
                E("vector", lambda h, b=b: h.scalar_tensor_tensor(
                    guqc[b].ap, gd[b].ap, -0.5, gt[b].ap,
                    ALU.is_lt, ALU.subtract),
                  reads=[gd[b].k(), gt[b].k()], writes=[guqc[b].k()])
                for ff in range(2):
                    f = 2 * i + ff
                    for sub in range(4):
                        for nb in range(2):
                            lo = sub * 1024 + nb * 512
                            E("tensor", lambda h, f=f, ff=ff, sub=sub, nb=nb,
                              b=b, lo=lo: h.matmul(
                                  ps(lo, 512),
                                  guqc[b].ap[:, ff * 512 + sub * 128:
                                             ff * 512 + sub * 128 + 128],
                                  wdq.ap[:, f * D + nb * 512:
                                         f * D + nb * 512 + 512],
                                  start=(f == 0), stop=(f == NFC - 1)),
                              reads=[guqc[b].k(ff * 512 + sub * 128,
                                               ff * 512 + sub * 128 + 128),
                                     wdq.k(f * D + nb * 512,
                                           f * D + nb * 512 + 512)],
                              writes=[psk(lo, 512)])

            g_load(0)
            for i in range(1, NF2):
                g_load(i)
                g_work(i - 1)
            g_work(NF2 - 1)
            for sub in range(4):
                ti = tt * 4 + sub
                E("vector", lambda h, ti=ti: h.tensor_tensor(
                    co["ct"].ap[:, :], co["sctok"].ap[:, ti:ti + 1],
                    co["md"].ap[:, 0:1], ALU.mult),
                  reads=[K("sctok", ti, ti + 1), K("md")], writes=[K("ct")])
                E("vector", lambda h, sub=sub, ti=ti: h.scalar_tensor_tensor(
                    tmpr2.ap, ps(sub * 1024, 1024), co["ct"].ap[:, 0:1],
                    stream_t(ti), ALU.mult, ALU.add),
                  reads=[psk(sub * 1024, 1024), K("ct"), skey(ti)],
                  writes=[tmpr2.k()])
                layer_norm(tmpr2, ostg.ap, [ostg.k()], lnsq_p2)
                E("sync", lambda h, ti=ti: h.dma_start(
                    out=out_h[ti * 128:(ti + 1) * 128, :], in_=ostg.ap),
                  reads=[ostg.k()],
                  writes=[dk("out", ti * 128 * D, (ti + 1) * 128 * D)],
                  dma=True)

        plan.barrier_on("gpsimd")

        with nc.Block() as block:
            plan.replay(block)

    return nc, plan



def _host_prep(inputs):
    f32 = np.float32
    import ml_dtypes
    bf = ml_dtypes.bfloat16
    g = {}
    g["wqkvT"] = np.ascontiguousarray(np.asarray(inputs["L0_attn_Wqkv"], f32).T)
    g["bqkvp"] = np.ascontiguousarray(
        np.asarray(inputs["L0_attn_bqkv"], f32).reshape(24, 128).T)
    g["wprojT"] = np.ascontiguousarray(np.asarray(inputs["L0_attn_Wproj"], f32).T)
    g["bprojr"] = np.ascontiguousarray(
        np.asarray(inputs["L0_attn_bproj"], f32).reshape(1, D))
    g["w1n"] = np.ascontiguousarray(np.asarray(inputs["L0_ffn_w1"], f32))
    g["b1p"] = np.ascontiguousarray(
        np.asarray(inputs["L0_ffn_b1"], f32).reshape(32, 128).T)
    g["w2n"] = np.ascontiguousarray(np.asarray(inputs["L0_ffn_w2"], f32))
    g["b2p"] = np.ascontiguousarray(
        np.asarray(inputs["L0_ffn_b2"], f32).reshape(8, 128).T)
    g["wqT1"] = np.ascontiguousarray(np.asarray(inputs["L1_attn_Wq"], f32).T)
    g["wkT1"] = np.ascontiguousarray(np.asarray(inputs["L1_attn_Wk"], f32).T)
    g["wvT1"] = np.ascontiguousarray(np.asarray(inputs["L1_attn_Wv"], f32).T)
    g["bqkv1"] = np.ascontiguousarray(np.stack(
        [np.asarray(inputs["L1_attn_bq"], f32),
         np.asarray(inputs["L1_attn_bk"], f32),
         np.asarray(inputs["L1_attn_bv"], f32)], axis=1))
    g["wgblk"] = np.ascontiguousarray(
        np.asarray(inputs["L1_ffn_w_gate"], f32).T.reshape(
            D, 32, 128).transpose(1, 0, 2))
    g["wublk"] = np.ascontiguousarray(
        np.asarray(inputs["L1_ffn_w_up"], f32).T.reshape(
            D, 32, 128).transpose(1, 0, 2))
    g["wdT"] = np.ascontiguousarray(np.asarray(inputs["L1_ffn_w_down"], f32).T)
    g["sc0"] = np.asarray(inputs["L0_attn_scale"], dtype=f32).reshape(1, 1)
    g["identf"] = np.eye(128, dtype=f32)
    g["identb"] = np.eye(128, dtype=bf)
    m16 = np.zeros((128, 128), dtype=bf)
    for c in range(8):
        m16[0:64, c * 16 + 2 * c] = 1
        m16[64:128, c * 16 + 2 * c + 1] = 1
    g["mask16"] = m16
    sel = np.zeros((16, 1024), dtype=bf)
    for c in range(8):
        sel[2 * c, c * 128: c * 128 + 64] = 1
        sel[2 * c + 1, c * 128 + 64: c * 128 + 128] = 1
    g["selb"] = sel
    g["onesb"] = np.ones((1, 128), dtype=bf)
    g["onesf"] = np.ones((1, 128), dtype=f32)
    return g


def kernel(**inputs):
    x = np.asarray(inputs["L0_attn_x"], dtype=np.float32)
    B, T, _ = x.shape
    assert B == NCORE
    g = _host_prep(inputs)
    nc, _ = build_program(T=T, ncores=NCORE)
    in_maps = []
    for c in range(NCORE):
        m = dict(g)
        m["x"] = np.ascontiguousarray(x[c])
        in_maps.append(m)
    res = run_bass_kernel_spmd(nc, in_maps, core_ids=list(range(NCORE)))
    out = np.stack([np.asarray(r["out"]) for r in res.results], axis=0)
    return out.astype(np.float32)
